# version 15
# baseline (speedup 1.0000x reference)
import os
import numpy as np
import ml_dtypes
import concourse.bass as bass
import concourse.mybir as mybir
from concourse.bass_utils import run_bass_kernel_spmd
from contextlib import ExitStack

F32 = mybir.dt.float32
BF16 = mybir.dt.bfloat16
AF = mybir.ActivationFunctionType
ALU = mybir.AluOpType
AX = mybir.AxisListType

D = 1024
HF = 4096
NCORES = 8
PHASE = int(os.environ.get("KPHASE", "4"))
PROLOG = int(os.environ.get("KPROLOG", "3"))


class _FakeInstr:
    def then_inc(self, sem, n=1):
        return self


class _FakeEng:
    def __getattr__(self, name):
        return lambda *a, **k: _FakeInstr()


class _Tr:

    def __init__(self, sems):
        self.sems = sems
        self.plan = True
        self.count = {k: 0 for k in sems}
        self.evt = {}
        self.waited = {}

    def begin_emit(self):
        assert self.plan
        self._plan_count = dict(self.count)
        self.plan = False
        self.count = {k: 0 for k in self.sems}
        self.waited = {}
        for q in getattr(self, "queues", []):
            q.cnt = 0

    def wait_val(self, eng, ename, sem, val):
        if self.plan or val <= 0:
            return
        w = self.waited.setdefault(ename, {})
        if w.get(sem, 0) >= val:
            return
        eng.wait_ge(self.sems[sem], val)
        w[sem] = val

    def check(self):
        assert self.count == self._plan_count, (self.count, self._plan_count)

    def inc(self, instr, sem, n=1, key=None):
        self.count[sem] += n
        tok = (sem, self.count[sem])
        if self.plan:
            if key is not None:
                assert key not in self.evt, key
                self.evt[key] = tok
        else:
            instr.then_inc(self.sems[sem], n)
            if key is not None:
                assert self.evt[key] == tok, (key, self.evt[key], tok)
        return tok

    def minc(self, instr, sem, key=None):
        if key is None:
            return None
        return self.inc(instr, sem, 1, key=key)

    def wait(self, eng, ename, *keys):
        if self.plan:
            return
        w = self.waited.setdefault(ename, {})
        for key in keys:
            if key is None:
                continue
            sem, val = self.evt[key]
            if val <= 0 or w.get(sem, 0) >= val:
                continue
            eng.wait_ge(self.sems[sem], val)
            w[sem] = val

    def wait_opt(self, eng, ename, *keys):
        if self.plan:
            return
        self.wait(eng, ename, *[k for k in keys if k in self.evt])


class _DmaQ:

    def __init__(self, tr, base, n):
        self.tr, self.n = tr, n
        self.names = [f"{base}{i}" for i in range(n)]
        self.cnt = 0
        tr.queues = getattr(tr, "queues", []) + [self]

    def issue(self, eng, ename, key, fn):
        i = self.cnt
        self.cnt += 1
        nm = self.names[i % self.n]
        self.tr.wait_val(eng, ename, nm, 16 * (i // self.n))
        self.tr.inc(fn(), nm, 16, key=key)


def build_program(t_core):
    NT1 = t_core // 256
    NT2 = t_core // 512
    assert NT1 * 256 == t_core and NT2 * 512 == t_core

    nc = bass.Bass("TRN2", target_bir_lowering=False)

    x_in = nc.dram_tensor("x_in", [t_core, D], F32, kind="ExternalInput").ap()
    xl16_d = nc.dram_tensor("xl16", [16, D], F32, kind="ExternalInput").ap()
    wq_d = nc.dram_tensor("wq", [128, 8 * D], F32, kind="ExternalInput").ap()
    wk_d = nc.dram_tensor("wk", [128, 8 * D], F32, kind="ExternalInput").ap()
    wv_d = nc.dram_tensor("wv", [128, 8 * D], F32, kind="ExternalInput").ap()
    wg_d = nc.dram_tensor("wg", [128, 8 * D], F32, kind="ExternalInput").ap()
    wo_d = nc.dram_tensor("wo", [128, 8 * D], F32, kind="ExternalInput").ap()
    w1_d = nc.dram_tensor("w1s", [128, 8 * HF], F32, kind="ExternalInput").ap()
    w2_d = nc.dram_tensor("w2s", [128, 32 * D], F32, kind="ExternalInput").ap()
    b1_d = nc.dram_tensor("b1c", [128, 32], F32, kind="ExternalInput").ap()
    b2_d = nc.dram_tensor("b2r", [1, D], F32, kind="ExternalInput").ap()
    gb_d = nc.dram_tensor("gbcol", [128, 16], F32, kind="ExternalInput").ap()
    sc_d = nc.dram_tensor("scl", [128, 1], F32, kind="ExternalInput").ap()
    mb_d = nc.dram_tensor("mkb", [128, 2], F32, kind="ExternalInput").ap()
    id_d = nc.dram_tensor("idn", [128, 128], BF16, kind="ExternalInput").ap()
    sa_d = nc.dram_tensor("selA", [128, 8 * 16], BF16, kind="ExternalInput").ap()
    sb_d = nc.dram_tensor("selB", [32, 8 * 128], BF16, kind="ExternalInput").ap()
    o1_d = nc.dram_tensor("ones128", [128, 1], BF16, kind="ExternalInput").ap()
    or_d = nc.dram_tensor("onesrow", [1, 128], BF16, kind="ExternalInput").ap()

    out_d = nc.dram_tensor("out", [t_core, D], F32, kind="ExternalOutput").ap()

    xnt_scr = nc.dram_tensor("xnt_scr", [NT2, 128, 8, 512], BF16).ap()
    k_scr = nc.dram_tensor("k_scr", [NT2, 128, 8, 512], BF16).ap()
    p1_scr = nc.dram_tensor("p1_scr", [NT2, 128, 4, D], F32).ap()
    cc_in = nc.dram_tensor("cc_in", [2, 128, 8], F32).ap()
    cc_out = nc.dram_tensor("cc_out", [2, 128, 8], F32).ap()

    ctx = ExitStack()
    with ctx:
        def sbuf(name, shape, dtype):
            return ctx.enter_context(nc.sbuf_tensor(name, shape, dtype))

        wA = sbuf("wA", [128, 73728], BF16)

        def w1eff(dc, lo, hi):
            return wA[:, dc * HF + lo: dc * HF + hi]

        def w2eff(cc, lo, hi):
            return wA[:, 32768 + cc * D + lo: 32768 + cc * D + hi]

        def wk_s(dc, oc):
            b = 65536 + dc * D + oc * 128
            return wA[:, b: b + 128]

        def wq_s(dc, oc):
            b = 65536 + dc * D + oc * 128
            return wA[:, b: b + 128]

        def wv_s(dc, oc):
            b = 49152 + dc * D + oc * 128
            return wA[:, b: b + 128]

        def wg_s(dc, oc):
            b = 57344 + dc * D + oc * 128
            return wA[:, b: b + 128]

        def wo_s(fc, lo, hi):
            return wA[:, fc * D + lo: fc * D + hi]

        xnT2v = wA[:, 32768:36864].rearrange("p (c n) -> p c n", c=8)
        k2v = wA[:, 36864:40960].rearrange("p (c n) -> p c n", c=8)
        qbf = wA[:, 40960:45056].rearrange("p (c n) -> p c n", c=8)
        vbf = wA[:, 45056:49152].rearrange("p (c n) -> p c n", c=8)
        sgbf = wA[:, 8192:12288].rearrange("p (c n) -> p c n", c=8)
        znb = wA[:, 12288:16384].rearrange("p (c n) -> p c n", c=8)

        aB = sbuf("aB", [128, 9728], BF16)
        xn_tm = aB[:, 0:2048].rearrange("p (s d) -> p s d", s=2)
        xnT1 = aB[:, 2048:4096].rearrange("p (c n) -> p c n", c=8)
        hc = aB[:, 4096:4608].rearrange("p (b n) -> p b n", b=2)
        kbf = aB[:, 4608:6656].rearrange("p (c n) -> p c n", c=8)
        mn1 = aB[:, 6656:7168].rearrange("p (b n) -> p b n", b=2)
        ex1 = aB[:, 9216:9728].rearrange("p (b n) -> p b n", b=2)
        xsq = aB[:, 7168:9216].rearrange("p (s d) -> p s d", s=2)
        prodp = aB[:, 0:1024].rearrange("p (b n) -> p b n", b=2)
        nprodp = aB[:, 1024:2048].rearrange("p (b n) -> p b n", b=2)
        t1p = aB[:, 2048:3072].rearrange("p (b n) -> p b n", b=2)
        mn2 = aB[:, 3072:4096].rearrange("p (b n) -> p b n", b=2)
        ex2 = aB[:, 5632:6656].rearrange("p (b n) -> p b n", b=2)
        xsq16 = aB[0:16, 7168:8192]

        aF = sbuf("aF", [128, 6656], F32)
        xF = aF[:, 0:4096].rearrange("p (b s d) -> p b s d", b=2, s=2)
        p1t = aF[:, 4096:6144].rearrange("p (s d) -> p s d", s=2)

        st1 = aF[:, 6144:6160]
        p1in = aF[:, 0:4096].rearrange("p (s d) -> p s d", s=4)
        recf = aF[0:16, 4096:4608]
        stat2 = aF[0:1, 4608:6656]
        xl16 = aF[0:16, 4096:5120]

        st16 = sbuf("st16_s", [16, 8], F32)
        xl8b = sbuf("xl8b_s", [128, 6144], BF16)
        b2bf = aB[0:1, 0:1024]
        rqk = aB[0:32, 4096:4608]
        mwr = aB[0:2, 4608:5120]
        rstdb = aB[0:1, 5120:5632]
        ident = sbuf("ident_s", [128, 128], BF16)
        selA = sbuf("selA_s", [128, 8 * 16], BF16)
        selB = sbuf("selB_s", [32, 8 * 128], BF16)
        ones1 = sbuf("ones1_s", [128, 1], BF16)
        onesr = sbuf("onesr_s", [1, 128], BF16)
        b1c = sbuf("b1c_s", [128, 32], F32)
        b2rep = sbuf("b2rep_s", [128, D], BF16)
        gbcol = sbuf("gbcol_s", [128, 16], F32)
        scl = sbuf("scl_s", [128, 1], F32)
        mkb = sbuf("mkb_s", [128, 2], F32)
        ksum_r = sbuf("ksum_r_s", [128, 8], F32)
        kredb = sbuf("kredb_s", [128, 8], F32)
        ccin_s = sbuf("ccin_s_s", [128, 16], F32)
        ccout_s = sbuf("ccout_s_s", [128, 16], F32)
        ksum_g = sbuf("ksum_g_s", [128, 8], F32)
        selAk = sbuf("selAk_s", [128, 8 * 16], BF16)

        pb = [ctx.enter_context(nc.psum_tensor(f"pb{i}", [128, 512], F32))
              for i in range(8)]

        sems = {}
        for s_ in (["cc", "v", "s", "t", "g"]
                   + [f"ld{i}" for i in range(4)]
                   + [f"st{i}" for i in range(4)]
                   + [f"wl{i}" for i in range(2)]):
            sems[s_] = ctx.enter_context(nc.semaphore("sem_" + s_))
        tr = _Tr(sems)
        Qld = _DmaQ(tr, "ld", 4)
        Qst = _DmaQ(tr, "st", 4)
        Qwl = _DmaQ(tr, "wl", 2)

        _PAIRMAP = {0: (0, 0), 1: (32, 0), 2: (64, 0), 3: (0, 1),
                    4: (32, 1), 5: (32, 2), 6: (64, 1), 7: (64, 2)}

        def xl8(j, lo, hi):
            pair = (j // 2) if j < 8 else (4 + (j - 8) // 2)
            base, blk = _PAIRMAP[pair]
            c0 = blk * 2048 + (j % 2) * 1024
            return xl8b[base:base + 1, c0 + lo: c0 + hi]

        RE = lambda i: 1.0 / i

        def body_sync(e):
            E = "sync"
            Qld.issue(e, E, "ld_xl",
                      lambda: e.dma_start(out=xl16, in_=xl16_d))
            for t in range(min(2, NT1) if PHASE >= 1 else 0):
                Qld.issue(e, E, ("xld", t), lambda t=t: e.dma_start(
                    out=xF[:, t % 2, :, :],
                    in_=x_in[t * 256:(t + 1) * 256, :].rearrange(
                        "(s p) d -> p s d", p=128)))
            for key_, ap_d, ap_s in [
                    ("c_id", id_d, ident[:, :]), ("c_sa", sa_d, selA[:, :]),
                    ("c_sb", sb_d, selB[:, :]), ("c_o1", o1_d, ones1[:, :]),
                    ("c_or", or_d, onesr[:, :]), ("c_b1", b1_d, b1c[:, :]),
                    ("c_sc", sc_d, scl[:, :]), ("c_mb", mb_d, mkb[:, :])]:
                Qld.issue(e, E, key_,
                          lambda a=ap_s, b=ap_d: e.dma_start(out=a, in_=b))
            tr.wait(e, E, "xnl16")
            Qld.issue(e, E, ("xl8", 0), lambda: e.dma_start(
                out=xl8b[0:1, 0:4096], in_=xsq16[0:4, :]))
            Qld.issue(e, E, ("xl8", 1), lambda: e.dma_start(
                out=xl8b[32:33, 0:6144], in_=xsq16[4:10, :]))
            Qld.issue(e, E, ("xl8", 2), lambda: e.dma_start(
                out=xl8b[64:65, 0:6144], in_=xsq16[10:16, :]))
            if PHASE < 1:
                if PROLOG >= 3:
                    tr.wait(e, E, ("w2e", 31, 1))
                elif PROLOG >= 2:
                    tr.wait(e, E, ("w1e", 7, 7))
                elif PROLOG >= 1:
                    tr.wait(e, E, ("b2e", 1))
                else:
                    tr.wait(e, E, "xnl16")
                return
            for t in range(2, NT1):
                if PHASE >= 2:
                    tr.wait_opt(e, E, ("p1b", t - 2, 1, 1))
                else:
                    tr.wait_opt(e, E, ("xn", t - 2, 1))
                Qld.issue(e, E, ("xld", t), lambda t=t: e.dma_start(
                    out=xF[:, t % 2, :, :],
                    in_=x_in[t * 256:(t + 1) * 256, :].rearrange(
                        "(s p) d -> p s d", p=128)))
            if PHASE < 3:
                tr.wait(e, E, ("xnTst", NT1 - 1), ("kst", NT1 - 1),
                        ("kred", NT1 - 1))
                if PHASE >= 2:
                    tr.wait(e, E, *[("p1st", t)
                                    for t in range(max(0, NT1 - 4), NT1)])
                    tr.wait(e, E, *[("xnTst", t)
                                    for t in range(max(0, NT1 - 4), NT1)])
                    tr.wait(e, E, *[("kst", t)
                                    for t in range(max(0, NT1 - 4), NT1)])
                return
            tr.wait(e, E, "ccin_v")
            Qld.issue(e, E, "ccin_d", lambda: e.dma_start(
                out=cc_in.rearrange("b p c -> p b c"),
                in_=ccin_s[:, :].rearrange("p (b c) -> p b c", b=2)))
            tr.wait(e, E, "cc_done")
            Qld.issue(e, E, "ccout_d", lambda: e.dma_start(
                out=ccout_s[:, :].rearrange("p (b c) -> p b c", b=2),
                in_=cc_out.rearrange("b p c -> p b c")))
            if PHASE < 4:
                tr.wait(e, E, "ccout_d")
                return
            for u in range(NT2):
                if u == 0:
                    tr.wait(e, E, ("fgrp", NT1 - 1, 3))
                else:
                    tr.wait(e, E, ("gmm", u - 1, 7))
                tr.wait(e, E, ("xnTst", 2 * u), ("xnTst", 2 * u + 1))
                Qld.issue(e, E, ("xnld", u), lambda u=u: e.dma_start(
                    out=xnT2v, in_=xnt_scr[u]))
                if u > 0:
                    tr.wait(e, E, ("prod", u - 1, 7))
                if u == 0:
                    tr.wait(e, E, ("fgrp", NT1 - 1, 7))
                tr.wait(e, E, ("kst", 2 * u), ("kst", 2 * u + 1))
                Qld.issue(e, E, ("kld", u), lambda u=u: e.dma_start(
                    out=k2v, in_=k_scr[u]))
                if u > 0:
                    tr.wait(e, E, ("ost", u - 1))
                else:
                    tr.wait(e, E, ("p1b", NT1 - 1, 1, 1))
                tr.wait(e, E, ("p1st", 2 * u), ("p1st", 2 * u + 1))
                Qld.issue(e, E, ("p1ld", u), lambda u=u: e.dma_start(
                    out=p1in, in_=p1_scr[u]))
            tr.wait(e, E, *[("ost", u)
                            for u in range(max(0, NT2 - 4), NT2)])

        def body_gpsimd(e):
            E = "g"
            Qwl.issue(e, E, "gbcol",
                      lambda: e.dma_start(out=gbcol[:, :], in_=gb_d))
            Qwl.issue(e, E, "b2bf",
                      lambda: e.dma_start(out=b2bf[:, :], in_=b2_d))
            tr.wait(e, E, "ld_xl")
            Qwl.issue(e, E, "wl_wk",
                      lambda: e.dma_start(out=wA[:, 65536:73728], in_=wk_d))
            if PHASE >= 1:
                tr.wait(e, E, ("xl8", 2), ("xld", 1))
            for dc in range(8):
                Qwl.issue(e, E, ("wl_w1", dc), lambda dc=dc: e.dma_start(
                    out=wA[:, dc * HF:(dc + 1) * HF],
                    in_=w1_d[:, dc * HF:(dc + 1) * HF]))
            for j in range(8):
                Qwl.issue(e, E, ("wl_w2", j), lambda j=j: e.dma_start(
                    out=wA[:, 32768 + j * HF: 32768 + (j + 1) * HF],
                    in_=w2_d[:, j * HF:(j + 1) * HF]))
            if PHASE < 3:
                return
            for t in range(NT1 if PHASE >= 1 else 0):
                pass
            if PHASE < 3:
                return
            if PHASE >= 4:
                tr.wait(e, E, ("kmm", NT1 - 1, 7))
                Qwl.issue(e, E, "wl_wq", lambda: e.dma_start(
                    out=wA[:, 65536:73728], in_=wq_d))
                tr.wait(e, E, ("fgrp", NT1 - 1, 23))
                Qwl.issue(e, E, "wl_wv", lambda: e.dma_start(
                    out=wA[:, 49152:57344], in_=wv_d))
            tr.wait(e, E, "ccin_d")
            tr.inc(e.collective_compute(
                "AllReduce", ALU.add, replica_groups=[list(range(NCORES))],
                ins=[cc_in[:, :, :]], outs=[cc_out[:, :, :]]),
                "cc", 1, key="cc_done")
            if PHASE < 4:
                return
            tr.wait(e, E, ("fgrp", NT1 - 1, 31))
            Qwl.issue(e, E, "wl_wg", lambda: e.dma_start(
                out=wA[:, 57344:65536], in_=wg_d))
            tr.wait(e, E, ("h", NT1 - 1, 31))
            Qwl.issue(e, E, "wl_wo", lambda: e.dma_start(
                out=wA[:, 0:8192], in_=wo_d))
            for u in range(NT2):

                for fc in range(8):
                    tr.wait(e, E, ("t3", u, fc), ("sgev", u, fc))
                    if u == 0 and fc < 2:
                        tr.wait(e, E, ("h", NT1 - 1, 31))
                    tr.wait_opt(e, E, ("womm", u - 1, 7))
                    tr.inc(e.tensor_tensor(
                        out=znb[:, fc, :], in0=t1p[:, fc % 2, :],
                        in1=sgbf[:, fc, :], op=ALU.mult),
                        "g", key=("zn", u, fc))

        def body_vector(e):
            E = "v"
            tr.wait(e, E, "ld_xl")
            tr.inc(e.tensor_tensor(out=xsq16, in0=xl16, in1=xl16,
                                   op=ALU.mult), "v", key=("pl", 0))
            tr.inc(e.reduce_sum(out=st16[:, 0:1], in_=xl16, axis=AX.X),
                   "v", key=("pl", 1))
            tr.wait(e, E, ("pl", 0))
            tr.inc(e.reduce_sum(out=st16[:, 1:2], in_=xsq16, axis=AX.X),
                   "v", key=("pl", 2))
            tr.wait(e, E, ("pl", 1), ("pl", 2))
            tr.inc(e.scalar_tensor_tensor(
                out=st16[:, 2:3], in0=st16[:, 0:1], scalar=RE(D),
                in1=st16[:, 0:1], op0=ALU.mult, op1=ALU.mult),
                "v", key=("pl", 3))
            tr.wait(e, E, ("pl", 3))
            tr.inc(e.tensor_sub(st16[:, 3:4], st16[:, 1:2], st16[:, 2:3]),
                   "v", key=("pl", 4))
            tr.wait(e, E, ("pl", 4))
            tr.inc(e.tensor_scalar_add(st16[:, 4:5], st16[:, 3:4], D * 1e-5),
                   "v", key=("pl", 5))
            tr.wait(e, E, ("pl", 5))
            tr.inc(e.reciprocal(st16[:, 5:6], st16[:, 4:5]),
                   "v", key=("pl", 6))
            tr.wait(e, E, "pl_sqrt")
            tr.inc(e.scalar_tensor_tensor(
                out=st16[:, 7:8], in0=st16[:, 0:1], scalar=-RE(D),
                in1=st16[:, 6:7], op0=ALU.mult, op1=ALU.mult),
                "v", key=("pl", 7))
            tr.wait(e, E, ("pl", 7))
            tr.inc(e.tensor_scalar(
                out=xsq16, in0=xl16, scalar1=st16[:, 6:7],
                scalar2=st16[:, 7:8], op0=ALU.mult, op1=ALU.add),
                "v", key="xnl16")
            tr.inc(e.memset(ksum_r[:, :], 0.0), "v", key="ksum0")
            for oh in range(2 if PROLOG >= 1 else 0):
                tr.wait(e, E, ("b2m", oh))
                tr.inc(e.tensor_copy(out=b2rep[:, oh * 512:(oh + 1) * 512],
                                     in_=pb[oh][:, :]),
                       "v", key=("b2e", oh))
            def v_folds():
                for dc in range(8 if PROLOG >= 2 else 0):
                    for hs in range(8):
                        tr.wait(e, E, ("w1m", dc, hs), ("wl_w1", dc), "c_sc")
                        sl_ = w1eff(dc, hs * 512, (hs + 1) * 512)
                        tr.inc(e.scalar_tensor_tensor(
                            out=sl_, in0=pb[4 + hs % 4][:, :],
                            scalar=scl[:, 0:1], in1=sl_,
                            op0=ALU.mult, op1=ALU.add),
                            "v", key=("w1e", dc, hs))
                for cc in range(32 if PROLOG >= 3 else 0):
                    pair = (4, 5) if cc % 2 == 0 else (6, 7)
                    for oh in range(2):
                        tr.wait(e, E, ("w2m", cc, oh), ("wl_w2", cc // 4),
                                "c_sc")
                        sl_ = w2eff(cc, oh * 512, (oh + 1) * 512)
                        tr.inc(e.scalar_tensor_tensor(
                            out=sl_, in0=pb[pair[oh]][:, :],
                            scalar=scl[:, 0:1], in1=sl_,
                            op0=ALU.mult, op1=ALU.add),
                            "v", key=("w2e", cc, oh))

            if PHASE < 1:
                v_folds()
            def ln_stats(t, xb):
                tr.wait(e, E, ("xld", t))
                tr.wait_opt(e, E, ("xn", t - 1, 1))
                tr.inc(e.reduce_sum(out=st1[:, 0:2], in_=xb, axis=AX.X),
                       "v", key=("mus", t))
                for s_ in range(2):
                    tr.wait(e, E, ("xsqs", t, s_))
                    tr.inc(e.reduce_sum(out=st1[:, 2 + s_:3 + s_],
                                        in_=xsq[:, s_, :],
                                        axis=AX.X), "v", key=("sqs", t, s_))
                tr.wait(e, E, ("mus", t), ("sqs", t, 1))
                tr.inc(e.scalar_tensor_tensor(
                    out=st1[:, 4:6], in0=st1[:, 0:2], scalar=RE(D),
                    in1=st1[:, 0:2], op0=ALU.mult, op1=ALU.mult),
                    "v", key=("lnb", t))
                tr.wait(e, E, ("lnb", t))
                tr.inc(e.tensor_sub(st1[:, 6:8], st1[:, 2:4], st1[:, 4:6]),
                       "v", key=("lnc", t))
                tr.wait(e, E, ("lnc", t))
                tr.inc(e.tensor_scalar_add(st1[:, 8:10], st1[:, 6:8],
                                           D * 1e-5), "v", key=("lne", t))
                tr.wait(e, E, ("lne", t))
                tr.inc(e.reciprocal(st1[:, 10:12], st1[:, 8:10]),
                       "v", key=("rec1", t))

            def ln_apply(t, xb):
                tr.wait(e, E, ("rstd1", t))
                tr.inc(e.scalar_tensor_tensor(
                    out=st1[:, 14:16], in0=st1[:, 0:2], scalar=-RE(D),
                    in1=st1[:, 12:14], op0=ALU.mult, op1=ALU.mult),
                    "v", key=("nmr1", t))
                for s in range(2):
                    tr.wait(e, E, ("nmr1", t))
                    tr.wait_opt(e, E, ("tT", t - 1, 2 * s + 1))
                    if t == 0:
                        tr.wait(e, E, "xnl16", ("b2m", 1))
                    tr.inc(e.tensor_scalar(
                        out=xn_tm[:, s, :], in0=xb[:, s, :],
                        scalar1=st1[:, 12 + s:13 + s],
                        scalar2=st1[:, 14 + s:15 + s],
                        op0=ALU.mult, op1=ALU.add),
                        "v", key=("xn", t, s))

            def ln_block(t, xb):
                ln_stats(t, xb)
                ln_apply(t, xb)

            def p1_block(t):
                xb_ = xF[:, t % 2, :, :]
                for s in range(2 if PHASE >= 2 else 0):
                    for oh in range(2):
                        sl_ = p1t[:, s, oh * 512:(oh + 1) * 512]
                        tr.wait(e, E, ("fgrp", t, 31), ("b2e", 1))
                        tr.wait_opt(e, E, ("p1st", t - 1))
                        if t == 0:
                            tr.wait(e, E, "xnl16")
                        tr.inc(e.tensor_add(
                            sl_, pb[4 + s * 2 + oh][:, :],
                            b2rep[:, oh * 512:(oh + 1) * 512]),
                            "v", key=("p1a", t, s, oh))
                        tr.wait(e, E, ("p1a", t, s, oh))
                        tr.inc(e.tensor_add(
                            sl_, sl_, xb_[:, s, oh * 512:(oh + 1) * 512]),
                            "v", key=("p1b", t, s, oh))

            for t in range(NT1 if PHASE >= 1 else 0):
                xb = xF[:, t % 2, :, :]
                if t == 0:
                    ln_block(0, xb)
                def k_evac(oc):
                    bank = pb[oc % 4]
                    tr.wait(e, E, ("kmm", t, oc), ("kex", t, oc))
                    tr.wait_opt(e, E, ("kred", t - 1), ("kst", t - 1))
                    tr.inc(e.tensor_scalar_max(kbf[:, oc, :],
                                               bank[:, 0:256], 0.0),
                           "v", key=("krl", t, oc))
                    tr.wait(e, E, ("krl", t, oc))
                    tr.inc(e.scalar_tensor_tensor(
                        out=kbf[:, oc, :], in0=ex1[:, oc % 2, :], scalar=1.0,
                        in1=kbf[:, oc, :], op0=ALU.min, op1=ALU.add),
                        "v", key=("kst3", t, oc))

                k_evac(0)
                k_evac(1)
                if t + 1 < NT1:
                    ln_stats(t + 1, xF[:, (t + 1) % 2, :, :])
                for oc in range(2, 8):
                    k_evac(oc)

                if t == 0 and PHASE >= 1:
                    v_folds()
                if t + 1 < NT1:
                    ln_apply(t + 1, xF[:, (t + 1) % 2, :, :])
                tr.wait(e, E, *[("kst3", t, oc) for oc in range(8)])
                tr.wait_opt(e, E, ("ksum", t - 1))
                tr.inc(e.reduce_sum(out=kredb[:, :], in_=kbf, axis=AX.X),
                       "v", key=("kred", t))
                tr.wait(e, E, "ksum0", ("kred", t))
                tr.inc(e.tensor_tensor(out=ksum_r[:, :], in0=ksum_r[:, :],
                                       in1=kredb[:, :], op=ALU.add),
                       "v", key=("ksum", t))
                if t + 1 >= NT1 and PHASE >= 3:
                    tr.wait(e, E, ("ksum", NT1 - 1), "c_mb")
                    tr.inc(e.tensor_scalar_mul(ccin_s[:, 0:8], ksum_r[:, :],
                                               mkb[:, 0:1]), "v",
                           key=("ccm", 0))
                    tr.wait(e, E, ("ccm", 0))
                    tr.inc(e.tensor_scalar_mul(ccin_s[:, 8:16], ksum_r[:, :],
                                               mkb[:, 1:2]), "v",
                           key="ccin_v")
                p1_block(t)
            if PHASE < 3:
                return
            tr.wait(e, E, ("fgrp", NT1 - 1, 31), ("kst", NT1 - 1),
                    ("kred", NT1 - 1))
            tr.inc(e.memset(rqk[:, :], 0.0), "v", key="rqk0")
            tr.inc(e.memset(mwr[:, :], 1.0), "v", key="mwr1")
            tr.wait(e, E, "ccout_d")
            tr.inc(e.tensor_scalar_mul(ksum_g[:, :], ccout_s[:, 0:8],
                                       mkb[:, 0:1]), "v", key=("ccg", 0))
            tr.wait(e, E, ("ccg", 0))
            tr.inc(e.scalar_tensor_tensor(
                out=ksum_g[:, :], in0=ccout_s[:, 8:16], scalar=mkb[:, 1:2],
                in1=ksum_g[:, :], op0=ALU.mult, op1=ALU.add),
                "v", key="ksumg")
            for fc in range(8):
                if fc == 0:
                    tr.wait(e, E, "ksumg", "c_sa")
                tr.inc(e.tensor_scalar_mul(
                    selAk[:, fc * 16:(fc + 1) * 16],
                    selA[:, fc * 16:(fc + 1) * 16], ksum_g[:, fc:fc + 1]),
                    "v", key=("sak", fc))
            def v_qrl(u, lo=0, hi=8):
                for oc in range(lo, hi):
                    tr.wait(e, E, ("qmm", u, oc), ("qex", u, oc))
                    if u == 0:
                        tr.wait(e, E, ("fgrp", NT1 - 1, 11))
                    tr.inc(e.tensor_scalar_max(qbf[:, oc, :],
                                               pb[oc % 2][:, :], 0.0),
                           "v", key=("qrl", u, oc))
                    tr.wait(e, E, ("qrl", u, oc), ("qex", u, oc))
                    tr.inc(e.scalar_tensor_tensor(
                        out=qbf[:, oc, :], in0=ex2[:, oc % 2, :], scalar=1.0,
                        in1=qbf[:, oc, :], op0=ALU.min, op1=ALU.add),
                        "v", key=("qst3", u, oc))

            for u in range(NT2 if PHASE >= 4 else 0):
                if u == 0:
                    v_qrl(0)
                for fc in range(8):
                    tr.wait(e, E, ("qst3", u, fc), ("kld", u))
                    tr.wait_opt(e, E, ("qkp", u, fc - 2),
                                ("sqp", u - 1, 6 + fc % 2),
                                ("tT", NT1 - 1, 3) if u == 0 else None)
                    tr.inc(e.tensor_tensor(
                        out=prodp[:, fc % 2, :], in0=qbf[:, fc, :],
                        in1=k2v[:, fc, :], op=ALU.mult),
                        "v", key=("prod", u, fc))
                for hh in range(2):
                    cs = slice(hh * 256, (hh + 1) * 256)
                    if hh == 0:
                        tr.wait(e, E, ("nmp", u, 7))
                        tr.wait_opt(e, E, ("rmul", u - 1, 1),
                                    ("p1st", NT1 - 1) if u == 0 else None)
                    tr.inc(e.tensor_scalar_add(recf[:, cs], pb[5][0:16, cs],
                                               1e-6),
                           "v", key=("rrec1", u, hh))
                    tr.wait(e, E, ("rrec1", u, hh))
                    tr.inc(e.reciprocal(recf[:, cs], recf[:, cs]),
                           "v", key=("rrec2", u, hh))
                    tr.wait(e, E, ("qkp", u, 7), ("rrec2", u, hh), "rqk0")
                    if hh == 0:
                        tr.wait_opt(e, E, ("rbc", u - 1, 7, 1))
                    tr.inc(e.tensor_tensor(out=rqk[0:16, cs],
                                           in0=pb[4][0:16, cs],
                                           in1=recf[:, cs], op=ALU.mult),
                           "v", key=("rmul", u, hh))
                for fc in range(8):
                    tr.wait(e, E, ("rbc", u, fc, 1), ("vev", u, fc))
                    tr.inc(e.tensor_tensor(
                        out=vbf[:, fc, :], in0=vbf[:, fc, :],
                        in1=pb[2 + fc % 2][:, :], op=ALU.mult),
                        "v", key=("zev", u, fc))
                tr.wait(e, E, ("smp", u, 7))
                tr.wait_opt(e, E, ("p1st", NT1 - 1) if u == 0 else
                            ("mwr0", u - 1))
                tr.inc(e.tensor_scalar_mul(stat2[0:1, 0:512], pb[6][0:1, :],
                                           RE(D)), "v", key=("mu", u))
                tr.wait(e, E, ("mu", u))
                tr.inc(e.tensor_tensor(out=stat2[0:1, 512:1024],
                                       in0=stat2[0:1, 0:512],
                                       in1=stat2[0:1, 0:512], op=ALU.mult),
                       "v", key=("musq", u))
                tr.wait(e, E, ("sqp", u, 7), ("musq", u))
                tr.inc(e.scalar_tensor_tensor(
                    out=stat2[0:1, 1024:1536], in0=pb[7][0:1, :],
                    scalar=RE(D), in1=stat2[0:1, 512:1024],
                    op0=ALU.mult, op1=ALU.subtract),
                    "v", key=("var", u))
                tr.wait(e, E, ("var", u))
                tr.inc(e.tensor_scalar_add(stat2[0:1, 1024:1536],
                                           stat2[0:1, 1024:1536], 1e-5),
                       "v", key=("vare", u))
                tr.wait(e, E, ("vare", u))
                if u == 0:
                    tr.wait(e, E, ("rstd1", NT1 - 1))
                tr.inc(e.reciprocal(stat2[0:1, 1536:2048],
                                    stat2[0:1, 1024:1536]),
                       "v", key=("rec2", u))
                tr.wait(e, E, ("rstds", u))
                if u == 0:
                    tr.wait(e, E, ("kst", NT1 - 1))
                tr.wait_opt(e, E, ("b1m", u - 1))
                tr.inc(e.tensor_copy(out=rstdb[0:1, :],
                                     in_=stat2[0:1, 512:1024]),
                       "v", key=("rstdbf", u))
                tr.wait(e, E, "mwr1")
                tr.wait_opt(e, E, ("b2m2", u - 1))
                tr.inc(e.scalar_tensor_tensor(
                    out=mwr[0:1, :], in0=stat2[0:1, 0:512], scalar=-1.0,
                    in1=stat2[0:1, 512:1024], op0=ALU.mult, op1=ALU.mult),
                    "v", key=("mwr0", u))
                for fc in range(8):
                    tr.wait(e, E, ("b1m", u), ("zev", u, fc))
                    if u == 0:
                        tr.wait(e, E, ("xnTst", NT1 - 1))
                    tr.wait_opt(e, E, ("zn", u, fc - 2))
                    tr.inc(e.tensor_tensor(
                        out=t1p[:, fc % 2, :], in0=vbf[:, fc, :],
                        in1=pb[6][:, :], op=ALU.mult),
                        "v", key=("t1", u, fc))
                    tr.wait(e, E, ("b2m2", u), ("t1", u, fc))
                    tr.inc(e.tensor_tensor(
                        out=t1p[:, fc % 2, :], in0=t1p[:, fc % 2, :],
                        in1=pb[7][:, :], op=ALU.add),
                        "v", key=("t2", u, fc))

                if u + 1 < NT2:
                    v_qrl(u + 1, 0, 2)
                for g in range(8):
                    s, oh = g // 2, g % 2
                    tr.wait(e, E, ("womm", u, g), ("p1ld", u))
                    tr.inc(e.tensor_add(
                        p1in[:, s, oh * 512:(oh + 1) * 512],
                        p1in[:, s, oh * 512:(oh + 1) * 512],
                        pb[2 + g % 2][:, :]),
                        "v", key=("oev", u, g))
                if u + 1 < NT2:
                    v_qrl(u + 1, 2, 8)

        def body_scalar(e):
            E = "s"
            tr.wait(e, E, ("pl", 6))
            tr.inc(e.activation(out=st16[:, 6:7], in_=st16[:, 5:6],
                                func=AF.Sqrt, scale=float(D)),
                   "s", key="pl_sqrt")
            def rstd_block(t):
                tr.wait(e, E, ("rec1", t))
                tr.inc(e.activation(out=st1[:, 12:14], in_=st1[:, 10:12],
                                    func=AF.Sqrt, scale=float(D)),
                       "s", key=("rstd1", t))

            for t in range(NT1 if PHASE >= 1 else 0):
                xb = xF[:, t % 2, :, :]
                if t == 0:
                    for s_ in range(2):
                        tr.wait(e, E, ("xld", 0))
                        if s_ == 0:
                            tr.wait(e, E, ("xl8", 0), ("xl8", 1),
                                    ("xl8", 2))
                        else:
                            tr.wait(e, E, ("sqs", 0, 0))
                        tr.inc(e.activation(out=xsq[:, s_, :],
                                            in_=xb[:, s_, :],
                                            func=AF.Square),
                               "s", key=("xsqs", 0, s_))
                    rstd_block(0)
                for g in range(4):
                    s_, half = g // 2, g % 2
                    tr.wait(e, E, ("tT", t, g))
                    tr.wait_opt(e, E, ("h", t - 1, 31), ("xnTst", t - 1))
                    dst = xnT1[:, half * 4:(half + 1) * 4,
                               s_ * 128:(s_ + 1) * 128]
                    tr.inc(e.activation(
                        out=dst,
                        in_=pb[g % 2][:, :].rearrange("p (q n) -> p q n",
                                                      q=4),
                        func=AF.Identity),
                        "s", key=("tTe", t, g))
                tr.wait(e, E, *[("tTe", t, g) for g in range(4)])
                Qst.issue(e, E, ("xnTst", t), lambda t=t: e.dma_start(
                    out=xnt_scr[t // 2, :, :,
                                (t % 2) * 256:(t % 2) * 256 + 256],
                    in_=xnT1))
                for oc in range(8):
                    tr.wait(e, E, ("kmm", t, oc))
                    tr.wait_opt(e, E, ("kst3", t, oc - 2))
                    tr.inc(e.activation(out=ex1[:, oc % 2, :],
                                        in_=pb[oc % 4][:, 0:256],
                                        func=AF.Exp),
                           "s", key=("kex", t, oc))
                    if t + 1 < NT1 and oc in (0, 2):
                        s_ = oc // 2
                        tr.wait(e, E, ("xld", t + 1))
                        tr.wait_opt(e, E, ("sqs", t, s_))
                        tr.inc(e.activation(out=xsq[:, s_, :],
                                            in_=xF[:, (t + 1) % 2, s_, :],
                                            func=AF.Square),
                               "s", key=("xsqs", t + 1, s_))
                if t > 0 and PHASE >= 2:
                    tr.wait(e, E, *[("p1b", t - 1, s_, oh)
                                    for s_ in range(2) for oh in range(2)])
                    Qst.issue(e, E, ("p1st", t - 1),
                              lambda t=t: e.dma_start(
                        out=p1_scr[(t - 1) // 2, :,
                                   ((t - 1) % 2) * 2:((t - 1) % 2) * 2 + 2,
                                   :],
                        in_=p1t))
                if t + 1 < NT1:
                    rstd_block(t + 1)
                tr.wait(e, E, *[("kst3", t, oc) for oc in range(8)])
                Qst.issue(e, E, ("kst", t), lambda t=t: e.dma_start(
                    out=k_scr[t // 2, :, :,
                              (t % 2) * 256:(t % 2) * 256 + 256],
                    in_=kbf))
                for cc in range(32 if PHASE >= 2 else 0):
                    tr.wait(e, E, ("h", t, cc), "c_b1")
                    tr.wait_opt(e, E, ("fgrp", t, cc - 2))
                    tr.inc(e.activation(out=hc[:, cc % 2, :],
                                        in_=pb[2 + cc % 2][:, 0:256],
                                        func=AF.Gelu, bias=b1c[:, cc:cc + 1]),
                           "s", key=("ge", t, cc))

                if PHASE >= 2 and t == NT1 - 1:
                    tr.wait(e, E, *[("p1b", t, s, oh)
                                    for s in range(2) for oh in range(2)])
                    Qst.issue(e, E, ("p1st", t), lambda t=t: e.dma_start(
                        out=p1_scr[t // 2, :,
                                   (t % 2) * 2:(t % 2) * 2 + 2, :],
                        in_=p1t))
            def ost_block(u):
                tr.wait(e, E, *[("oev", u, g) for g in range(8)])
                Qst.issue(e, E, ("ost", u), lambda u=u: e.dma_start(
                    out=out_d[u * 512:(u + 1) * 512, :].rearrange(
                        "(s p) d -> p s d", p=128),
                    in_=p1in))

            for u in range(NT2 if PHASE >= 4 else 0):
                for oc in range(8):
                    tr.wait(e, E, ("qmm", u, oc))
                    if u == 0 and oc < 2:
                        tr.wait(e, E, ("kst", NT1 - 1))
                    tr.wait_opt(e, E, ("qst3", u, oc - 2))
                    tr.inc(e.activation(out=ex2[:, oc % 2, :],
                                        in_=pb[oc % 2][:, :], func=AF.Exp),
                           "s", key=("qex", u, oc))
                if u > 0:
                    ost_block(u - 1)
                for oc in range(8):
                    tr.wait(e, E, ("vmm", u, oc))
                    if u == 0:
                        tr.wait(e, E, ("fgrp", NT1 - 1, 15))
                    tr.wait_opt(e, E, ("smp", u - 1, oc), ("t1", u - 1, oc))
                    tr.inc(e.activation(out=vbf[:, oc, :],
                                        in_=pb[2 + oc % 2][:, :],
                                        func=AF.Identity),
                           "s", key=("vev", u, oc))
                for oc in range(8):
                    tr.wait(e, E, ("gmm", u, oc))
                    if u == 0:
                        tr.wait(e, E, ("h", NT1 - 1, 31))
                    else:
                        tr.wait_opt(e, E, ("zn", u - 1, oc))
                    tr.inc(e.activation(out=sgbf[:, oc, :],
                                        in_=pb[oc % 2][:, :], func=AF.Silu),
                           "s", key=("sgev", u, oc))
                for fc in range(8):
                    tr.wait(e, E, ("zev", u, fc))
                    tr.wait_opt(e, E, ("qkp", u, 7), ("sqp", u, fc - 2))
                    tr.inc(e.activation(out=prodp[:, fc % 2, :],
                                        in_=vbf[:, fc, :], func=AF.Square),
                           "s", key=("zsq", u, fc))
                tr.wait(e, E, ("rec2", u), ("var", u))
                tr.inc(e.activation(out=stat2[0:1, 512:1024],
                                    in_=stat2[0:1, 1536:2048], func=AF.Sqrt),
                       "s", key=("rstds", u))
                for fc in range(8):
                    tr.wait(e, E, ("t2", u, fc), "gbcol")
                    tr.wait_opt(e, E, ("zn", u, fc - 2))
                    tr.inc(e.activation(out=t1p[:, fc % 2, :],
                                        in_=t1p[:, fc % 2, :],
                                        func=AF.Identity,
                                        bias=gbcol[:, 8 + fc:9 + fc],
                                        scale=gbcol[:, fc:fc + 1]),
                           "s", key=("t3", u, fc))
                if u == NT2 - 1:
                    ost_block(u)

        def body_tensor(e):
            E = "t"
            for oh in range(2 if PROLOG >= 1 else 0):
                tr.wait(e, E, "b2bf", "c_or")
                tr.minc(e.matmul(pb[oh][:, :], onesr[0:1, :],
                                b2bf[0:1, oh * 512:(oh + 1) * 512],
                                start=True, stop=True),
                       "t", key=("b2m", oh))
            def w1fold():
              for dc in range(8 if PROLOG >= 2 else 0):
                for hs in range(8):
                    i, rsl = hs // 2, hs % 2
                    tr.wait(e, E, ("xl8", 0), ("xl8", 1), ("xl8", 2))
                    if hs >= 4:
                        tr.wait(e, E, ("w1e", dc, hs - 4))
                    elif dc > 0:
                        tr.wait(e, E, ("w1e", dc - 1, hs + 4))
                    tr.minc(e.matmul(
                        pb[4 + hs % 4][:, :],
                        xl8(2 * i + 1, dc * 128, (dc + 1) * 128),
                        xl8(2 * i, rsl * 512, (rsl + 1) * 512),
                        start=True, stop=True),
                        "t", key=("w1m", dc, hs))
            def w2fold():
              for cc in range(32 if PROLOG >= 3 else 0):
                j, c0 = cc // 8, (cc % 8) * 128
                pair = (4, 5) if cc % 2 == 0 else (6, 7)
                for i in range(4):
                    bank = pair[i // 2]
                    off = (i % 2) * 256
                    tr.wait(e, E, ("xl8", 0), ("xl8", 1), ("xl8", 2))
                    if cc < 2:
                        tr.wait(e, E, ("w1e", 7, bank - 4), ("w1e", 7, bank))
                    else:
                        tr.wait(e, E, ("w2e", cc - 2, i // 2))
                    rhs = xl8(2 * i + 8, j * 256, (j + 1) * 256)
                    tr.minc(e.matmul(
                        pb[bank][:, off:off + 256],
                        xl8(2 * i + 9, c0, c0 + 128), rhs,
                        start=True, stop=True),
                        "t", key=("w2m", cc, i // 2) if i % 2 == 1 else None)
            if PHASE < 1:
                w1fold()
                w2fold()
            for t in range(NT1 if PHASE >= 1 else 0):
                for g in range(4):
                    s, half = g // 2, g % 2
                    tr.wait(e, E, ("xn", t, s), "c_id")
                    if g >= 2:
                        tr.wait(e, E, ("tTe", t, g - 2))
                    elif t > 0:
                        tr.wait(e, E, ("krl", t - 1, 4 + g),
                                ("kex", t - 1, 4 + g))
                    else:
                        tr.wait(e, E, ("b2e", g))
                    for q in range(4):
                        fc = half * 4 + q
                        tr.minc(e.matmul(
                            pb[g % 2][:, q * 128:(q + 1) * 128],
                            xn_tm[:, s, fc * 128:(fc + 1) * 128],
                            ident[:, :], start=True, stop=True),
                            "t", key=("tT", t, g) if q == 3 else None)
                for oc in range(8):
                    bank = pb[oc % 4]
                    tr.wait(e, E, ("tTe", t, 3), "wl_wk")
                    if oc < 2:
                        tr.wait(e, E, ("tTe", t, 2 + oc))
                    elif oc < 4:
                        tr.wait_opt(e, E, ("ge", t - 1, 28 + oc)
                                    if PHASE >= 2 else ("krl", t - 1, 4 + oc))
                    else:
                        tr.wait(e, E, ("krl", t, oc - 4), ("kex", t, oc - 4))
                    for dc in range(8):
                        tr.minc(e.matmul(
                            bank[:, 0:256], wk_s(dc, oc), xnT1[:, dc, :],
                            start=(dc == 0), stop=(dc == 7)),
                            "t", key=("kmm", t, oc) if dc == 7 else None)
                if t == 0:
                    w1fold()
                    w2fold()
                for cc in range(33 if PHASE >= 2 else 0):
                    if cc < 32:
                        tr.wait(e, E, ("w1e", 7, 7))
                        if cc < 2:
                            tr.wait(e, E, ("krl", t, 6 + cc),
                                    ("kex", t, 6 + cc))
                        else:
                            tr.wait(e, E, ("ge", t, cc - 2))
                        for dc in range(8):
                            tr.minc(e.matmul(
                                pb[2 + cc % 2][:, 0:256],
                                w1eff(dc, cc * 128, (cc + 1) * 128),
                                xnT1[:, dc, :],
                                start=(dc == 0), stop=(dc == 7)),
                                "t", key=("h", t, cc) if dc == 7 else None)
                    if cc >= 1:
                        ccw = cc - 1
                        tr.wait(e, E, ("ge", t, ccw), ("w2e", 31, 1))
                        if ccw == 0:
                            if t == 0:
                                tr.wait(e, E, ("w2e", 30, 0), ("w2e", 30, 1),
                                        ("w2e", 31, 0), ("w2e", 31, 1))
                            else:
                                tr.wait(e, E, *[("p1a", t - 1, s, oh)
                                                for s in range(2)
                                                for oh in range(2)])
                        nmm = 0
                        for s in range(2):
                            for oh in range(2):
                                nmm += 1
                                tr.minc(e.matmul(
                                    pb[4 + s * 2 + oh][:, :],
                                    hc[:, ccw % 2, s * 128:(s + 1) * 128],
                                    w2eff(ccw, oh * 512, (oh + 1) * 512),
                                    start=(ccw == 0), stop=(ccw == 31)),
                                    "t", key=("fgrp", t, ccw)
                                    if nmm == 4 else None)
            def s2_q(u, lo=0, hi=8):
                for oc in range(lo, hi):
                    tr.wait(e, E, ("xnld", u), "wl_wq")
                    if oc < 2:
                        if u == 0:
                            tr.wait(e, E, ("tTe", NT1 - 1, 2 + oc))
                        else:
                            tr.wait(e, E, ("sgev", u - 1, 6 + oc))
                    else:
                        tr.wait(e, E, ("qrl", u, oc - 2), ("qex", u, oc - 2))
                    for dc in range(8):
                        tr.minc(e.matmul(
                            pb[oc % 2][:, :], wq_s(dc, oc), xnT2v[:, dc, :],
                            start=(dc == 0), stop=(dc == 7)),
                            "t", key=("qmm", u, oc) if dc == 7 else None)

            def s2_v(u):
                for oc in range(8):
                    tr.wait(e, E, "wl_wv")
                    if oc < 2:
                        if u == 0:
                            tr.wait(e, E, ("ge", NT1 - 1, 30 + oc))
                        else:
                            tr.wait(e, E, ("oev", u - 1, 6 + oc))
                    else:
                        tr.wait(e, E, ("vev", u, oc - 2))
                    for dc in range(8):
                        tr.minc(e.matmul(
                            pb[2 + oc % 2][:, :], wv_s(dc, oc),
                            xnT2v[:, dc, :],
                            start=(dc == 0), stop=(dc == 7)),
                            "t", key=("vmm", u, oc) if dc == 7 else None)

            def s2_g(u):
                for oc in range(8):
                    tr.wait(e, E, "wl_wg")
                    if oc < 2:
                        tr.wait(e, E, ("qrl", u, 6 + oc), ("qex", u, 6 + oc))
                    else:
                        tr.wait(e, E, ("sgev", u, oc - 2))
                    for dc in range(8):
                        tr.minc(e.matmul(
                            pb[oc % 2][:, :], wg_s(dc, oc), xnT2v[:, dc, :],
                            start=(dc == 0), stop=(dc == 7)),
                            "t", key=("gmm", u, oc) if dc == 7 else None)

            for u in range(NT2 if PHASE >= 4 else 0):
                if u == 0:
                    s2_q(0)
                    s2_v(0)
                    s2_g(0)
                for fc in range(8):
                    tr.wait(e, E, ("prod", u, fc), "c_sa")
                    if fc == 0:
                        if u == 0:
                            tr.wait(e, E, ("p1a", NT1 - 1, 0, 0))
                        else:
                            tr.wait(e, E, ("rmul", u - 1, 1))
                    tr.inc(e.matmul(
                        pb[4][0:16, :], selA[:, fc * 16:(fc + 1) * 16],
                        prodp[:, fc % 2, :],
                        start=(fc == 0), stop=(fc == 7)),
                        "t", key=("qkp", u, fc))
                    tr.wait(e, E, ("qst3", u, fc))
                    if fc == 0:
                        if u == 0:
                            tr.wait(e, E, ("p1a", NT1 - 1, 0, 1), ("sak", 7))
                        else:
                            tr.wait(e, E, ("rrec1", u - 1, 1))
                    tr.inc(e.matmul(
                        pb[5][0:16, :], selAk[:, fc * 16:(fc + 1) * 16],
                        qbf[:, fc, :],
                        start=(fc == 0), stop=(fc == 7)),
                        "t", key=("nmp", u, fc))
                for fc in range(8):
                    for hh in range(2):
                        cs = slice(hh * 256, (hh + 1) * 256)
                        tr.wait(e, E, ("rmul", u, hh), "c_sb")
                        if hh == 0:
                            if fc < 2:
                                tr.wait(e, E, ("vev", u, 6 + fc))
                            else:
                                tr.wait(e, E, ("zev", u, fc - 2))
                        tr.minc(e.matmul(
                            pb[2 + fc % 2][:, cs],
                            selB[:, fc * 128:(fc + 1) * 128], rqk[:, cs],
                            start=True, stop=True),
                            "t", key=("rbc", u, fc, hh)
                            if hh == 1 else None)
                for fc in range(8):
                    tr.wait(e, E, ("zev", u, fc), "c_o1")
                    if fc == 0:
                        if u == 0:
                            tr.wait(e, E, ("p1a", NT1 - 1, 1, 0))
                        else:
                            tr.wait(e, E, ("t1", u - 1, 7))
                    tr.inc(e.matmul(
                        pb[6][0:1, :], ones1[:, :], vbf[:, fc, :],
                        start=(fc == 0), stop=(fc == 7)),
                        "t", key=("smp", u, fc))
                    tr.wait(e, E, ("zsq", u, fc))
                    if fc == 0:
                        if u == 0:
                            tr.wait(e, E, ("p1a", NT1 - 1, 1, 1))
                        else:
                            tr.wait(e, E, ("t2", u - 1, 7))
                    tr.inc(e.matmul(
                        pb[7][0:1, :], ones1[:, :], prodp[:, fc % 2, :],
                        start=(fc == 0), stop=(fc == 7)),
                        "t", key=("sqp", u, fc))
                if u + 1 < NT2:
                    s2_q(u + 1, 0, 2)
                tr.wait(e, E, ("rstdbf", u), ("mu", u), "c_or")
                tr.inc(e.matmul(pb[6][:, :], onesr[0:1, :], rstdb[0:1, :],
                                start=True, stop=True),
                       "t", key=("b1m", u))
                tr.wait(e, E, ("mwr0", u), ("var", u))
                tr.inc(e.matmul(pb[7][:, :], onesr[0:1, :], mwr[0:1, :],
                                start=True, stop=True),
                       "t", key=("b2m2", u))
                for g in range(8):
                    s, oh = g // 2, g % 2
                    if g == 0:
                        tr.wait(e, E, "wl_wo")
                    if g < 2:
                        tr.wait(e, E, ("zev", u, 6 + g))
                    else:
                        tr.wait(e, E, ("oev", u, g - 2))
                    for fc in range(8):
                        if g == 0:
                            tr.wait(e, E, ("zn", u, fc))
                        tr.minc(e.matmul(
                            pb[2 + g % 2][:, :],
                            znb[:, fc, s * 128:(s + 1) * 128],
                            wo_s(fc, oh * 512, (oh + 1) * 512),
                            start=(fc == 0), stop=(fc == 7)),
                            "t", key=("womm", u, g) if fc == 7 else None)
                if u + 1 < NT2:
                    s2_q(u + 1, 2, 8)
                    s2_v(u + 1)
                    s2_g(u + 1)

        bodies = [("sync", body_sync), ("g", body_gpsimd),
                  ("v", body_vector), ("s", body_scalar),
                  ("t", body_tensor)]

        fake = _FakeEng()
        for _, b in bodies:
            b(fake)
        tr.begin_emit()
        block = ctx.enter_context(nc.Block())
        block.sync(body_sync)
        block.gpsimd(body_gpsimd)
        block.vector(body_vector)
        block.scalar(body_scalar)
        block.tensor(body_tensor)
        tr.check()
    return nc


def _rearr_w(w):
    M = w.shape[1]
    return np.ascontiguousarray(
        w.reshape(8, 128, M).transpose(1, 0, 2).reshape(128, 8 * M))


def _rearr_w32(w):
    M = w.shape[1]
    return np.ascontiguousarray(
        w.reshape(32, 128, M).transpose(1, 0, 2).reshape(128, 32 * M))


_E4PERM = (np.arange(1024).reshape(256, 4).T.reshape(-1))


def xl16_host(lt):
    rows = []
    for lti in [0, 1, 6, 7, 2, 3, 8, 9, 10, 11, 4, 5, 12, 13, 14, 15]:
        r = lt[lti]
        if lti >= 8 and lti % 2 == 0:
            r = r[_E4PERM]
        rows.append(r)
    return np.ascontiguousarray(np.stack(rows))


def _consts():
    ident = np.eye(128, dtype=ml_dtypes.bfloat16)
    selA = np.zeros((128, 8 * 16), dtype=ml_dtypes.bfloat16)
    selB = np.zeros((32, 8 * 128), dtype=ml_dtypes.bfloat16)
    for fc in range(8):
        for p in range(128):
            h = 2 * fc + p // 64
            selA[p, fc * 16 + h] = 1
            selB[h, fc * 128 + p] = 1
    ones1 = np.ones((128, 1), dtype=ml_dtypes.bfloat16)
    onesr = np.ones((1, 128), dtype=ml_dtypes.bfloat16)
    return ident, selA, selB, ones1, onesr


def prepare(L0_attn_hidden_states, L0_attn_Wq, L0_attn_Wk, L0_attn_Wv,
            L0_attn_Wg, L0_attn_Wo, L0_attn_gamma, L0_attn_beta,
            L0_ffn_W1, L0_ffn_b1, L0_ffn_W2, L0_ffn_b2, L0_ffn_scale):
    x = np.asarray(L0_attn_hidden_states)
    B, S, _ = x.shape
    t_core = B * S // NCORES
    cores_per_b = NCORES // B

    wq = _rearr_w(np.asarray(L0_attn_Wq))
    wk = _rearr_w(np.asarray(L0_attn_Wk))
    wv = _rearr_w(np.asarray(L0_attn_Wv))
    wg = _rearr_w(np.asarray(L0_attn_Wg))
    wo = _rearr_w(np.asarray(L0_attn_Wo))
    w1s = _rearr_w(np.ascontiguousarray(np.asarray(L0_ffn_W1).T))
    w2s = _rearr_w32(np.ascontiguousarray(np.asarray(L0_ffn_W2).T))
    b1c = np.ascontiguousarray(np.asarray(L0_ffn_b1).reshape(32, 128).T)
    b2r = np.ascontiguousarray(np.asarray(L0_ffn_b2).reshape(1, D))
    gbcol = np.ascontiguousarray(np.concatenate(
        [np.asarray(L0_attn_gamma).reshape(8, 128).T,
         np.asarray(L0_attn_beta).reshape(8, 128).T], axis=1))
    scl = np.ascontiguousarray(
        np.broadcast_to(np.asarray(L0_ffn_scale).reshape(1, 1), (128, 1))
    ).astype(np.float32)
    ident, selA, selB, ones1, onesr = _consts()

    nc = build_program(t_core)
    in_maps = []
    for c in range(NCORES):
        b = c // cores_per_b
        q = c % cores_per_b
        mkb = np.zeros((128, 2), dtype=np.float32)
        mkb[:, b] = 1.0
        in_maps.append(dict(
            x_in=np.ascontiguousarray(x[b, q * t_core:(q + 1) * t_core]),
            xl16=xl16_host(x[b, 1:17]),
            wq=wq, wk=wk, wv=wv, wg=wg, wo=wo, w1s=w1s, w2s=w2s,
            b1c=b1c, b2r=b2r, gbcol=gbcol, scl=scl, mkb=mkb,
            idn=ident, selA=selA, selB=selB, ones128=ones1, onesrow=onesr,
        ))
    return nc, in_maps


def assemble(results, B, S):
    t_core = B * S // NCORES
    cores_per_b = NCORES // B
    out = np.empty((B, S, D), dtype=np.float32)
    for c in range(NCORES):
        b = c // cores_per_b
        q = c % cores_per_b
        out[b, q * t_core:(q + 1) * t_core] = results[c]["out"]
    return out


def kernel(L0_attn_hidden_states, **kw):
    x = np.asarray(L0_attn_hidden_states)
    nc, in_maps = prepare(L0_attn_hidden_states, **kw)
    res = run_bass_kernel_spmd(nc, in_maps, core_ids=list(range(NCORES)))
    return assemble(res.results, x.shape[0], x.shape[1])


# version 16
# speedup vs baseline: 1.0126x; 1.0126x over previous
import os
import numpy as np
import ml_dtypes
import concourse.bass as bass
import concourse.mybir as mybir
from concourse.bass_utils import run_bass_kernel_spmd
from contextlib import ExitStack

F32 = mybir.dt.float32
BF16 = mybir.dt.bfloat16
AF = mybir.ActivationFunctionType
ALU = mybir.AluOpType
AX = mybir.AxisListType

D = 1024
HF = 4096
NCORES = 8
PHASE = int(os.environ.get("KPHASE", "4"))
PROLOG = int(os.environ.get("KPROLOG", "3"))


class _FakeInstr:
    def then_inc(self, sem, n=1):
        return self


class _FakeEng:
    def __getattr__(self, name):
        return lambda *a, **k: _FakeInstr()


class _Tr:

    def __init__(self, sems):
        self.sems = sems
        self.plan = True
        self.count = {k: 0 for k in sems}
        self.evt = {}
        self.waited = {}

    def begin_emit(self):
        assert self.plan
        self._plan_count = dict(self.count)
        self.plan = False
        self.count = {k: 0 for k in self.sems}
        self.waited = {}
        for q in getattr(self, "queues", []):
            q.cnt = 0

    def wait_val(self, eng, ename, sem, val):
        if self.plan or val <= 0:
            return
        w = self.waited.setdefault(ename, {})
        if w.get(sem, 0) >= val:
            return
        eng.wait_ge(self.sems[sem], val)
        w[sem] = val

    def check(self):
        assert self.count == self._plan_count, (self.count, self._plan_count)

    def inc(self, instr, sem, n=1, key=None):
        self.count[sem] += n
        tok = (sem, self.count[sem])
        if self.plan:
            if key is not None:
                assert key not in self.evt, key
                self.evt[key] = tok
        else:
            instr.then_inc(self.sems[sem], n)
            if key is not None:
                assert self.evt[key] == tok, (key, self.evt[key], tok)
        return tok

    def minc(self, instr, sem, key=None):
        if key is None:
            return None
        return self.inc(instr, sem, 1, key=key)

    def wait(self, eng, ename, *keys):
        if self.plan:
            return
        w = self.waited.setdefault(ename, {})
        for key in keys:
            if key is None:
                continue
            sem, val = self.evt[key]
            if val <= 0 or w.get(sem, 0) >= val:
                continue
            eng.wait_ge(self.sems[sem], val)
            w[sem] = val

    def wait_opt(self, eng, ename, *keys):
        if self.plan:
            return
        self.wait(eng, ename, *[k for k in keys if k in self.evt])


class _DmaQ:

    def __init__(self, tr, base, n):
        self.tr, self.n = tr, n
        self.names = [f"{base}{i}" for i in range(n)]
        self.cnt = 0
        tr.queues = getattr(tr, "queues", []) + [self]

    def issue(self, eng, ename, key, fn):
        i = self.cnt
        self.cnt += 1
        nm = self.names[i % self.n]
        self.tr.wait_val(eng, ename, nm, 16 * (i // self.n))
        self.tr.inc(fn(), nm, 16, key=key)


def build_program(t_core):
    NT1 = t_core // 256
    NT2 = t_core // 512
    assert NT1 * 256 == t_core and NT2 * 512 == t_core

    nc = bass.Bass("TRN2", target_bir_lowering=False)

    x_in = nc.dram_tensor("x_in", [t_core, D], F32, kind="ExternalInput").ap()
    xl16_d = nc.dram_tensor("xl16", [16, D], F32, kind="ExternalInput").ap()
    wq_d = nc.dram_tensor("wq", [128, 8 * D], F32, kind="ExternalInput").ap()
    wk_d = nc.dram_tensor("wk", [128, 8 * D], F32, kind="ExternalInput").ap()
    wv_d = nc.dram_tensor("wv", [128, 8 * D], F32, kind="ExternalInput").ap()
    wg_d = nc.dram_tensor("wg", [128, 8 * D], F32, kind="ExternalInput").ap()
    wo_d = nc.dram_tensor("wo", [128, 8 * D], F32, kind="ExternalInput").ap()
    w1_d = nc.dram_tensor("w1s", [128, 8 * HF], F32, kind="ExternalInput").ap()
    w2_d = nc.dram_tensor("w2s", [128, 32 * D], F32, kind="ExternalInput").ap()
    b1_d = nc.dram_tensor("b1c", [128, 32], F32, kind="ExternalInput").ap()
    b2_d = nc.dram_tensor("b2r", [1, D], F32, kind="ExternalInput").ap()
    gb_d = nc.dram_tensor("gbcol", [128, 16], F32, kind="ExternalInput").ap()
    sc_d = nc.dram_tensor("scl", [128, 1], F32, kind="ExternalInput").ap()
    mb_d = nc.dram_tensor("mkb", [128, 2], F32, kind="ExternalInput").ap()
    id_d = nc.dram_tensor("idn", [128, 128], BF16, kind="ExternalInput").ap()
    sa_d = nc.dram_tensor("selA", [128, 8 * 16], BF16, kind="ExternalInput").ap()
    sb_d = nc.dram_tensor("selB", [32, 8 * 128], BF16, kind="ExternalInput").ap()
    o1_d = nc.dram_tensor("ones128", [128, 1], BF16, kind="ExternalInput").ap()
    or_d = nc.dram_tensor("onesrow", [1, 128], BF16, kind="ExternalInput").ap()

    out_d = nc.dram_tensor("out", [t_core, D], F32, kind="ExternalOutput").ap()

    xnt_scr = nc.dram_tensor("xnt_scr", [NT2, 128, 8, 512], BF16).ap()
    k_scr = nc.dram_tensor("k_scr", [NT2, 128, 8, 512], BF16).ap()
    p1_scr = nc.dram_tensor("p1_scr", [NT2, 128, 4, D], F32).ap()
    cc_in = nc.dram_tensor("cc_in", [2, 128, 8], F32).ap()
    cc_out = nc.dram_tensor("cc_out", [2, 128, 8], F32).ap()

    ctx = ExitStack()
    with ctx:
        def sbuf(name, shape, dtype):
            return ctx.enter_context(nc.sbuf_tensor(name, shape, dtype))

        wA = sbuf("wA", [128, 73728], BF16)

        def w1eff(dc, lo, hi):
            return wA[:, dc * HF + lo: dc * HF + hi]

        def w2eff(cc, lo, hi):
            return wA[:, 32768 + cc * D + lo: 32768 + cc * D + hi]

        def wk_s(dc, oc):
            b = 65536 + dc * D + oc * 128
            return wA[:, b: b + 128]

        def wq_s(dc, oc):
            b = 65536 + dc * D + oc * 128
            return wA[:, b: b + 128]

        def wv_s(dc, oc):
            b = 49152 + dc * D + oc * 128
            return wA[:, b: b + 128]

        def wg_s(dc, oc):
            b = 57344 + dc * D + oc * 128
            return wA[:, b: b + 128]

        def wo_s(fc, lo, hi):
            return wA[:, fc * D + lo: fc * D + hi]

        xnT2v = wA[:, 32768:36864].rearrange("p (c n) -> p c n", c=8)
        k2v = wA[:, 36864:40960].rearrange("p (c n) -> p c n", c=8)
        qbf = wA[:, 40960:45056].rearrange("p (c n) -> p c n", c=8)
        vbf = wA[:, 45056:49152].rearrange("p (c n) -> p c n", c=8)
        sgbf = wA[:, 8192:12288].rearrange("p (c n) -> p c n", c=8)
        znb = wA[:, 12288:16384].rearrange("p (c n) -> p c n", c=8)

        aB = sbuf("aB", [128, 9728], BF16)
        xn_tm = aB[:, 0:2048].rearrange("p (s d) -> p s d", s=2)
        xnT1 = aB[:, 2048:4096].rearrange("p (c n) -> p c n", c=8)
        hc = aB[:, 4096:4608].rearrange("p (b n) -> p b n", b=2)
        kbf = aB[:, 4608:6656].rearrange("p (c n) -> p c n", c=8)
        mn1 = aB[:, 6656:7168].rearrange("p (b n) -> p b n", b=2)
        ex1 = aB[:, 9216:9728].rearrange("p (b n) -> p b n", b=2)
        xsq = aB[:, 7168:9216].rearrange("p (s d) -> p s d", s=2)
        prodp = aB[:, 0:1024].rearrange("p (b n) -> p b n", b=2)
        nprodp = aB[:, 1024:2048].rearrange("p (b n) -> p b n", b=2)
        t1p = aB[:, 2048:3072].rearrange("p (b n) -> p b n", b=2)
        mn2 = aB[:, 3072:4096].rearrange("p (b n) -> p b n", b=2)
        ex2 = aB[:, 5632:6656].rearrange("p (b n) -> p b n", b=2)
        xsq16 = aB[0:16, 7168:8192]

        aF = sbuf("aF", [128, 6656], F32)
        xF = aF[:, 0:4096].rearrange("p (b s d) -> p b s d", b=2, s=2)
        p1t = aF[:, 4096:6144].rearrange("p (s d) -> p s d", s=2)

        st1 = aF[:, 6144:6160]
        p1in = aF[:, 0:4096].rearrange("p (s d) -> p s d", s=4)
        recf = aF[0:16, 4096:4608]
        stat2 = aF[0:1, 4608:6656]
        xl16 = aF[0:16, 4096:5120]

        st16 = sbuf("st16_s", [16, 8], F32)
        xl8b = sbuf("xl8b_s", [128, 6144], BF16)
        b2bf = aB[0:1, 0:1024]
        rqk = aB[0:32, 4096:4608]
        mwr = aB[0:2, 4608:5120]
        rstdb = aB[0:1, 5120:5632]
        ident = sbuf("ident_s", [128, 128], BF16)
        selA = sbuf("selA_s", [128, 8 * 16], BF16)
        selB = sbuf("selB_s", [32, 8 * 128], BF16)
        ones1 = sbuf("ones1_s", [128, 1], BF16)
        onesr = sbuf("onesr_s", [1, 128], BF16)
        b1c = sbuf("b1c_s", [128, 32], F32)
        b2rep = sbuf("b2rep_s", [128, D], BF16)
        gbcol = sbuf("gbcol_s", [128, 16], F32)
        scl = sbuf("scl_s", [128, 1], F32)
        mkb = sbuf("mkb_s", [128, 2], F32)
        ksum_r = sbuf("ksum_r_s", [128, 8], F32)
        kredb = sbuf("kredb_s", [128, 8], F32)
        ccin_s = sbuf("ccin_s_s", [128, 16], F32)
        ccout_s = sbuf("ccout_s_s", [128, 16], F32)
        ksum_g = sbuf("ksum_g_s", [128, 8], F32)
        selAk = sbuf("selAk_s", [128, 8 * 16], BF16)

        pb = [ctx.enter_context(nc.psum_tensor(f"pb{i}", [128, 512], F32))
              for i in range(8)]

        sems = {}
        for s_ in (["cc", "v", "s", "t", "g"]
                   + [f"ld{i}" for i in range(4)]
                   + [f"st{i}" for i in range(4)]
                   + [f"wl{i}" for i in range(2)]):
            sems[s_] = ctx.enter_context(nc.semaphore("sem_" + s_))
        tr = _Tr(sems)
        Qld = _DmaQ(tr, "ld", 4)
        Qst = _DmaQ(tr, "st", 4)
        Qwl = _DmaQ(tr, "wl", 2)

        _PAIRMAP = {0: (0, 0), 1: (32, 0), 2: (64, 0), 3: (0, 1),
                    4: (32, 1), 5: (32, 2), 6: (64, 1), 7: (64, 2)}

        def xl8(j, lo, hi):
            pair = (j // 2) if j < 8 else (4 + (j - 8) // 2)
            base, blk = _PAIRMAP[pair]
            c0 = blk * 2048 + (j % 2) * 1024
            return xl8b[base:base + 1, c0 + lo: c0 + hi]

        RE = lambda i: 1.0 / i

        def body_sync(e):
            E = "sync"
            Qld.issue(e, E, "ld_xl",
                      lambda: e.dma_start(out=xl16, in_=xl16_d))
            for t in range(min(2, NT1) if PHASE >= 1 else 0):
                Qld.issue(e, E, ("xld", t), lambda t=t: e.dma_start(
                    out=xF[:, t % 2, :, :],
                    in_=x_in[t * 256:(t + 1) * 256, :].rearrange(
                        "(s p) d -> p s d", p=128)))
            for key_, ap_d, ap_s in [
                    ("c_id", id_d, ident[:, :]), ("c_sa", sa_d, selA[:, :]),
                    ("c_sb", sb_d, selB[:, :]), ("c_o1", o1_d, ones1[:, :]),
                    ("c_or", or_d, onesr[:, :]), ("c_b1", b1_d, b1c[:, :]),
                    ("c_sc", sc_d, scl[:, :]), ("c_mb", mb_d, mkb[:, :])]:
                Qld.issue(e, E, key_,
                          lambda a=ap_s, b=ap_d: e.dma_start(out=a, in_=b))
            tr.wait(e, E, "xnl16")
            Qld.issue(e, E, ("xl8", 0), lambda: e.dma_start(
                out=xl8b[0:1, 0:4096], in_=xsq16[0:4, :]))
            Qld.issue(e, E, ("xl8", 1), lambda: e.dma_start(
                out=xl8b[32:33, 0:6144], in_=xsq16[4:10, :]))
            Qld.issue(e, E, ("xl8", 2), lambda: e.dma_start(
                out=xl8b[64:65, 0:6144], in_=xsq16[10:16, :]))
            if PHASE < 1:
                if PROLOG >= 3:
                    tr.wait(e, E, ("w2e", 31, 1))
                elif PROLOG >= 2:
                    tr.wait(e, E, ("w1e", 7, 7))
                elif PROLOG >= 1:
                    tr.wait(e, E, ("b2e", 1))
                else:
                    tr.wait(e, E, "xnl16")
                return
            for t in range(2, NT1):
                if PHASE >= 2:
                    tr.wait_opt(e, E, ("p1b", t - 2, 1, 1))
                else:
                    tr.wait_opt(e, E, ("xn", t - 2, 1))
                Qld.issue(e, E, ("xld", t), lambda t=t: e.dma_start(
                    out=xF[:, t % 2, :, :],
                    in_=x_in[t * 256:(t + 1) * 256, :].rearrange(
                        "(s p) d -> p s d", p=128)))
            if PHASE < 3:
                tr.wait(e, E, ("xnTst", NT1 - 1), ("kst", NT1 - 1),
                        ("kred", NT1 - 1))
                if PHASE >= 2:
                    tr.wait(e, E, *[("p1st", t)
                                    for t in range(max(0, NT1 - 4), NT1)])
                    tr.wait(e, E, *[("xnTst", t)
                                    for t in range(max(0, NT1 - 4), NT1)])
                    tr.wait(e, E, *[("kst", t)
                                    for t in range(max(0, NT1 - 4), NT1)])
                return
            tr.wait(e, E, "ccin_v")
            Qld.issue(e, E, "ccin_d", lambda: e.dma_start(
                out=cc_in.rearrange("b p c -> p b c"),
                in_=ccin_s[:, :].rearrange("p (b c) -> p b c", b=2)))
            tr.wait(e, E, "cc_done")
            Qld.issue(e, E, "ccout_d", lambda: e.dma_start(
                out=ccout_s[:, :].rearrange("p (b c) -> p b c", b=2),
                in_=cc_out.rearrange("b p c -> p b c")))
            if PHASE < 4:
                tr.wait(e, E, "ccout_d")
                return
            for u in range(NT2):
                if u == 0:
                    tr.wait(e, E, ("fgrp", NT1 - 1, 3))
                else:
                    tr.wait(e, E, ("gmm", u - 1, 7))
                tr.wait(e, E, ("xnTst", 2 * u), ("xnTst", 2 * u + 1))
                Qld.issue(e, E, ("xnld", u), lambda u=u: e.dma_start(
                    out=xnT2v, in_=xnt_scr[u]))
                if u > 0:
                    tr.wait(e, E, ("prod", u - 1, 7))
                if u == 0:
                    tr.wait(e, E, ("fgrp", NT1 - 1, 7))
                tr.wait(e, E, ("kst", 2 * u), ("kst", 2 * u + 1))
                Qld.issue(e, E, ("kld", u), lambda u=u: e.dma_start(
                    out=k2v, in_=k_scr[u]))
                if u > 0:
                    tr.wait(e, E, ("ost", u - 1))
                else:
                    tr.wait(e, E, ("p1b", NT1 - 1, 1, 1))
                tr.wait(e, E, ("p1st", 2 * u), ("p1st", 2 * u + 1))
                Qld.issue(e, E, ("p1ld", u), lambda u=u: e.dma_start(
                    out=p1in, in_=p1_scr[u]))
            tr.wait(e, E, *[("ost", u)
                            for u in range(max(0, NT2 - 4), NT2)])

        def body_gpsimd(e):
            E = "g"
            Qwl.issue(e, E, "gbcol",
                      lambda: e.dma_start(out=gbcol[:, :], in_=gb_d))
            Qwl.issue(e, E, "b2bf",
                      lambda: e.dma_start(out=b2bf[:, :], in_=b2_d))
            tr.wait(e, E, "ld_xl")
            Qwl.issue(e, E, "wl_wk",
                      lambda: e.dma_start(out=wA[:, 65536:73728], in_=wk_d))
            if PHASE >= 1:
                tr.wait(e, E, ("xl8", 2), ("xld", 1))
            for dc in range(8):
                Qwl.issue(e, E, ("wl_w1", dc), lambda dc=dc: e.dma_start(
                    out=wA[:, dc * HF:(dc + 1) * HF],
                    in_=w1_d[:, dc * HF:(dc + 1) * HF]))
            for j in range(8):
                Qwl.issue(e, E, ("wl_w2", j), lambda j=j: e.dma_start(
                    out=wA[:, 32768 + j * HF: 32768 + (j + 1) * HF],
                    in_=w2_d[:, j * HF:(j + 1) * HF]))
            if PHASE < 3:
                return
            for t in range(NT1 if PHASE >= 1 else 0):
                pass
            if PHASE < 3:
                return
            if PHASE >= 4:
                tr.wait(e, E, ("kmm", NT1 - 1, 7))
                Qwl.issue(e, E, "wl_wq", lambda: e.dma_start(
                    out=wA[:, 65536:73728], in_=wq_d))
                tr.wait(e, E, ("fgrp", NT1 - 1, 23))
                Qwl.issue(e, E, "wl_wv", lambda: e.dma_start(
                    out=wA[:, 49152:57344], in_=wv_d))
            tr.wait(e, E, "ccin_d")
            tr.inc(e.collective_compute(
                "AllReduce", ALU.add, replica_groups=[list(range(NCORES))],
                ins=[cc_in[:, :, :]], outs=[cc_out[:, :, :]]),
                "cc", 1, key="cc_done")
            if PHASE < 4:
                return
            tr.wait(e, E, ("fgrp", NT1 - 1, 31))
            Qwl.issue(e, E, "wl_wg", lambda: e.dma_start(
                out=wA[:, 57344:65536], in_=wg_d))
            tr.wait(e, E, ("h", NT1 - 1, 31))
            Qwl.issue(e, E, "wl_wo", lambda: e.dma_start(
                out=wA[:, 0:8192], in_=wo_d))
            for u in range(NT2):

                for fc in range(8):
                    tr.wait(e, E, ("t3", u, fc), ("sgev", u, fc))
                    if u == 0 and fc < 2:
                        tr.wait(e, E, ("h", NT1 - 1, 31))
                    tr.wait_opt(e, E, ("womm", u - 1, 7))
                    tr.inc(e.tensor_tensor(
                        out=znb[:, fc, :], in0=t1p[:, fc % 2, :],
                        in1=sgbf[:, fc, :], op=ALU.mult),
                        "g", key=("zn", u, fc))

        def body_vector(e):
            E = "v"
            tr.wait(e, E, "ld_xl")
            tr.inc(e.tensor_tensor(out=xsq16, in0=xl16, in1=xl16,
                                   op=ALU.mult), "v", key=("pl", 0))
            tr.inc(e.reduce_sum(out=st16[:, 0:1], in_=xl16, axis=AX.X),
                   "v", key=("pl", 1))
            tr.wait(e, E, ("pl", 0))
            tr.inc(e.reduce_sum(out=st16[:, 1:2], in_=xsq16, axis=AX.X),
                   "v", key=("pl", 2))
            tr.wait(e, E, ("pl", 1), ("pl", 2))
            tr.inc(e.scalar_tensor_tensor(
                out=st16[:, 2:3], in0=st16[:, 0:1], scalar=RE(D),
                in1=st16[:, 0:1], op0=ALU.mult, op1=ALU.mult),
                "v", key=("pl", 3))
            tr.wait(e, E, ("pl", 3))
            tr.inc(e.tensor_sub(st16[:, 3:4], st16[:, 1:2], st16[:, 2:3]),
                   "v", key=("pl", 4))
            tr.wait(e, E, ("pl", 4))
            tr.inc(e.tensor_scalar_add(st16[:, 4:5], st16[:, 3:4], D * 1e-5),
                   "v", key=("pl", 5))
            tr.wait(e, E, ("pl", 5))
            tr.inc(e.reciprocal(st16[:, 5:6], st16[:, 4:5]),
                   "v", key=("pl", 6))
            tr.wait(e, E, "pl_sqrt")
            tr.inc(e.scalar_tensor_tensor(
                out=st16[:, 7:8], in0=st16[:, 0:1], scalar=-RE(D),
                in1=st16[:, 6:7], op0=ALU.mult, op1=ALU.mult),
                "v", key=("pl", 7))
            tr.wait(e, E, ("pl", 7))
            tr.inc(e.tensor_scalar(
                out=xsq16, in0=xl16, scalar1=st16[:, 6:7],
                scalar2=st16[:, 7:8], op0=ALU.mult, op1=ALU.add),
                "v", key="xnl16")
            tr.inc(e.memset(ksum_r[:, :], 0.0), "v", key="ksum0")
            for oh in range(2 if PROLOG >= 1 else 0):
                tr.wait(e, E, ("b2m", oh))
                tr.inc(e.tensor_copy(out=b2rep[:, oh * 512:(oh + 1) * 512],
                                     in_=pb[oh][:, :]),
                       "v", key=("b2e", oh))
            def v_folds():
                for dc in range(8 if PROLOG >= 2 else 0):
                    for hs in range(8):
                        tr.wait(e, E, ("w1m", dc, hs), ("wl_w1", dc), "c_sc")
                        sl_ = w1eff(dc, hs * 512, (hs + 1) * 512)
                        tr.inc(e.scalar_tensor_tensor(
                            out=sl_, in0=pb[4 + hs % 4][:, :],
                            scalar=scl[:, 0:1], in1=sl_,
                            op0=ALU.mult, op1=ALU.add),
                            "v", key=("w1e", dc, hs))
                for cc in range(32 if PROLOG >= 3 else 0):
                    pair = (4, 5) if cc % 2 == 0 else (6, 7)
                    for oh in range(2):
                        tr.wait(e, E, ("w2m", cc, oh), ("wl_w2", cc // 4),
                                "c_sc")
                        sl_ = w2eff(cc, oh * 512, (oh + 1) * 512)
                        tr.inc(e.scalar_tensor_tensor(
                            out=sl_, in0=pb[pair[oh]][:, :],
                            scalar=scl[:, 0:1], in1=sl_,
                            op0=ALU.mult, op1=ALU.add),
                            "v", key=("w2e", cc, oh))

            if PHASE < 1:
                v_folds()
            def ln_stats(t, xb):
                tr.wait(e, E, ("xld", t))
                tr.wait_opt(e, E, ("xn", t - 1, 1))
                tr.inc(e.reduce_sum(out=st1[:, 0:2], in_=xb, axis=AX.X),
                       "v", key=("mus", t))
                for s_ in range(2):
                    tr.wait(e, E, ("xsqs", t, s_))
                    tr.inc(e.reduce_sum(out=st1[:, 2 + s_:3 + s_],
                                        in_=xsq[:, s_, :],
                                        axis=AX.X), "v", key=("sqs", t, s_))
                tr.wait(e, E, ("mus", t), ("sqs", t, 1))
                tr.inc(e.scalar_tensor_tensor(
                    out=st1[:, 4:6], in0=st1[:, 0:2], scalar=RE(D),
                    in1=st1[:, 0:2], op0=ALU.mult, op1=ALU.mult),
                    "v", key=("lnb", t))
                tr.wait(e, E, ("lnb", t))
                tr.inc(e.tensor_sub(st1[:, 6:8], st1[:, 2:4], st1[:, 4:6]),
                       "v", key=("lnc", t))
                tr.wait(e, E, ("lnc", t))
                tr.inc(e.tensor_scalar_add(st1[:, 8:10], st1[:, 6:8],
                                           D * 1e-5), "v", key=("lne", t))
                tr.wait(e, E, ("lne", t))
                tr.inc(e.reciprocal(st1[:, 10:12], st1[:, 8:10]),
                       "v", key=("rec1", t))

            def ln_apply(t, xb):
                tr.wait(e, E, ("rstd1", t))
                tr.inc(e.scalar_tensor_tensor(
                    out=st1[:, 14:16], in0=st1[:, 0:2], scalar=-RE(D),
                    in1=st1[:, 12:14], op0=ALU.mult, op1=ALU.mult),
                    "v", key=("nmr1", t))
                for s in range(2):
                    tr.wait(e, E, ("nmr1", t))
                    tr.wait_opt(e, E, ("tT", t - 1, 2 * s + 1))
                    if t == 0:
                        tr.wait(e, E, "xnl16", ("b2m", 1))
                    tr.inc(e.tensor_scalar(
                        out=xn_tm[:, s, :], in0=xb[:, s, :],
                        scalar1=st1[:, 12 + s:13 + s],
                        scalar2=st1[:, 14 + s:15 + s],
                        op0=ALU.mult, op1=ALU.add),
                        "v", key=("xn", t, s))

            def ln_block(t, xb):
                ln_stats(t, xb)
                ln_apply(t, xb)

            def p1_block(t):
                xb_ = xF[:, t % 2, :, :]
                for s in range(2 if PHASE >= 2 else 0):
                    for oh in range(2):
                        sl_ = p1t[:, s, oh * 512:(oh + 1) * 512]
                        tr.wait(e, E, ("fgrp", t, 31), ("b2e", 1))
                        tr.wait_opt(e, E, ("p1st", t - 1))
                        if t == 0:
                            tr.wait(e, E, "xnl16")
                        tr.inc(e.tensor_add(
                            sl_, pb[4 + s * 2 + oh][:, :],
                            b2rep[:, oh * 512:(oh + 1) * 512]),
                            "v", key=("p1a", t, s, oh))
                        tr.wait(e, E, ("p1a", t, s, oh))
                        tr.inc(e.tensor_add(
                            sl_, sl_, xb_[:, s, oh * 512:(oh + 1) * 512]),
                            "v", key=("p1b", t, s, oh))

            for t in range(NT1 if PHASE >= 1 else 0):
                xb = xF[:, t % 2, :, :]
                if t == 0:
                    ln_block(0, xb)
                def k_evac(oc):
                    bank = pb[oc % 4]
                    tr.wait(e, E, ("kmm", t, oc), ("kex", t, oc))
                    tr.wait_opt(e, E, ("kred", t - 1), ("kst", t - 1))
                    tr.inc(e.tensor_scalar_max(kbf[:, oc, :],
                                               bank[:, 0:256], 0.0),
                           "v", key=("krl", t, oc))
                    tr.wait(e, E, ("krl", t, oc))
                    tr.inc(e.scalar_tensor_tensor(
                        out=kbf[:, oc, :], in0=ex1[:, oc % 2, :], scalar=1.0,
                        in1=kbf[:, oc, :], op0=ALU.min, op1=ALU.add),
                        "v", key=("kst3", t, oc))

                k_evac(0)
                k_evac(1)
                if t + 1 < NT1:
                    ln_stats(t + 1, xF[:, (t + 1) % 2, :, :])
                for oc in range(2, 8):
                    k_evac(oc)

                if t == 0 and PHASE >= 1:
                    v_folds()
                if t + 1 < NT1:
                    ln_apply(t + 1, xF[:, (t + 1) % 2, :, :])
                tr.wait(e, E, *[("kst3", t, oc) for oc in range(8)])
                tr.wait_opt(e, E, ("ksum", t - 1))
                tr.inc(e.reduce_sum(out=kredb[:, :], in_=kbf, axis=AX.X),
                       "v", key=("kred", t))
                tr.wait(e, E, "ksum0", ("kred", t))
                tr.inc(e.tensor_tensor(out=ksum_r[:, :], in0=ksum_r[:, :],
                                       in1=kredb[:, :], op=ALU.add),
                       "v", key=("ksum", t))
                if t + 1 >= NT1 and PHASE >= 3:
                    tr.wait(e, E, ("ksum", NT1 - 1), "c_mb")
                    tr.inc(e.tensor_scalar_mul(ccin_s[:, 0:8], ksum_r[:, :],
                                               mkb[:, 0:1]), "v",
                           key=("ccm", 0))
                    tr.wait(e, E, ("ccm", 0))
                    tr.inc(e.tensor_scalar_mul(ccin_s[:, 8:16], ksum_r[:, :],
                                               mkb[:, 1:2]), "v",
                           key="ccin_v")
                p1_block(t)
            if PHASE < 3:
                return
            tr.wait(e, E, ("fgrp", NT1 - 1, 31), ("kst", NT1 - 1),
                    ("kred", NT1 - 1))
            tr.inc(e.memset(rqk[:, :], 0.0), "v", key="rqk0")
            tr.inc(e.memset(mwr[:, :], 1.0), "v", key="mwr1")

            def v_qrl(u, lo=0, hi=8):
                for oc in range(lo, hi):
                    tr.wait(e, E, ("qmm", u, oc), ("qex", u, oc))
                    if u == 0:
                        tr.wait(e, E, ("fgrp", NT1 - 1, 11))
                    tr.inc(e.tensor_scalar_max(qbf[:, oc, :],
                                               pb[oc % 2][:, :], 0.0),
                           "v", key=("qrl", u, oc))
                    tr.wait(e, E, ("qrl", u, oc), ("qex", u, oc))
                    tr.inc(e.scalar_tensor_tensor(
                        out=qbf[:, oc, :], in0=ex2[:, oc % 2, :], scalar=1.0,
                        in1=qbf[:, oc, :], op0=ALU.min, op1=ALU.add),
                        "v", key=("qst3", u, oc))

            if PHASE >= 4:
                v_qrl(0)
            tr.wait(e, E, "ccout_d")
            tr.inc(e.tensor_scalar_mul(ksum_g[:, :], ccout_s[:, 0:8],
                                       mkb[:, 0:1]), "v", key=("ccg", 0))
            tr.wait(e, E, ("ccg", 0))
            tr.inc(e.scalar_tensor_tensor(
                out=ksum_g[:, :], in0=ccout_s[:, 8:16], scalar=mkb[:, 1:2],
                in1=ksum_g[:, :], op0=ALU.mult, op1=ALU.add),
                "v", key="ksumg")
            for fc in range(8):
                if fc == 0:
                    tr.wait(e, E, "ksumg", "c_sa")
                tr.inc(e.tensor_scalar_mul(
                    selAk[:, fc * 16:(fc + 1) * 16],
                    selA[:, fc * 16:(fc + 1) * 16], ksum_g[:, fc:fc + 1]),
                    "v", key=("sak", fc))
            for u in range(NT2 if PHASE >= 4 else 0):
                for fc in range(8):
                    tr.wait(e, E, ("qst3", u, fc), ("kld", u))
                    tr.wait_opt(e, E, ("qkp", u, fc - 2),
                                ("sqp", u - 1, 6 + fc % 2),
                                ("tT", NT1 - 1, 3) if u == 0 else None)
                    tr.inc(e.tensor_tensor(
                        out=prodp[:, fc % 2, :], in0=qbf[:, fc, :],
                        in1=k2v[:, fc, :], op=ALU.mult),
                        "v", key=("prod", u, fc))
                for hh in range(2):
                    cs = slice(hh * 256, (hh + 1) * 256)
                    if hh == 0:
                        tr.wait(e, E, ("nmp", u, 7))
                        tr.wait_opt(e, E, ("rmul", u - 1, 1),
                                    ("p1st", NT1 - 1) if u == 0 else None)
                    tr.inc(e.tensor_scalar_add(recf[:, cs], pb[5][0:16, cs],
                                               1e-6),
                           "v", key=("rrec1", u, hh))
                    tr.wait(e, E, ("rrec1", u, hh))
                    tr.inc(e.reciprocal(recf[:, cs], recf[:, cs]),
                           "v", key=("rrec2", u, hh))
                    tr.wait(e, E, ("qkp", u, 7), ("rrec2", u, hh), "rqk0")
                    if hh == 0:
                        tr.wait_opt(e, E, ("rbc", u - 1, 7, 1))
                    tr.inc(e.tensor_tensor(out=rqk[0:16, cs],
                                           in0=pb[4][0:16, cs],
                                           in1=recf[:, cs], op=ALU.mult),
                           "v", key=("rmul", u, hh))
                for fc in range(8):
                    tr.wait(e, E, ("rbc", u, fc, 1), ("vev", u, fc))
                    tr.inc(e.tensor_tensor(
                        out=vbf[:, fc, :], in0=vbf[:, fc, :],
                        in1=pb[2 + fc % 2][:, :], op=ALU.mult),
                        "v", key=("zev", u, fc))
                tr.wait(e, E, ("smp", u, 7))
                tr.wait_opt(e, E, ("p1st", NT1 - 1) if u == 0 else
                            ("mwr0", u - 1))
                tr.inc(e.tensor_scalar_mul(stat2[0:1, 0:512], pb[6][0:1, :],
                                           RE(D)), "v", key=("mu", u))
                tr.wait(e, E, ("mu", u))
                tr.inc(e.tensor_tensor(out=stat2[0:1, 512:1024],
                                       in0=stat2[0:1, 0:512],
                                       in1=stat2[0:1, 0:512], op=ALU.mult),
                       "v", key=("musq", u))
                tr.wait(e, E, ("sqp", u, 7), ("musq", u))
                tr.inc(e.scalar_tensor_tensor(
                    out=stat2[0:1, 1024:1536], in0=pb[7][0:1, :],
                    scalar=RE(D), in1=stat2[0:1, 512:1024],
                    op0=ALU.mult, op1=ALU.subtract),
                    "v", key=("var", u))
                tr.wait(e, E, ("var", u))
                tr.inc(e.tensor_scalar_add(stat2[0:1, 1024:1536],
                                           stat2[0:1, 1024:1536], 1e-5),
                       "v", key=("vare", u))
                tr.wait(e, E, ("vare", u))
                if u == 0:
                    tr.wait(e, E, ("rstd1", NT1 - 1))
                tr.inc(e.reciprocal(stat2[0:1, 1536:2048],
                                    stat2[0:1, 1024:1536]),
                       "v", key=("rec2", u))
                tr.wait(e, E, ("rstds", u))
                if u == 0:
                    tr.wait(e, E, ("kst", NT1 - 1))
                tr.wait_opt(e, E, ("b1m", u - 1))
                tr.inc(e.tensor_copy(out=rstdb[0:1, :],
                                     in_=stat2[0:1, 512:1024]),
                       "v", key=("rstdbf", u))
                tr.wait(e, E, "mwr1")
                tr.wait_opt(e, E, ("b2m2", u - 1))
                tr.inc(e.scalar_tensor_tensor(
                    out=mwr[0:1, :], in0=stat2[0:1, 0:512], scalar=-1.0,
                    in1=stat2[0:1, 512:1024], op0=ALU.mult, op1=ALU.mult),
                    "v", key=("mwr0", u))
                for fc in range(8):
                    tr.wait(e, E, ("b1m", u), ("zev", u, fc))
                    if u == 0:
                        tr.wait(e, E, ("xnTst", NT1 - 1))
                    tr.wait_opt(e, E, ("zn", u, fc - 2))
                    tr.inc(e.tensor_tensor(
                        out=t1p[:, fc % 2, :], in0=vbf[:, fc, :],
                        in1=pb[6][:, :], op=ALU.mult),
                        "v", key=("t1", u, fc))
                    tr.wait(e, E, ("b2m2", u), ("t1", u, fc))
                    tr.inc(e.tensor_tensor(
                        out=t1p[:, fc % 2, :], in0=t1p[:, fc % 2, :],
                        in1=pb[7][:, :], op=ALU.add),
                        "v", key=("t2", u, fc))

                if u + 1 < NT2:
                    v_qrl(u + 1, 0, 2)
                for g in range(8):
                    s, oh = g // 2, g % 2
                    tr.wait(e, E, ("womm", u, g), ("p1ld", u))
                    tr.inc(e.tensor_add(
                        p1in[:, s, oh * 512:(oh + 1) * 512],
                        p1in[:, s, oh * 512:(oh + 1) * 512],
                        pb[2 + g % 2][:, :]),
                        "v", key=("oev", u, g))
                if u + 1 < NT2:
                    v_qrl(u + 1, 2, 8)

        def body_scalar(e):
            E = "s"
            tr.wait(e, E, ("pl", 6))
            tr.inc(e.activation(out=st16[:, 6:7], in_=st16[:, 5:6],
                                func=AF.Sqrt, scale=float(D)),
                   "s", key="pl_sqrt")
            def rstd_block(t):
                tr.wait(e, E, ("rec1", t))
                tr.inc(e.activation(out=st1[:, 12:14], in_=st1[:, 10:12],
                                    func=AF.Sqrt, scale=float(D)),
                       "s", key=("rstd1", t))

            for t in range(NT1 if PHASE >= 1 else 0):
                xb = xF[:, t % 2, :, :]
                if t == 0:
                    for s_ in range(2):
                        tr.wait(e, E, ("xld", 0))
                        if s_ == 0:
                            tr.wait(e, E, ("xl8", 0), ("xl8", 1),
                                    ("xl8", 2))
                        else:
                            tr.wait(e, E, ("sqs", 0, 0))
                        tr.inc(e.activation(out=xsq[:, s_, :],
                                            in_=xb[:, s_, :],
                                            func=AF.Square),
                               "s", key=("xsqs", 0, s_))
                    rstd_block(0)
                for g in range(4):
                    s_, half = g // 2, g % 2
                    tr.wait(e, E, ("tT", t, g))
                    tr.wait_opt(e, E, ("h", t - 1, 31), ("xnTst", t - 1))
                    dst = xnT1[:, half * 4:(half + 1) * 4,
                               s_ * 128:(s_ + 1) * 128]
                    tr.inc(e.activation(
                        out=dst,
                        in_=pb[g % 2][:, :].rearrange("p (q n) -> p q n",
                                                      q=4),
                        func=AF.Identity),
                        "s", key=("tTe", t, g))
                tr.wait(e, E, *[("tTe", t, g) for g in range(4)])
                Qst.issue(e, E, ("xnTst", t), lambda t=t: e.dma_start(
                    out=xnt_scr[t // 2, :, :,
                                (t % 2) * 256:(t % 2) * 256 + 256],
                    in_=xnT1))
                for oc in range(8):
                    tr.wait(e, E, ("kmm", t, oc))
                    tr.wait_opt(e, E, ("kst3", t, oc - 2))
                    tr.inc(e.activation(out=ex1[:, oc % 2, :],
                                        in_=pb[oc % 4][:, 0:256],
                                        func=AF.Exp),
                           "s", key=("kex", t, oc))
                    if t + 1 < NT1 and oc in (0, 2):
                        s_ = oc // 2
                        tr.wait(e, E, ("xld", t + 1))
                        tr.wait_opt(e, E, ("sqs", t, s_))
                        tr.inc(e.activation(out=xsq[:, s_, :],
                                            in_=xF[:, (t + 1) % 2, s_, :],
                                            func=AF.Square),
                               "s", key=("xsqs", t + 1, s_))
                if t > 0 and PHASE >= 2:
                    tr.wait(e, E, *[("p1b", t - 1, s_, oh)
                                    for s_ in range(2) for oh in range(2)])
                    Qst.issue(e, E, ("p1st", t - 1),
                              lambda t=t: e.dma_start(
                        out=p1_scr[(t - 1) // 2, :,
                                   ((t - 1) % 2) * 2:((t - 1) % 2) * 2 + 2,
                                   :],
                        in_=p1t))
                if t + 1 < NT1:
                    rstd_block(t + 1)
                tr.wait(e, E, *[("kst3", t, oc) for oc in range(8)])
                Qst.issue(e, E, ("kst", t), lambda t=t: e.dma_start(
                    out=k_scr[t // 2, :, :,
                              (t % 2) * 256:(t % 2) * 256 + 256],
                    in_=kbf))
                for cc in range(32 if PHASE >= 2 else 0):
                    tr.wait(e, E, ("h", t, cc), "c_b1")
                    tr.wait_opt(e, E, ("fgrp", t, cc - 2))
                    tr.inc(e.activation(out=hc[:, cc % 2, :],
                                        in_=pb[2 + cc % 2][:, 0:256],
                                        func=AF.Gelu, bias=b1c[:, cc:cc + 1]),
                           "s", key=("ge", t, cc))

                if PHASE >= 2 and t == NT1 - 1:
                    tr.wait(e, E, *[("p1b", t, s, oh)
                                    for s in range(2) for oh in range(2)])
                    Qst.issue(e, E, ("p1st", t), lambda t=t: e.dma_start(
                        out=p1_scr[t // 2, :,
                                   (t % 2) * 2:(t % 2) * 2 + 2, :],
                        in_=p1t))
            def ost_block(u):
                tr.wait(e, E, *[("oev", u, g) for g in range(8)])
                Qst.issue(e, E, ("ost", u), lambda u=u: e.dma_start(
                    out=out_d[u * 512:(u + 1) * 512, :].rearrange(
                        "(s p) d -> p s d", p=128),
                    in_=p1in))

            for u in range(NT2 if PHASE >= 4 else 0):
                for oc in range(8):
                    tr.wait(e, E, ("qmm", u, oc))
                    if u == 0 and oc < 2:
                        tr.wait(e, E, ("kst", NT1 - 1))
                    tr.wait_opt(e, E, ("qst3", u, oc - 2))
                    tr.inc(e.activation(out=ex2[:, oc % 2, :],
                                        in_=pb[oc % 2][:, :], func=AF.Exp),
                           "s", key=("qex", u, oc))
                if u > 0:
                    ost_block(u - 1)
                for oc in range(8):
                    tr.wait(e, E, ("vmm", u, oc))
                    if u == 0:
                        tr.wait(e, E, ("fgrp", NT1 - 1, 15))
                    tr.wait_opt(e, E, ("smp", u - 1, oc), ("t1", u - 1, oc))
                    tr.inc(e.activation(out=vbf[:, oc, :],
                                        in_=pb[2 + oc % 2][:, :],
                                        func=AF.Identity),
                           "s", key=("vev", u, oc))
                for oc in range(8):
                    tr.wait(e, E, ("gmm", u, oc))
                    if u == 0:
                        tr.wait(e, E, ("h", NT1 - 1, 31))
                    else:
                        tr.wait_opt(e, E, ("zn", u - 1, oc))
                    tr.inc(e.activation(out=sgbf[:, oc, :],
                                        in_=pb[oc % 2][:, :], func=AF.Silu),
                           "s", key=("sgev", u, oc))
                for fc in range(8):
                    tr.wait(e, E, ("zev", u, fc))
                    tr.wait_opt(e, E, ("qkp", u, 7), ("sqp", u, fc - 2))
                    tr.inc(e.activation(out=prodp[:, fc % 2, :],
                                        in_=vbf[:, fc, :], func=AF.Square),
                           "s", key=("zsq", u, fc))
                tr.wait(e, E, ("rec2", u), ("var", u))
                tr.inc(e.activation(out=stat2[0:1, 512:1024],
                                    in_=stat2[0:1, 1536:2048], func=AF.Sqrt),
                       "s", key=("rstds", u))
                for fc in range(8):
                    tr.wait(e, E, ("t2", u, fc), "gbcol")
                    tr.wait_opt(e, E, ("zn", u, fc - 2))
                    tr.inc(e.activation(out=t1p[:, fc % 2, :],
                                        in_=t1p[:, fc % 2, :],
                                        func=AF.Identity,
                                        bias=gbcol[:, 8 + fc:9 + fc],
                                        scale=gbcol[:, fc:fc + 1]),
                           "s", key=("t3", u, fc))
                if u == NT2 - 1:
                    ost_block(u)

        def body_tensor(e):
            E = "t"
            for oh in range(2 if PROLOG >= 1 else 0):
                tr.wait(e, E, "b2bf", "c_or")
                tr.minc(e.matmul(pb[oh][:, :], onesr[0:1, :],
                                b2bf[0:1, oh * 512:(oh + 1) * 512],
                                start=True, stop=True),
                       "t", key=("b2m", oh))
            def w1fold():
              for dc in range(8 if PROLOG >= 2 else 0):
                for hs in range(8):
                    i, rsl = hs // 2, hs % 2
                    tr.wait(e, E, ("xl8", 0), ("xl8", 1), ("xl8", 2))
                    if hs >= 4:
                        tr.wait(e, E, ("w1e", dc, hs - 4))
                    elif dc > 0:
                        tr.wait(e, E, ("w1e", dc - 1, hs + 4))
                    tr.minc(e.matmul(
                        pb[4 + hs % 4][:, :],
                        xl8(2 * i + 1, dc * 128, (dc + 1) * 128),
                        xl8(2 * i, rsl * 512, (rsl + 1) * 512),
                        start=True, stop=True),
                        "t", key=("w1m", dc, hs))
            def w2fold():
              for cc in range(32 if PROLOG >= 3 else 0):
                j, c0 = cc // 8, (cc % 8) * 128
                pair = (4, 5) if cc % 2 == 0 else (6, 7)
                for i in range(4):
                    bank = pair[i // 2]
                    off = (i % 2) * 256
                    tr.wait(e, E, ("xl8", 0), ("xl8", 1), ("xl8", 2))
                    if cc < 2:
                        tr.wait(e, E, ("w1e", 7, bank - 4), ("w1e", 7, bank))
                    else:
                        tr.wait(e, E, ("w2e", cc - 2, i // 2))
                    rhs = xl8(2 * i + 8, j * 256, (j + 1) * 256)
                    tr.minc(e.matmul(
                        pb[bank][:, off:off + 256],
                        xl8(2 * i + 9, c0, c0 + 128), rhs,
                        start=True, stop=True),
                        "t", key=("w2m", cc, i // 2) if i % 2 == 1 else None)
            if PHASE < 1:
                w1fold()
                w2fold()
            for t in range(NT1 if PHASE >= 1 else 0):
                for g in range(4):
                    s, half = g // 2, g % 2
                    tr.wait(e, E, ("xn", t, s), "c_id")
                    if g >= 2:
                        tr.wait(e, E, ("tTe", t, g - 2))
                    elif t > 0:
                        tr.wait(e, E, ("krl", t - 1, 4 + g),
                                ("kex", t - 1, 4 + g))
                    else:
                        tr.wait(e, E, ("b2e", g))
                    for q in range(4):
                        fc = half * 4 + q
                        tr.minc(e.matmul(
                            pb[g % 2][:, q * 128:(q + 1) * 128],
                            xn_tm[:, s, fc * 128:(fc + 1) * 128],
                            ident[:, :], start=True, stop=True),
                            "t", key=("tT", t, g) if q == 3 else None)
                for oc in range(8):
                    bank = pb[oc % 4]
                    tr.wait(e, E, ("tTe", t, 3), "wl_wk")
                    if oc < 2:
                        tr.wait(e, E, ("tTe", t, 2 + oc))
                    elif oc < 4:
                        tr.wait_opt(e, E, ("ge", t - 1, 28 + oc)
                                    if PHASE >= 2 else ("krl", t - 1, 4 + oc))
                    else:
                        tr.wait(e, E, ("krl", t, oc - 4), ("kex", t, oc - 4))
                    for dc in range(8):
                        tr.minc(e.matmul(
                            bank[:, 0:256], wk_s(dc, oc), xnT1[:, dc, :],
                            start=(dc == 0), stop=(dc == 7)),
                            "t", key=("kmm", t, oc) if dc == 7 else None)
                if t == 0:
                    w1fold()
                    w2fold()
                for cc in range(33 if PHASE >= 2 else 0):
                    if cc < 32:
                        tr.wait(e, E, ("w1e", 7, 7))
                        if cc < 2:
                            tr.wait(e, E, ("krl", t, 6 + cc),
                                    ("kex", t, 6 + cc))
                        else:
                            tr.wait(e, E, ("ge", t, cc - 2))
                        for dc in range(8):
                            tr.minc(e.matmul(
                                pb[2 + cc % 2][:, 0:256],
                                w1eff(dc, cc * 128, (cc + 1) * 128),
                                xnT1[:, dc, :],
                                start=(dc == 0), stop=(dc == 7)),
                                "t", key=("h", t, cc) if dc == 7 else None)
                    if cc >= 1:
                        ccw = cc - 1
                        tr.wait(e, E, ("ge", t, ccw), ("w2e", 31, 1))
                        if ccw == 0:
                            if t == 0:
                                tr.wait(e, E, ("w2e", 30, 0), ("w2e", 30, 1),
                                        ("w2e", 31, 0), ("w2e", 31, 1))
                            else:
                                tr.wait(e, E, *[("p1a", t - 1, s, oh)
                                                for s in range(2)
                                                for oh in range(2)])
                        nmm = 0
                        for s in range(2):
                            for oh in range(2):
                                nmm += 1
                                tr.minc(e.matmul(
                                    pb[4 + s * 2 + oh][:, :],
                                    hc[:, ccw % 2, s * 128:(s + 1) * 128],
                                    w2eff(ccw, oh * 512, (oh + 1) * 512),
                                    start=(ccw == 0), stop=(ccw == 31)),
                                    "t", key=("fgrp", t, ccw)
                                    if nmm == 4 else None)
            def s2_q(u, lo=0, hi=8):
                for oc in range(lo, hi):
                    tr.wait(e, E, ("xnld", u), "wl_wq")
                    if oc < 2:
                        if u == 0:
                            tr.wait(e, E, ("tTe", NT1 - 1, 2 + oc))
                        else:
                            tr.wait(e, E, ("sgev", u - 1, 6 + oc))
                    else:
                        tr.wait(e, E, ("qrl", u, oc - 2), ("qex", u, oc - 2))
                    for dc in range(8):
                        tr.minc(e.matmul(
                            pb[oc % 2][:, :], wq_s(dc, oc), xnT2v[:, dc, :],
                            start=(dc == 0), stop=(dc == 7)),
                            "t", key=("qmm", u, oc) if dc == 7 else None)

            def s2_v(u):
                for oc in range(8):
                    tr.wait(e, E, "wl_wv")
                    if oc < 2:
                        if u == 0:
                            tr.wait(e, E, ("ge", NT1 - 1, 30 + oc))
                        else:
                            tr.wait(e, E, ("oev", u - 1, 6 + oc))
                    else:
                        tr.wait(e, E, ("vev", u, oc - 2))
                    for dc in range(8):
                        tr.minc(e.matmul(
                            pb[2 + oc % 2][:, :], wv_s(dc, oc),
                            xnT2v[:, dc, :],
                            start=(dc == 0), stop=(dc == 7)),
                            "t", key=("vmm", u, oc) if dc == 7 else None)

            def s2_g(u):
                for oc in range(8):
                    tr.wait(e, E, "wl_wg")
                    if oc < 2:
                        tr.wait(e, E, ("qrl", u, 6 + oc), ("qex", u, 6 + oc))
                    else:
                        tr.wait(e, E, ("sgev", u, oc - 2))
                    for dc in range(8):
                        tr.minc(e.matmul(
                            pb[oc % 2][:, :], wg_s(dc, oc), xnT2v[:, dc, :],
                            start=(dc == 0), stop=(dc == 7)),
                            "t", key=("gmm", u, oc) if dc == 7 else None)

            for u in range(NT2 if PHASE >= 4 else 0):
                if u == 0:
                    s2_q(0)
                    s2_v(0)
                    s2_g(0)
                for fc in range(8):
                    tr.wait(e, E, ("prod", u, fc), "c_sa")
                    if fc == 0:
                        if u == 0:
                            tr.wait(e, E, ("p1a", NT1 - 1, 0, 0))
                        else:
                            tr.wait(e, E, ("rmul", u - 1, 1))
                    tr.inc(e.matmul(
                        pb[4][0:16, :], selA[:, fc * 16:(fc + 1) * 16],
                        prodp[:, fc % 2, :],
                        start=(fc == 0), stop=(fc == 7)),
                        "t", key=("qkp", u, fc))
                    tr.wait(e, E, ("qst3", u, fc))
                    if fc == 0:
                        if u == 0:
                            tr.wait(e, E, ("p1a", NT1 - 1, 0, 1), ("sak", 7))
                        else:
                            tr.wait(e, E, ("rrec1", u - 1, 1))
                    tr.inc(e.matmul(
                        pb[5][0:16, :], selAk[:, fc * 16:(fc + 1) * 16],
                        qbf[:, fc, :],
                        start=(fc == 0), stop=(fc == 7)),
                        "t", key=("nmp", u, fc))
                for fc in range(8):
                    for hh in range(2):
                        cs = slice(hh * 256, (hh + 1) * 256)
                        tr.wait(e, E, ("rmul", u, hh), "c_sb")
                        if hh == 0:
                            if fc < 2:
                                tr.wait(e, E, ("vev", u, 6 + fc))
                            else:
                                tr.wait(e, E, ("zev", u, fc - 2))
                        tr.minc(e.matmul(
                            pb[2 + fc % 2][:, cs],
                            selB[:, fc * 128:(fc + 1) * 128], rqk[:, cs],
                            start=True, stop=True),
                            "t", key=("rbc", u, fc, hh)
                            if hh == 1 else None)
                for fc in range(8):
                    tr.wait(e, E, ("zev", u, fc), "c_o1")
                    if fc == 0:
                        if u == 0:
                            tr.wait(e, E, ("p1a", NT1 - 1, 1, 0))
                        else:
                            tr.wait(e, E, ("t1", u - 1, 7))
                    tr.inc(e.matmul(
                        pb[6][0:1, :], ones1[:, :], vbf[:, fc, :],
                        start=(fc == 0), stop=(fc == 7)),
                        "t", key=("smp", u, fc))
                    tr.wait(e, E, ("zsq", u, fc))
                    if fc == 0:
                        if u == 0:
                            tr.wait(e, E, ("p1a", NT1 - 1, 1, 1))
                        else:
                            tr.wait(e, E, ("t2", u - 1, 7))
                    tr.inc(e.matmul(
                        pb[7][0:1, :], ones1[:, :], prodp[:, fc % 2, :],
                        start=(fc == 0), stop=(fc == 7)),
                        "t", key=("sqp", u, fc))
                if u + 1 < NT2:
                    s2_q(u + 1, 0, 2)
                tr.wait(e, E, ("rstdbf", u), ("mu", u), "c_or")
                tr.inc(e.matmul(pb[6][:, :], onesr[0:1, :], rstdb[0:1, :],
                                start=True, stop=True),
                       "t", key=("b1m", u))
                tr.wait(e, E, ("mwr0", u), ("var", u))
                tr.inc(e.matmul(pb[7][:, :], onesr[0:1, :], mwr[0:1, :],
                                start=True, stop=True),
                       "t", key=("b2m2", u))
                for g in range(8):
                    s, oh = g // 2, g % 2
                    if g == 0:
                        tr.wait(e, E, "wl_wo")
                    if g < 2:
                        tr.wait(e, E, ("zev", u, 6 + g))
                    else:
                        tr.wait(e, E, ("oev", u, g - 2))
                    for fc in range(8):
                        if g == 0:
                            tr.wait(e, E, ("zn", u, fc))
                        tr.minc(e.matmul(
                            pb[2 + g % 2][:, :],
                            znb[:, fc, s * 128:(s + 1) * 128],
                            wo_s(fc, oh * 512, (oh + 1) * 512),
                            start=(fc == 0), stop=(fc == 7)),
                            "t", key=("womm", u, g) if fc == 7 else None)
                if u + 1 < NT2:
                    s2_q(u + 1, 2, 8)
                    s2_v(u + 1)
                    s2_g(u + 1)

        bodies = [("sync", body_sync), ("g", body_gpsimd),
                  ("v", body_vector), ("s", body_scalar),
                  ("t", body_tensor)]

        fake = _FakeEng()
        for _, b in bodies:
            b(fake)
        tr.begin_emit()
        block = ctx.enter_context(nc.Block())
        block.sync(body_sync)
        block.gpsimd(body_gpsimd)
        block.vector(body_vector)
        block.scalar(body_scalar)
        block.tensor(body_tensor)
        tr.check()
    return nc


def _rearr_w(w):
    M = w.shape[1]
    return np.ascontiguousarray(
        w.reshape(8, 128, M).transpose(1, 0, 2).reshape(128, 8 * M))


def _rearr_w32(w):
    M = w.shape[1]
    return np.ascontiguousarray(
        w.reshape(32, 128, M).transpose(1, 0, 2).reshape(128, 32 * M))


_E4PERM = (np.arange(1024).reshape(256, 4).T.reshape(-1))


def xl16_host(lt):
    rows = []
    for lti in [0, 1, 6, 7, 2, 3, 8, 9, 10, 11, 4, 5, 12, 13, 14, 15]:
        r = lt[lti]
        if lti >= 8 and lti % 2 == 0:
            r = r[_E4PERM]
        rows.append(r)
    return np.ascontiguousarray(np.stack(rows))


def _consts():
    ident = np.eye(128, dtype=ml_dtypes.bfloat16)
    selA = np.zeros((128, 8 * 16), dtype=ml_dtypes.bfloat16)
    selB = np.zeros((32, 8 * 128), dtype=ml_dtypes.bfloat16)
    for fc in range(8):
        for p in range(128):
            h = 2 * fc + p // 64
            selA[p, fc * 16 + h] = 1
            selB[h, fc * 128 + p] = 1
    ones1 = np.ones((128, 1), dtype=ml_dtypes.bfloat16)
    onesr = np.ones((1, 128), dtype=ml_dtypes.bfloat16)
    return ident, selA, selB, ones1, onesr


def prepare(L0_attn_hidden_states, L0_attn_Wq, L0_attn_Wk, L0_attn_Wv,
            L0_attn_Wg, L0_attn_Wo, L0_attn_gamma, L0_attn_beta,
            L0_ffn_W1, L0_ffn_b1, L0_ffn_W2, L0_ffn_b2, L0_ffn_scale):
    x = np.asarray(L0_attn_hidden_states)
    B, S, _ = x.shape
    t_core = B * S // NCORES
    cores_per_b = NCORES // B

    wq = _rearr_w(np.asarray(L0_attn_Wq))
    wk = _rearr_w(np.asarray(L0_attn_Wk))
    wv = _rearr_w(np.asarray(L0_attn_Wv))
    wg = _rearr_w(np.asarray(L0_attn_Wg))
    wo = _rearr_w(np.asarray(L0_attn_Wo))
    w1s = _rearr_w(np.ascontiguousarray(np.asarray(L0_ffn_W1).T))
    w2s = _rearr_w32(np.ascontiguousarray(np.asarray(L0_ffn_W2).T))
    b1c = np.ascontiguousarray(np.asarray(L0_ffn_b1).reshape(32, 128).T)
    b2r = np.ascontiguousarray(np.asarray(L0_ffn_b2).reshape(1, D))
    gbcol = np.ascontiguousarray(np.concatenate(
        [np.asarray(L0_attn_gamma).reshape(8, 128).T,
         np.asarray(L0_attn_beta).reshape(8, 128).T], axis=1))
    scl = np.ascontiguousarray(
        np.broadcast_to(np.asarray(L0_ffn_scale).reshape(1, 1), (128, 1))
    ).astype(np.float32)
    ident, selA, selB, ones1, onesr = _consts()

    nc = build_program(t_core)
    in_maps = []
    for c in range(NCORES):
        b = c // cores_per_b
        q = c % cores_per_b
        mkb = np.zeros((128, 2), dtype=np.float32)
        mkb[:, b] = 1.0
        in_maps.append(dict(
            x_in=np.ascontiguousarray(x[b, q * t_core:(q + 1) * t_core]),
            xl16=xl16_host(x[b, 1:17]),
            wq=wq, wk=wk, wv=wv, wg=wg, wo=wo, w1s=w1s, w2s=w2s,
            b1c=b1c, b2r=b2r, gbcol=gbcol, scl=scl, mkb=mkb,
            idn=ident, selA=selA, selB=selB, ones128=ones1, onesrow=onesr,
        ))
    return nc, in_maps


def assemble(results, B, S):
    t_core = B * S // NCORES
    cores_per_b = NCORES // B
    out = np.empty((B, S, D), dtype=np.float32)
    for c in range(NCORES):
        b = c // cores_per_b
        q = c % cores_per_b
        out[b, q * t_core:(q + 1) * t_core] = results[c]["out"]
    return out


def kernel(L0_attn_hidden_states, **kw):
    x = np.asarray(L0_attn_hidden_states)
    nc, in_maps = prepare(L0_attn_hidden_states, **kw)
    res = run_bass_kernel_spmd(nc, in_maps, core_ids=list(range(NCORES)))
    return assemble(res.results, x.shape[0], x.shape[1])


# version 18
# speedup vs baseline: 1.0591x; 1.0459x over previous
import os
import numpy as np
import ml_dtypes
import concourse.bass as bass
import concourse.mybir as mybir
from concourse.bass_utils import run_bass_kernel_spmd
from contextlib import ExitStack

F32 = mybir.dt.float32
BF16 = mybir.dt.bfloat16
AF = mybir.ActivationFunctionType
ALU = mybir.AluOpType
AX = mybir.AxisListType

D = 1024
HF = 4096
NCORES = 8
PHASE = int(os.environ.get("KPHASE", "4"))
PROLOG = int(os.environ.get("KPROLOG", "3"))


class _FakeInstr:
    def then_inc(self, sem, n=1):
        return self


class _FakeEng:
    def __getattr__(self, name):
        return lambda *a, **k: _FakeInstr()


class _Tr:

    def __init__(self, sems):
        self.sems = sems
        self.plan = True
        self.count = {k: 0 for k in sems}
        self.evt = {}
        self.waited = {}

    def begin_emit(self):
        assert self.plan
        self._plan_count = dict(self.count)
        self.plan = False
        self.count = {k: 0 for k in self.sems}
        self.waited = {}
        for q in getattr(self, "queues", []):
            q.cnt = 0

    def wait_val(self, eng, ename, sem, val):
        if self.plan or val <= 0:
            return
        w = self.waited.setdefault(ename, {})
        if w.get(sem, 0) >= val:
            return
        eng.wait_ge(self.sems[sem], val)
        w[sem] = val

    def check(self):
        assert self.count == self._plan_count, (self.count, self._plan_count)

    def inc(self, instr, sem, n=1, key=None):
        self.count[sem] += n
        tok = (sem, self.count[sem])
        if self.plan:
            if key is not None:
                assert key not in self.evt, key
                self.evt[key] = tok
        else:
            instr.then_inc(self.sems[sem], n)
            if key is not None:
                assert self.evt[key] == tok, (key, self.evt[key], tok)
        return tok

    def minc(self, instr, sem, key=None):
        if key is None:
            return None
        return self.inc(instr, sem, 1, key=key)

    def wait(self, eng, ename, *keys):
        if self.plan:
            return
        w = self.waited.setdefault(ename, {})
        for key in keys:
            if key is None:
                continue
            sem, val = self.evt[key]
            if val <= 0 or w.get(sem, 0) >= val:
                continue
            eng.wait_ge(self.sems[sem], val)
            w[sem] = val

    def wait_opt(self, eng, ename, *keys):
        if self.plan:
            return
        self.wait(eng, ename, *[k for k in keys if k in self.evt])


class _DmaQ:

    def __init__(self, tr, base, n):
        self.tr, self.n = tr, n
        self.names = [f"{base}{i}" for i in range(n)]
        self.cnt = 0
        tr.queues = getattr(tr, "queues", []) + [self]

    def issue(self, eng, ename, key, fn):
        i = self.cnt
        self.cnt += 1
        nm = self.names[i % self.n]
        self.tr.wait_val(eng, ename, nm, 16 * (i // self.n))
        self.tr.inc(fn(), nm, 16, key=key)


def build_program(t_core):
    NT1 = t_core // 256
    NT2 = t_core // 512
    assert NT1 * 256 == t_core and NT2 * 512 == t_core

    nc = bass.Bass("TRN2", target_bir_lowering=False)

    x_in = nc.dram_tensor("x_in", [t_core, D], F32, kind="ExternalInput").ap()
    xl16_d = nc.dram_tensor("xl16", [16, D], F32, kind="ExternalInput").ap()
    wq_d = nc.dram_tensor("wq", [128, 8 * D], F32, kind="ExternalInput").ap()
    wk_d = nc.dram_tensor("wk", [128, 8 * D], F32, kind="ExternalInput").ap()
    wv_d = nc.dram_tensor("wv", [128, 8 * D], F32, kind="ExternalInput").ap()
    wg_d = nc.dram_tensor("wg", [128, 8 * D], F32, kind="ExternalInput").ap()
    wo_d = nc.dram_tensor("wo", [128, 8 * D], F32, kind="ExternalInput").ap()
    w1_d = nc.dram_tensor("w1s", [128, 8 * HF], F32, kind="ExternalInput").ap()
    w2_d = nc.dram_tensor("w2s", [128, 32 * D], F32, kind="ExternalInput").ap()
    b1_d = nc.dram_tensor("b1c", [128, 32], F32, kind="ExternalInput").ap()
    b2_d = nc.dram_tensor("b2r", [1, D], F32, kind="ExternalInput").ap()
    gb_d = nc.dram_tensor("gbcol", [128, 16], F32, kind="ExternalInput").ap()
    sc_d = nc.dram_tensor("scl", [128, 1], F32, kind="ExternalInput").ap()
    mb_d = nc.dram_tensor("mkb", [128, 2], F32, kind="ExternalInput").ap()
    id_d = nc.dram_tensor("idn", [128, 128], BF16, kind="ExternalInput").ap()
    sa_d = nc.dram_tensor("selA", [128, 8 * 16], BF16, kind="ExternalInput").ap()
    sb_d = nc.dram_tensor("selB", [32, 8 * 128], BF16, kind="ExternalInput").ap()
    o1_d = nc.dram_tensor("ones128", [128, 1], BF16, kind="ExternalInput").ap()
    or_d = nc.dram_tensor("onesrow", [1, 128], BF16, kind="ExternalInput").ap()

    out_d = nc.dram_tensor("out", [t_core, D], F32, kind="ExternalOutput").ap()

    xnt_scr = nc.dram_tensor("xnt_scr", [NT2, 128, 8, 512], BF16).ap()
    k_scr = nc.dram_tensor("k_scr", [NT2, 128, 8, 512], BF16).ap()
    p1_scr = nc.dram_tensor("p1_scr", [NT2, 128, 4, D], F32).ap()
    cc_in = nc.dram_tensor("cc_in", [2, 128, 8], F32).ap()
    cc_out = nc.dram_tensor("cc_out", [2, 128, 8], F32).ap()

    ctx = ExitStack()
    with ctx:
        def sbuf(name, shape, dtype):
            return ctx.enter_context(nc.sbuf_tensor(name, shape, dtype))

        wA = sbuf("wA", [128, 73728], BF16)

        def w1eff(dc, lo, hi):
            return wA[:, dc * HF + lo: dc * HF + hi]

        def w2eff(cc, lo, hi):
            return wA[:, 32768 + cc * D + lo: 32768 + cc * D + hi]

        def wk_s(dc, oc):
            b = 65536 + dc * D + oc * 128
            return wA[:, b: b + 128]

        def wq_s(dc, oc):
            b = 65536 + dc * D + oc * 128
            return wA[:, b: b + 128]

        def wv_s(dc, oc):
            b = 49152 + dc * D + oc * 128
            return wA[:, b: b + 128]

        def wg_s(dc, oc):
            b = 57344 + dc * D + oc * 128
            return wA[:, b: b + 128]

        def wo_s(fc, lo, hi):
            return wA[:, fc * D + lo: fc * D + hi]

        xnT2v = wA[:, 32768:36864].rearrange("p (c n) -> p c n", c=8)
        k2v = wA[:, 36864:40960].rearrange("p (c n) -> p c n", c=8)
        qbf = wA[:, 40960:45056].rearrange("p (c n) -> p c n", c=8)
        vbf = wA[:, 45056:49152].rearrange("p (c n) -> p c n", c=8)
        sgbf = wA[:, 8192:12288].rearrange("p (c n) -> p c n", c=8)
        znb = wA[:, 12288:16384].rearrange("p (c n) -> p c n", c=8)

        aB = sbuf("aB", [128, 9728], BF16)
        xn_tm = aB[:, 0:2048].rearrange("p (s d) -> p s d", s=2)
        xnT1 = aB[:, 2048:4096].rearrange("p (c n) -> p c n", c=8)
        hc = aB[:, 4096:4608].rearrange("p (b n) -> p b n", b=2)
        kbf = aB[:, 4608:6656].rearrange("p (c n) -> p c n", c=8)
        mn1 = aB[:, 6656:7168].rearrange("p (b n) -> p b n", b=2)
        ex1 = aB[:, 9216:9728].rearrange("p (b n) -> p b n", b=2)
        xsq = aB[:, 7168:9216].rearrange("p (s d) -> p s d", s=2)
        prodp = aB[:, 0:1024].rearrange("p (b n) -> p b n", b=2)
        nprodp = aB[:, 1024:2048].rearrange("p (b n) -> p b n", b=2)
        t1p = aB[:, 2048:3072].rearrange("p (b n) -> p b n", b=2)
        mn2 = aB[:, 3072:4096].rearrange("p (b n) -> p b n", b=2)
        ex2 = aB[:, 5632:6656].rearrange("p (b n) -> p b n", b=2)
        xsq16 = aB[0:16, 7168:8192]

        aF = sbuf("aF", [128, 6656], F32)
        xF = aF[:, 0:4096].rearrange("p (b s d) -> p b s d", b=2, s=2)
        p1t = aF[:, 4096:6144].rearrange("p (s d) -> p s d", s=2)

        st1 = aF[:, 6144:6160]
        p1in = aF[:, 0:4096].rearrange("p (s d) -> p s d", s=4)
        recf = aF[0:16, 4096:4608]
        stat2 = aF[0:1, 4608:6656]
        xl16 = aF[0:16, 4096:5120]

        st16 = sbuf("st16_s", [16, 8], F32)
        xl8b = sbuf("xl8b_s", [128, 6144], BF16)
        b2bf = aB[0:1, 0:1024]
        rqk = aB[0:32, 4096:4608]
        mwr = aB[0:2, 4608:5120]
        rstdb = aB[0:1, 5120:5632]
        ident = sbuf("ident_s", [128, 128], BF16)
        selA = sbuf("selA_s", [128, 8 * 16], BF16)
        selB = sbuf("selB_s", [32, 8 * 128], BF16)
        ones1 = sbuf("ones1_s", [128, 1], BF16)
        onesr = sbuf("onesr_s", [1, 128], BF16)
        b1c = sbuf("b1c_s", [128, 32], F32)
        b2rep = sbuf("b2rep_s", [128, D], BF16)
        gbcol = sbuf("gbcol_s", [128, 16], F32)
        scl = sbuf("scl_s", [128, 1], F32)
        mkb = sbuf("mkb_s", [128, 2], F32)
        ksum_r = sbuf("ksum_r_s", [128, 8], F32)
        kredb = sbuf("kredb_s", [128, 8], F32)
        ccin_s = sbuf("ccin_s_s", [128, 16], F32)
        ccout_s = sbuf("ccout_s_s", [128, 16], F32)
        ksum_g = sbuf("ksum_g_s", [128, 8], F32)
        selAk = sbuf("selAk_s", [128, 8 * 16], BF16)

        pb = [ctx.enter_context(nc.psum_tensor(f"pb{i}", [128, 512], F32))
              for i in range(8)]

        sems = {}
        for s_ in (["cc", "v", "s", "t", "g"]
                   + [f"ld{i}" for i in range(4)]
                   + [f"st{i}" for i in range(4)]
                   + [f"wl{i}" for i in range(2)]):
            sems[s_] = ctx.enter_context(nc.semaphore("sem_" + s_))
        tr = _Tr(sems)
        Qld = _DmaQ(tr, "ld", 4)
        Qst = _DmaQ(tr, "st", 4)
        Qwl = _DmaQ(tr, "wl", 2)

        _PAIRMAP = {0: (0, 0), 1: (32, 0), 2: (64, 0), 3: (0, 1),
                    4: (32, 1), 5: (32, 2), 6: (64, 1), 7: (64, 2)}

        def xl8(j, lo, hi):
            pair = (j // 2) if j < 8 else (4 + (j - 8) // 2)
            base, blk = _PAIRMAP[pair]
            c0 = blk * 2048 + (j % 2) * 1024
            return xl8b[base:base + 1, c0 + lo: c0 + hi]

        RE = lambda i: 1.0 / i

        def body_sync(e):
            E = "sync"
            Qld.issue(e, E, "ld_xl",
                      lambda: e.dma_start(out=xl16, in_=xl16_d))
            for t in range(min(2, NT1) if PHASE >= 1 else 0):
                Qld.issue(e, E, ("xld", t), lambda t=t: e.dma_start(
                    out=xF[:, t % 2, :, :],
                    in_=x_in[t * 256:(t + 1) * 256, :].rearrange(
                        "(s p) d -> p s d", p=128)))
            for key_, ap_d, ap_s in [
                    ("c_id", id_d, ident[:, :]), ("c_sa", sa_d, selA[:, :]),
                    ("c_sb", sb_d, selB[:, :]), ("c_o1", o1_d, ones1[:, :]),
                    ("c_or", or_d, onesr[:, :]), ("c_b1", b1_d, b1c[:, :]),
                    ("c_sc", sc_d, scl[:, :]), ("c_mb", mb_d, mkb[:, :])]:
                Qld.issue(e, E, key_,
                          lambda a=ap_s, b=ap_d: e.dma_start(out=a, in_=b))
            tr.wait(e, E, "xnl16")
            Qld.issue(e, E, ("xl8", 0), lambda: e.dma_start(
                out=xl8b[0:1, 0:4096], in_=xsq16[0:4, :]))
            Qld.issue(e, E, ("xl8", 1), lambda: e.dma_start(
                out=xl8b[32:33, 0:6144], in_=xsq16[4:10, :]))
            Qld.issue(e, E, ("xl8", 2), lambda: e.dma_start(
                out=xl8b[64:65, 0:6144], in_=xsq16[10:16, :]))
            if PHASE < 1:
                if PROLOG >= 3:
                    tr.wait(e, E, ("w2e", 31, 1))
                elif PROLOG >= 2:
                    tr.wait(e, E, ("w1e", 7, 7))
                elif PROLOG >= 1:
                    tr.wait(e, E, ("b2e", 1))
                else:
                    tr.wait(e, E, "xnl16")
                return
            for t in range(2, NT1):
                if PHASE >= 2:
                    tr.wait_opt(e, E, ("p1x", t - 2, 1, 1))
                else:
                    tr.wait_opt(e, E, ("xn", t - 2, 1))
                Qld.issue(e, E, ("xld", t), lambda t=t: e.dma_start(
                    out=xF[:, t % 2, :, :],
                    in_=x_in[t * 256:(t + 1) * 256, :].rearrange(
                        "(s p) d -> p s d", p=128)))
            if PHASE < 3:
                tr.wait(e, E, ("xnTst", NT1 - 1), ("kst", NT1 - 1),
                        ("kred", NT1 - 1))
                if PHASE >= 2:
                    tr.wait(e, E, *[("p1st", t)
                                    for t in range(max(0, NT1 - 4), NT1)])
                    tr.wait(e, E, *[("xnTst", t)
                                    for t in range(max(0, NT1 - 4), NT1)])
                    tr.wait(e, E, *[("kst", t)
                                    for t in range(max(0, NT1 - 4), NT1)])
                return
            tr.wait(e, E, "ccin_v")
            Qld.issue(e, E, "ccin_d", lambda: e.dma_start(
                out=cc_in.rearrange("b p c -> p b c"),
                in_=ccin_s[:, :].rearrange("p (b c) -> p b c", b=2)))
            tr.wait(e, E, "cc_done")
            Qld.issue(e, E, "ccout_d", lambda: e.dma_start(
                out=ccout_s[:, :].rearrange("p (b c) -> p b c", b=2),
                in_=cc_out.rearrange("b p c -> p b c")))
            if PHASE < 4:
                tr.wait(e, E, "ccout_d")
                return
            for u in range(NT2):
                if u == 0:
                    tr.wait(e, E, ("fgrp", NT1 - 1, 3))
                else:
                    tr.wait(e, E, ("gmm", u - 1, 7))
                tr.wait(e, E, ("xnTst", 2 * u), ("xnTst", 2 * u + 1))
                Qld.issue(e, E, ("xnld", u), lambda u=u: e.dma_start(
                    out=xnT2v, in_=xnt_scr[u]))
                if u > 0:
                    tr.wait(e, E, ("prod", u - 1, 7))
                if u == 0:
                    tr.wait(e, E, ("fgrp", NT1 - 1, 7))
                tr.wait(e, E, ("kst", 2 * u), ("kst", 2 * u + 1))
                Qld.issue(e, E, ("kld", u), lambda u=u: e.dma_start(
                    out=k2v, in_=k_scr[u]))
                if u > 0:
                    tr.wait(e, E, ("ost", u - 1))
                else:
                    tr.wait(e, E, ("p1x", NT1 - 1, 1, 1))
                tr.wait(e, E, ("p1st", 2 * u), ("p1st", 2 * u + 1))
                Qld.issue(e, E, ("p1ld", u), lambda u=u: e.dma_start(
                    out=p1in, in_=p1_scr[u]))
            tr.wait(e, E, *[("ost", u)
                            for u in range(max(0, NT2 - 4), NT2)])

        def body_gpsimd(e):
            E = "g"
            Qwl.issue(e, E, "gbcol",
                      lambda: e.dma_start(out=gbcol[:, :], in_=gb_d))
            Qwl.issue(e, E, "b2bf",
                      lambda: e.dma_start(out=b2bf[:, :], in_=b2_d))
            tr.wait(e, E, "ld_xl")
            Qwl.issue(e, E, "wl_wk",
                      lambda: e.dma_start(out=wA[:, 65536:73728], in_=wk_d))
            if PHASE >= 1:
                tr.wait(e, E, ("xl8", 2), ("xld", 1))
            for dc in range(8):
                Qwl.issue(e, E, ("wl_w1", dc), lambda dc=dc: e.dma_start(
                    out=wA[:, dc * HF:(dc + 1) * HF],
                    in_=w1_d[:, dc * HF:(dc + 1) * HF]))
            for j in range(8):
                Qwl.issue(e, E, ("wl_w2", j), lambda j=j: e.dma_start(
                    out=wA[:, 32768 + j * HF: 32768 + (j + 1) * HF],
                    in_=w2_d[:, j * HF:(j + 1) * HF]))
            if PHASE < 3:
                return
            for t in range(NT1 if PHASE >= 1 else 0):
                pass
            if PHASE < 3:
                return
            tr.wait(e, E, "ccin_d")
            tr.inc(e.collective_compute(
                "AllReduce", ALU.add, replica_groups=[list(range(NCORES))],
                ins=[cc_in[:, :, :]], outs=[cc_out[:, :, :]]),
                "cc", 1, key="cc_done")
            if PHASE < 4:
                return
            tr.wait(e, E, ("kmm", NT1 - 1, 7))
            Qwl.issue(e, E, "wl_wq", lambda: e.dma_start(
                out=wA[:, 65536:73728], in_=wq_d))
            tr.wait(e, E, ("fgrp", NT1 - 1, 23))
            Qwl.issue(e, E, "wl_wv", lambda: e.dma_start(
                out=wA[:, 49152:57344], in_=wv_d))
            tr.wait(e, E, ("fgrp", NT1 - 1, 31))
            Qwl.issue(e, E, "wl_wg", lambda: e.dma_start(
                out=wA[:, 57344:65536], in_=wg_d))
            tr.wait(e, E, ("h", NT1 - 1, 31))
            Qwl.issue(e, E, "wl_wo", lambda: e.dma_start(
                out=wA[:, 0:8192], in_=wo_d))
            for u in range(NT2):

                for fc in range(8):
                    tr.wait(e, E, ("t3", u, fc), ("sgev", u, fc))
                    if u == 0 and fc < 2:
                        tr.wait(e, E, ("h", NT1 - 1, 31))
                    tr.wait_opt(e, E, ("womm", u - 1, 7))
                    tr.inc(e.tensor_tensor(
                        out=znb[:, fc, :], in0=t1p[:, fc % 2, :],
                        in1=sgbf[:, fc, :], op=ALU.mult),
                        "g", key=("zn", u, fc))

        def body_vector(e):
            E = "v"
            tr.wait(e, E, "ld_xl")
            tr.inc(e.tensor_tensor(out=xsq16, in0=xl16, in1=xl16,
                                   op=ALU.mult), "v", key=("pl", 0))
            tr.inc(e.reduce_sum(out=st16[:, 0:1], in_=xl16, axis=AX.X),
                   "v", key=("pl", 1))
            tr.wait(e, E, ("pl", 0))
            tr.inc(e.reduce_sum(out=st16[:, 1:2], in_=xsq16, axis=AX.X),
                   "v", key=("pl", 2))
            tr.wait(e, E, ("pl", 1), ("pl", 2))
            tr.inc(e.scalar_tensor_tensor(
                out=st16[:, 2:3], in0=st16[:, 0:1], scalar=RE(D),
                in1=st16[:, 0:1], op0=ALU.mult, op1=ALU.mult),
                "v", key=("pl", 3))
            tr.wait(e, E, ("pl", 3))
            tr.inc(e.tensor_sub(st16[:, 3:4], st16[:, 1:2], st16[:, 2:3]),
                   "v", key=("pl", 4))
            tr.wait(e, E, ("pl", 4))
            tr.inc(e.tensor_scalar_add(st16[:, 4:5], st16[:, 3:4], D * 1e-5),
                   "v", key=("pl", 5))
            tr.wait(e, E, ("pl", 5))
            tr.inc(e.reciprocal(st16[:, 5:6], st16[:, 4:5]),
                   "v", key=("pl", 6))
            tr.wait(e, E, "pl_sqrt")
            tr.inc(e.scalar_tensor_tensor(
                out=st16[:, 7:8], in0=st16[:, 0:1], scalar=-RE(D),
                in1=st16[:, 6:7], op0=ALU.mult, op1=ALU.mult),
                "v", key=("pl", 7))
            tr.wait(e, E, ("pl", 7))
            tr.inc(e.tensor_scalar(
                out=xsq16, in0=xl16, scalar1=st16[:, 6:7],
                scalar2=st16[:, 7:8], op0=ALU.mult, op1=ALU.add),
                "v", key="xnl16")
            tr.inc(e.memset(ksum_r[:, :], 0.0), "v", key="ksum0")
            for oh in range(2 if PROLOG >= 1 else 0):
                tr.wait(e, E, ("b2m", oh))
                tr.inc(e.tensor_copy(out=b2rep[:, oh * 512:(oh + 1) * 512],
                                     in_=pb[oh][:, :]),
                       "v", key=("b2e", oh))
            def v_folds():
                for dc in range(8 if PROLOG >= 2 else 0):
                    for hs in range(8):
                        tr.wait(e, E, ("w1m", dc, hs), ("wl_w1", dc), "c_sc")
                        sl_ = w1eff(dc, hs * 512, (hs + 1) * 512)
                        tr.inc(e.scalar_tensor_tensor(
                            out=sl_, in0=pb[4 + hs % 4][:, :],
                            scalar=scl[:, 0:1], in1=sl_,
                            op0=ALU.mult, op1=ALU.add),
                            "v", key=("w1e", dc, hs))
                for cc in range(32 if PROLOG >= 3 else 0):
                    pair = (4, 5) if cc % 2 == 0 else (6, 7)
                    for oh in range(2):
                        tr.wait(e, E, ("w2m", cc, oh), ("wl_w2", cc // 4),
                                "c_sc")
                        sl_ = w2eff(cc, oh * 512, (oh + 1) * 512)
                        tr.inc(e.scalar_tensor_tensor(
                            out=sl_, in0=pb[pair[oh]][:, :],
                            scalar=scl[:, 0:1], in1=sl_,
                            op0=ALU.mult, op1=ALU.add),
                            "v", key=("w2e", cc, oh))

            if PHASE < 1:
                v_folds()
            def ln_stats(t, xb):
                tr.wait(e, E, ("xld", t))
                tr.wait_opt(e, E, ("xn", t - 1, 1))
                tr.inc(e.reduce_sum(out=st1[:, 0:2], in_=xb, axis=AX.X),
                       "v", key=("mus", t))
                for s_ in range(2):
                    tr.wait(e, E, ("xsqs", t, s_))
                    tr.inc(e.reduce_sum(out=st1[:, 2 + s_:3 + s_],
                                        in_=xsq[:, s_, :],
                                        axis=AX.X), "v", key=("sqs", t, s_))
                tr.wait(e, E, ("mus", t), ("sqs", t, 1))
                tr.inc(e.scalar_tensor_tensor(
                    out=st1[:, 4:6], in0=st1[:, 0:2], scalar=RE(D),
                    in1=st1[:, 0:2], op0=ALU.mult, op1=ALU.mult),
                    "v", key=("lnb", t))
                tr.wait(e, E, ("lnb", t))
                tr.inc(e.tensor_sub(st1[:, 6:8], st1[:, 2:4], st1[:, 4:6]),
                       "v", key=("lnc", t))
                tr.wait(e, E, ("lnc", t))
                tr.inc(e.tensor_scalar_add(st1[:, 8:10], st1[:, 6:8],
                                           D * 1e-5), "v", key=("lne", t))
                tr.wait(e, E, ("lne", t))
                tr.inc(e.reciprocal(st1[:, 10:12], st1[:, 8:10]),
                       "v", key=("rec1", t))

            def ln_apply(t, xb):
                tr.wait(e, E, ("rstd1", t))
                tr.inc(e.scalar_tensor_tensor(
                    out=st1[:, 14:16], in0=st1[:, 0:2], scalar=-RE(D),
                    in1=st1[:, 12:14], op0=ALU.mult, op1=ALU.mult),
                    "v", key=("nmr1", t))
                for s in range(2):
                    tr.wait(e, E, ("nmr1", t))
                    tr.wait_opt(e, E, ("tT", t - 1, 2 * s + 1))
                    if t == 0:
                        tr.wait(e, E, "xnl16", ("b2m", 1))
                    tr.inc(e.tensor_scalar(
                        out=xn_tm[:, s, :], in0=xb[:, s, :],
                        scalar1=st1[:, 12 + s:13 + s],
                        scalar2=st1[:, 14 + s:15 + s],
                        op0=ALU.mult, op1=ALU.add),
                        "v", key=("xn", t, s))

            def ln_block(t, xb):
                ln_stats(t, xb)
                ln_apply(t, xb)

            def p1x_block(t):
                xb_ = xF[:, t % 2, :, :]
                for s in range(2 if PHASE >= 2 else 0):
                    for oh in range(2):
                        sl_ = p1t[:, s, oh * 512:(oh + 1) * 512]
                        tr.wait(e, E, ("b2e", 1))
                        tr.wait_opt(e, E, ("p1st", t - 1))
                        if t == 0:
                            tr.wait(e, E, "xnl16")
                        tr.inc(e.tensor_add(
                            sl_, xb_[:, s, oh * 512:(oh + 1) * 512],
                            b2rep[:, oh * 512:(oh + 1) * 512]),
                            "v", key=("p1x", t, s, oh))

            def p1_block(t):
                for s in range(2 if PHASE >= 2 else 0):
                    for oh in range(2):
                        sl_ = p1t[:, s, oh * 512:(oh + 1) * 512]
                        tr.wait(e, E, ("fgrp", t, 31), ("p1x", t, s, oh))
                        tr.inc(e.tensor_add(
                            sl_, sl_, pb[4 + s * 2 + oh][:, :]),
                            "v", key=("p1a", t, s, oh))

            for t in range(NT1 if PHASE >= 1 else 0):
                xb = xF[:, t % 2, :, :]
                if t == 0:
                    ln_block(0, xb)
                def k_evac(oc):
                    bank = pb[oc % 4]
                    tr.wait(e, E, ("kmm", t, oc), ("kex", t, oc))
                    tr.wait_opt(e, E, ("kred", t - 1), ("kst", t - 1))
                    tr.inc(e.tensor_scalar_max(kbf[:, oc, :],
                                               bank[:, 0:256], 0.0),
                           "v", key=("krl", t, oc))
                    tr.wait(e, E, ("krl", t, oc))
                    tr.inc(e.scalar_tensor_tensor(
                        out=kbf[:, oc, :], in0=ex1[:, oc % 2, :], scalar=1.0,
                        in1=kbf[:, oc, :], op0=ALU.min, op1=ALU.add),
                        "v", key=("kst3", t, oc))

                k_evac(0)
                k_evac(1)
                if t + 1 < NT1:
                    ln_stats(t + 1, xF[:, (t + 1) % 2, :, :])
                for oc in range(2, 8):
                    k_evac(oc)

                if t == 0 and PHASE >= 1:
                    v_folds()
                if t + 1 < NT1:
                    ln_apply(t + 1, xF[:, (t + 1) % 2, :, :])
                p1x_block(t)
                tr.wait(e, E, *[("kst3", t, oc) for oc in range(8)])
                tr.wait_opt(e, E, ("ksum", t - 1))
                tr.inc(e.reduce_sum(out=kredb[:, :], in_=kbf, axis=AX.X),
                       "v", key=("kred", t))
                tr.wait(e, E, "ksum0", ("kred", t))
                tr.inc(e.tensor_tensor(out=ksum_r[:, :], in0=ksum_r[:, :],
                                       in1=kredb[:, :], op=ALU.add),
                       "v", key=("ksum", t))
                if t + 1 >= NT1 and PHASE >= 3:
                    tr.wait(e, E, ("ksum", NT1 - 1), "c_mb")
                    tr.inc(e.tensor_scalar_mul(ccin_s[:, 0:8], ksum_r[:, :],
                                               mkb[:, 0:1]), "v",
                           key=("ccm", 0))
                    tr.wait(e, E, ("ccm", 0))
                    tr.inc(e.tensor_scalar_mul(ccin_s[:, 8:16], ksum_r[:, :],
                                               mkb[:, 1:2]), "v",
                           key="ccin_v")
                p1_block(t)
            if PHASE < 3:
                return
            tr.wait(e, E, ("fgrp", NT1 - 1, 31), ("kst", NT1 - 1),
                    ("kred", NT1 - 1))
            tr.inc(e.memset(rqk[:, :], 0.0), "v", key="rqk0")
            tr.inc(e.memset(mwr[:, :], 1.0), "v", key="mwr1")
            tr.wait(e, E, "ccout_d")
            tr.inc(e.tensor_scalar_mul(ksum_g[:, :], ccout_s[:, 0:8],
                                       mkb[:, 0:1]), "v", key=("ccg", 0))
            tr.wait(e, E, ("ccg", 0))
            tr.inc(e.scalar_tensor_tensor(
                out=ksum_g[:, :], in0=ccout_s[:, 8:16], scalar=mkb[:, 1:2],
                in1=ksum_g[:, :], op0=ALU.mult, op1=ALU.add),
                "v", key="ksumg")
            for fc in range(8):
                if fc == 0:
                    tr.wait(e, E, "ksumg", "c_sa")
                tr.inc(e.tensor_scalar_mul(
                    selAk[:, fc * 16:(fc + 1) * 16],
                    selA[:, fc * 16:(fc + 1) * 16], ksum_g[:, fc:fc + 1]),
                    "v", key=("sak", fc))
            def v_qrl(u, lo=0, hi=8):
                for oc in range(lo, hi):
                    tr.wait(e, E, ("qmm", u, oc), ("qex", u, oc))
                    if u == 0:
                        tr.wait(e, E, ("fgrp", NT1 - 1, 11))
                    tr.inc(e.tensor_scalar_max(qbf[:, oc, :],
                                               pb[oc % 2][:, :], 0.0),
                           "v", key=("qrl", u, oc))
                    tr.wait(e, E, ("qrl", u, oc), ("qex", u, oc))
                    tr.inc(e.scalar_tensor_tensor(
                        out=qbf[:, oc, :], in0=ex2[:, oc % 2, :], scalar=1.0,
                        in1=qbf[:, oc, :], op0=ALU.min, op1=ALU.add),
                        "v", key=("qst3", u, oc))

            for u in range(NT2 if PHASE >= 4 else 0):
                if u == 0:
                    v_qrl(0)
                for fc in range(8):
                    tr.wait(e, E, ("qst3", u, fc), ("kld", u))
                    tr.wait_opt(e, E, ("qkp", u, fc - 2),
                                ("sqp", u - 1, 6 + fc % 2),
                                ("tT", NT1 - 1, 3) if u == 0 else None)
                    tr.inc(e.tensor_tensor(
                        out=prodp[:, fc % 2, :], in0=qbf[:, fc, :],
                        in1=k2v[:, fc, :], op=ALU.mult),
                        "v", key=("prod", u, fc))
                for hh in range(2):
                    cs = slice(hh * 256, (hh + 1) * 256)
                    if hh == 0:
                        tr.wait(e, E, ("nmp", u, 7))
                        tr.wait_opt(e, E, ("rmul", u - 1, 1),
                                    ("p1st", NT1 - 1) if u == 0 else None)
                    tr.inc(e.tensor_scalar_add(recf[:, cs], pb[5][0:16, cs],
                                               1e-6),
                           "v", key=("rrec1", u, hh))
                    tr.wait(e, E, ("rrec1", u, hh))
                    tr.inc(e.reciprocal(recf[:, cs], recf[:, cs]),
                           "v", key=("rrec2", u, hh))
                    tr.wait(e, E, ("qkp", u, 7), ("rrec2", u, hh), "rqk0")
                    if hh == 0:
                        tr.wait_opt(e, E, ("rbc", u - 1, 7, 1))
                    tr.inc(e.tensor_tensor(out=rqk[0:16, cs],
                                           in0=pb[4][0:16, cs],
                                           in1=recf[:, cs], op=ALU.mult),
                           "v", key=("rmul", u, hh))
                for fc in range(8):
                    tr.wait(e, E, ("rbc", u, fc, 1), ("vev", u, fc))
                    tr.inc(e.tensor_tensor(
                        out=vbf[:, fc, :], in0=vbf[:, fc, :],
                        in1=pb[2 + fc % 2][:, :], op=ALU.mult),
                        "v", key=("zev", u, fc))
                tr.wait(e, E, ("smp", u, 7))
                tr.wait_opt(e, E, ("p1st", NT1 - 1) if u == 0 else
                            ("mwr0", u - 1))
                tr.inc(e.tensor_scalar_mul(stat2[0:1, 0:512], pb[6][0:1, :],
                                           RE(D)), "v", key=("mu", u))
                tr.wait(e, E, ("mu", u))
                tr.inc(e.tensor_tensor(out=stat2[0:1, 512:1024],
                                       in0=stat2[0:1, 0:512],
                                       in1=stat2[0:1, 0:512], op=ALU.mult),
                       "v", key=("musq", u))
                tr.wait(e, E, ("sqp", u, 7), ("musq", u))
                tr.inc(e.scalar_tensor_tensor(
                    out=stat2[0:1, 1024:1536], in0=pb[7][0:1, :],
                    scalar=RE(D), in1=stat2[0:1, 512:1024],
                    op0=ALU.mult, op1=ALU.subtract),
                    "v", key=("var", u))
                tr.wait(e, E, ("var", u))
                tr.inc(e.tensor_scalar_add(stat2[0:1, 1024:1536],
                                           stat2[0:1, 1024:1536], 1e-5),
                       "v", key=("vare", u))
                tr.wait(e, E, ("vare", u))
                if u == 0:
                    tr.wait(e, E, ("rstd1", NT1 - 1))
                tr.inc(e.reciprocal(stat2[0:1, 1536:2048],
                                    stat2[0:1, 1024:1536]),
                       "v", key=("rec2", u))
                tr.wait(e, E, ("rstds", u))
                if u == 0:
                    tr.wait(e, E, ("kst", NT1 - 1))
                tr.wait_opt(e, E, ("b1m", u - 1))
                tr.inc(e.tensor_copy(out=rstdb[0:1, :],
                                     in_=stat2[0:1, 512:1024]),
                       "v", key=("rstdbf", u))
                tr.wait(e, E, "mwr1")
                tr.wait_opt(e, E, ("b2m2", u - 1))
                tr.inc(e.scalar_tensor_tensor(
                    out=mwr[0:1, :], in0=stat2[0:1, 0:512], scalar=-1.0,
                    in1=stat2[0:1, 512:1024], op0=ALU.mult, op1=ALU.mult),
                    "v", key=("mwr0", u))
                for fc in range(8):
                    tr.wait(e, E, ("b1m", u), ("zev", u, fc))
                    if u == 0:
                        tr.wait(e, E, ("xnTst", NT1 - 1))
                    tr.wait_opt(e, E, ("zn", u, fc - 2))
                    tr.inc(e.tensor_tensor(
                        out=t1p[:, fc % 2, :], in0=vbf[:, fc, :],
                        in1=pb[6][:, :], op=ALU.mult),
                        "v", key=("t1", u, fc))
                    tr.wait(e, E, ("b2m2", u), ("t1", u, fc))
                    tr.inc(e.tensor_tensor(
                        out=t1p[:, fc % 2, :], in0=t1p[:, fc % 2, :],
                        in1=pb[7][:, :], op=ALU.add),
                        "v", key=("t2", u, fc))

                if u + 1 < NT2:
                    v_qrl(u + 1, 0, 2)
                for g in range(8):
                    s, oh = g // 2, g % 2
                    tr.wait(e, E, ("womm", u, g), ("p1ld", u))
                    tr.inc(e.tensor_add(
                        p1in[:, s, oh * 512:(oh + 1) * 512],
                        p1in[:, s, oh * 512:(oh + 1) * 512],
                        pb[2 + g % 2][:, :]),
                        "v", key=("oev", u, g))
                if u + 1 < NT2:
                    v_qrl(u + 1, 2, 8)

        def body_scalar(e):
            E = "s"
            tr.wait(e, E, ("pl", 6))
            tr.inc(e.activation(out=st16[:, 6:7], in_=st16[:, 5:6],
                                func=AF.Sqrt, scale=float(D)),
                   "s", key="pl_sqrt")
            def rstd_block(t):
                tr.wait(e, E, ("rec1", t))
                tr.inc(e.activation(out=st1[:, 12:14], in_=st1[:, 10:12],
                                    func=AF.Sqrt, scale=float(D)),
                       "s", key=("rstd1", t))

            for t in range(NT1 if PHASE >= 1 else 0):
                xb = xF[:, t % 2, :, :]
                if t == 0:
                    for s_ in range(2):
                        tr.wait(e, E, ("xld", 0))
                        if s_ == 0:
                            tr.wait(e, E, ("xl8", 0), ("xl8", 1),
                                    ("xl8", 2))
                        else:
                            tr.wait(e, E, ("sqs", 0, 0))
                        tr.inc(e.activation(out=xsq[:, s_, :],
                                            in_=xb[:, s_, :],
                                            func=AF.Square),
                               "s", key=("xsqs", 0, s_))
                    rstd_block(0)
                for g in range(4):
                    s_, half = g // 2, g % 2
                    tr.wait(e, E, ("tT", t, g))
                    tr.wait_opt(e, E, ("h", t - 1, 31), ("xnTst", t - 1))
                    dst = xnT1[:, half * 4:(half + 1) * 4,
                               s_ * 128:(s_ + 1) * 128]
                    tr.inc(e.activation(
                        out=dst,
                        in_=pb[g % 2][:, :].rearrange("p (q n) -> p q n",
                                                      q=4),
                        func=AF.Identity),
                        "s", key=("tTe", t, g))
                tr.wait(e, E, *[("tTe", t, g) for g in range(4)])
                Qst.issue(e, E, ("xnTst", t), lambda t=t: e.dma_start(
                    out=xnt_scr[t // 2, :, :,
                                (t % 2) * 256:(t % 2) * 256 + 256],
                    in_=xnT1))
                for oc in range(8):
                    tr.wait(e, E, ("kmm", t, oc))
                    tr.wait_opt(e, E, ("kst3", t, oc - 2))
                    tr.inc(e.activation(out=ex1[:, oc % 2, :],
                                        in_=pb[oc % 4][:, 0:256],
                                        func=AF.Exp),
                           "s", key=("kex", t, oc))
                    if t + 1 < NT1 and oc in (0, 2):
                        s_ = oc // 2
                        tr.wait(e, E, ("xld", t + 1))
                        tr.wait_opt(e, E, ("sqs", t, s_))
                        tr.inc(e.activation(out=xsq[:, s_, :],
                                            in_=xF[:, (t + 1) % 2, s_, :],
                                            func=AF.Square),
                               "s", key=("xsqs", t + 1, s_))
                if t > 0 and PHASE >= 2:
                    tr.wait(e, E, *[("p1a", t - 1, s_, oh)
                                    for s_ in range(2) for oh in range(2)])
                    Qst.issue(e, E, ("p1st", t - 1),
                              lambda t=t: e.dma_start(
                        out=p1_scr[(t - 1) // 2, :,
                                   ((t - 1) % 2) * 2:((t - 1) % 2) * 2 + 2,
                                   :],
                        in_=p1t))
                if t + 1 < NT1:
                    rstd_block(t + 1)
                tr.wait(e, E, *[("kst3", t, oc) for oc in range(8)])
                Qst.issue(e, E, ("kst", t), lambda t=t: e.dma_start(
                    out=k_scr[t // 2, :, :,
                              (t % 2) * 256:(t % 2) * 256 + 256],
                    in_=kbf))
                for cc in range(32 if PHASE >= 2 else 0):
                    tr.wait(e, E, ("h", t, cc), "c_b1")
                    tr.wait_opt(e, E, ("fgrp", t, cc - 2))
                    tr.inc(e.activation(out=hc[:, cc % 2, :],
                                        in_=pb[2 + cc % 2][:, 0:256],
                                        func=AF.Gelu, bias=b1c[:, cc:cc + 1]),
                           "s", key=("ge", t, cc))

                if PHASE >= 2 and t == NT1 - 1:
                    tr.wait(e, E, *[("p1a", t, s, oh)
                                    for s in range(2) for oh in range(2)])
                    Qst.issue(e, E, ("p1st", t), lambda t=t: e.dma_start(
                        out=p1_scr[t // 2, :,
                                   (t % 2) * 2:(t % 2) * 2 + 2, :],
                        in_=p1t))
            def ost_block(u):
                tr.wait(e, E, *[("oev", u, g) for g in range(8)])
                Qst.issue(e, E, ("ost", u), lambda u=u: e.dma_start(
                    out=out_d[u * 512:(u + 1) * 512, :].rearrange(
                        "(s p) d -> p s d", p=128),
                    in_=p1in))

            for u in range(NT2 if PHASE >= 4 else 0):
                for oc in range(8):
                    tr.wait(e, E, ("qmm", u, oc))
                    if u == 0 and oc < 2:
                        tr.wait(e, E, ("kst", NT1 - 1))
                    tr.wait_opt(e, E, ("qst3", u, oc - 2))
                    tr.inc(e.activation(out=ex2[:, oc % 2, :],
                                        in_=pb[oc % 2][:, :], func=AF.Exp),
                           "s", key=("qex", u, oc))
                if u > 0:
                    ost_block(u - 1)
                for oc in range(8):
                    tr.wait(e, E, ("vmm", u, oc))
                    if u == 0:
                        tr.wait(e, E, ("fgrp", NT1 - 1, 15))
                    tr.wait_opt(e, E, ("smp", u - 1, oc), ("t1", u - 1, oc))
                    tr.inc(e.activation(out=vbf[:, oc, :],
                                        in_=pb[2 + oc % 2][:, :],
                                        func=AF.Identity),
                           "s", key=("vev", u, oc))
                for oc in range(8):
                    tr.wait(e, E, ("gmm", u, oc))
                    if u == 0:
                        tr.wait(e, E, ("h", NT1 - 1, 31))
                    else:
                        tr.wait_opt(e, E, ("zn", u - 1, oc))
                    tr.inc(e.activation(out=sgbf[:, oc, :],
                                        in_=pb[oc % 2][:, :], func=AF.Silu),
                           "s", key=("sgev", u, oc))
                for fc in range(8):
                    tr.wait(e, E, ("zev", u, fc))
                    tr.wait_opt(e, E, ("qkp", u, 7), ("sqp", u, fc - 2))
                    tr.inc(e.activation(out=prodp[:, fc % 2, :],
                                        in_=vbf[:, fc, :], func=AF.Square),
                           "s", key=("zsq", u, fc))
                tr.wait(e, E, ("rec2", u), ("var", u))
                tr.inc(e.activation(out=stat2[0:1, 512:1024],
                                    in_=stat2[0:1, 1536:2048], func=AF.Sqrt),
                       "s", key=("rstds", u))
                for fc in range(8):
                    tr.wait(e, E, ("t2", u, fc), "gbcol")
                    tr.wait_opt(e, E, ("zn", u, fc - 2))
                    tr.inc(e.activation(out=t1p[:, fc % 2, :],
                                        in_=t1p[:, fc % 2, :],
                                        func=AF.Identity,
                                        bias=gbcol[:, 8 + fc:9 + fc],
                                        scale=gbcol[:, fc:fc + 1]),
                           "s", key=("t3", u, fc))
                if u == NT2 - 1:
                    ost_block(u)

        def body_tensor(e):
            E = "t"
            for oh in range(2 if PROLOG >= 1 else 0):
                tr.wait(e, E, "b2bf", "c_or")
                tr.minc(e.matmul(pb[oh][:, :], onesr[0:1, :],
                                b2bf[0:1, oh * 512:(oh + 1) * 512],
                                start=True, stop=True),
                       "t", key=("b2m", oh))
            def w1fold():
              for dc in range(8 if PROLOG >= 2 else 0):
                for hs in range(8):
                    i, rsl = hs // 2, hs % 2
                    tr.wait(e, E, ("xl8", 0), ("xl8", 1), ("xl8", 2))
                    if hs >= 4:
                        tr.wait(e, E, ("w1e", dc, hs - 4))
                    elif dc > 0:
                        tr.wait(e, E, ("w1e", dc - 1, hs + 4))
                    tr.minc(e.matmul(
                        pb[4 + hs % 4][:, :],
                        xl8(2 * i + 1, dc * 128, (dc + 1) * 128),
                        xl8(2 * i, rsl * 512, (rsl + 1) * 512),
                        start=True, stop=True),
                        "t", key=("w1m", dc, hs))
            def w2fold():
              for cc in range(32 if PROLOG >= 3 else 0):
                j, c0 = cc // 8, (cc % 8) * 128
                pair = (4, 5) if cc % 2 == 0 else (6, 7)
                for i in range(4):
                    bank = pair[i // 2]
                    off = (i % 2) * 256
                    tr.wait(e, E, ("xl8", 0), ("xl8", 1), ("xl8", 2))
                    if cc < 2:
                        tr.wait(e, E, ("w1e", 7, bank - 4), ("w1e", 7, bank))
                    else:
                        tr.wait(e, E, ("w2e", cc - 2, i // 2))
                    rhs = xl8(2 * i + 8, j * 256, (j + 1) * 256)
                    tr.minc(e.matmul(
                        pb[bank][:, off:off + 256],
                        xl8(2 * i + 9, c0, c0 + 128), rhs,
                        start=True, stop=True),
                        "t", key=("w2m", cc, i // 2) if i % 2 == 1 else None)
            if PHASE < 1:
                w1fold()
                w2fold()
            for t in range(NT1 if PHASE >= 1 else 0):
                for g in range(4):
                    s, half = g // 2, g % 2
                    tr.wait(e, E, ("xn", t, s), "c_id")
                    if g >= 2:
                        tr.wait(e, E, ("tTe", t, g - 2))
                    elif t > 0:
                        tr.wait(e, E, ("krl", t - 1, 4 + g),
                                ("kex", t - 1, 4 + g))
                    else:
                        tr.wait(e, E, ("b2e", g))
                    for q in range(4):
                        fc = half * 4 + q
                        tr.minc(e.matmul(
                            pb[g % 2][:, q * 128:(q + 1) * 128],
                            xn_tm[:, s, fc * 128:(fc + 1) * 128],
                            ident[:, :], start=True, stop=True),
                            "t", key=("tT", t, g) if q == 3 else None)
                for oc in range(8):
                    bank = pb[oc % 4]
                    tr.wait(e, E, ("tTe", t, 3), "wl_wk")
                    if oc < 2:
                        tr.wait(e, E, ("tTe", t, 2 + oc))
                    elif oc < 4:
                        tr.wait_opt(e, E, ("ge", t - 1, 28 + oc)
                                    if PHASE >= 2 else ("krl", t - 1, 4 + oc))
                    else:
                        tr.wait(e, E, ("krl", t, oc - 4), ("kex", t, oc - 4))
                    for dc in range(8):
                        tr.minc(e.matmul(
                            bank[:, 0:256], wk_s(dc, oc), xnT1[:, dc, :],
                            start=(dc == 0), stop=(dc == 7)),
                            "t", key=("kmm", t, oc) if dc == 7 else None)
                if t == 0:
                    w1fold()
                    w2fold()
                for cc in range(33 if PHASE >= 2 else 0):
                    if cc < 32:
                        tr.wait(e, E, ("w1e", 7, 7))
                        if cc < 2:
                            tr.wait(e, E, ("krl", t, 6 + cc),
                                    ("kex", t, 6 + cc))
                        else:
                            tr.wait(e, E, ("ge", t, cc - 2))
                        for dc in range(8):
                            tr.minc(e.matmul(
                                pb[2 + cc % 2][:, 0:256],
                                w1eff(dc, cc * 128, (cc + 1) * 128),
                                xnT1[:, dc, :],
                                start=(dc == 0), stop=(dc == 7)),
                                "t", key=("h", t, cc) if dc == 7 else None)
                    if cc >= 1:
                        ccw = cc - 1
                        tr.wait(e, E, ("ge", t, ccw), ("w2e", 31, 1))
                        if ccw == 0:
                            if t == 0:
                                tr.wait(e, E, ("w2e", 30, 0), ("w2e", 30, 1),
                                        ("w2e", 31, 0), ("w2e", 31, 1))
                            else:
                                tr.wait(e, E, *[("p1a", t - 1, s, oh)
                                                for s in range(2)
                                                for oh in range(2)])
                        nmm = 0
                        for s in range(2):
                            for oh in range(2):
                                nmm += 1
                                tr.minc(e.matmul(
                                    pb[4 + s * 2 + oh][:, :],
                                    hc[:, ccw % 2, s * 128:(s + 1) * 128],
                                    w2eff(ccw, oh * 512, (oh + 1) * 512),
                                    start=(ccw == 0), stop=(ccw == 31)),
                                    "t", key=("fgrp", t, ccw)
                                    if nmm == 4 else None)
            def s2_q(u, lo=0, hi=8):
                for oc in range(lo, hi):
                    tr.wait(e, E, ("xnld", u), "wl_wq")
                    if oc < 2:
                        if u == 0:
                            tr.wait(e, E, ("tTe", NT1 - 1, 2 + oc))
                        else:
                            tr.wait(e, E, ("sgev", u - 1, 6 + oc))
                    else:
                        tr.wait(e, E, ("qrl", u, oc - 2), ("qex", u, oc - 2))
                    for dc in range(8):
                        tr.minc(e.matmul(
                            pb[oc % 2][:, :], wq_s(dc, oc), xnT2v[:, dc, :],
                            start=(dc == 0), stop=(dc == 7)),
                            "t", key=("qmm", u, oc) if dc == 7 else None)

            def s2_v(u):
                for oc in range(8):
                    tr.wait(e, E, "wl_wv")
                    if oc < 2:
                        if u == 0:
                            tr.wait(e, E, ("ge", NT1 - 1, 30 + oc))
                        else:
                            tr.wait(e, E, ("oev", u - 1, 6 + oc))
                    else:
                        tr.wait(e, E, ("vev", u, oc - 2))
                    for dc in range(8):
                        tr.minc(e.matmul(
                            pb[2 + oc % 2][:, :], wv_s(dc, oc),
                            xnT2v[:, dc, :],
                            start=(dc == 0), stop=(dc == 7)),
                            "t", key=("vmm", u, oc) if dc == 7 else None)

            def s2_g(u):
                for oc in range(8):
                    tr.wait(e, E, "wl_wg")
                    if oc < 2:
                        tr.wait(e, E, ("qrl", u, 6 + oc), ("qex", u, 6 + oc))
                    else:
                        tr.wait(e, E, ("sgev", u, oc - 2))
                    for dc in range(8):
                        tr.minc(e.matmul(
                            pb[oc % 2][:, :], wg_s(dc, oc), xnT2v[:, dc, :],
                            start=(dc == 0), stop=(dc == 7)),
                            "t", key=("gmm", u, oc) if dc == 7 else None)

            for u in range(NT2 if PHASE >= 4 else 0):
                if u == 0:
                    s2_q(0)
                    s2_v(0)
                    s2_g(0)
                for fc in range(8):
                    tr.wait(e, E, ("prod", u, fc), "c_sa")
                    if fc == 0:
                        if u == 0:
                            tr.wait(e, E, ("p1a", NT1 - 1, 0, 0))
                        else:
                            tr.wait(e, E, ("rmul", u - 1, 1))
                    tr.inc(e.matmul(
                        pb[4][0:16, :], selA[:, fc * 16:(fc + 1) * 16],
                        prodp[:, fc % 2, :],
                        start=(fc == 0), stop=(fc == 7)),
                        "t", key=("qkp", u, fc))
                    tr.wait(e, E, ("qst3", u, fc))
                    if fc == 0:
                        if u == 0:
                            tr.wait(e, E, ("p1a", NT1 - 1, 0, 1), ("sak", 7))
                        else:
                            tr.wait(e, E, ("rrec1", u - 1, 1))
                    tr.inc(e.matmul(
                        pb[5][0:16, :], selAk[:, fc * 16:(fc + 1) * 16],
                        qbf[:, fc, :],
                        start=(fc == 0), stop=(fc == 7)),
                        "t", key=("nmp", u, fc))
                for fc in range(8):
                    for hh in range(2):
                        cs = slice(hh * 256, (hh + 1) * 256)
                        tr.wait(e, E, ("rmul", u, hh), "c_sb")
                        if hh == 0:
                            if fc < 2:
                                tr.wait(e, E, ("vev", u, 6 + fc))
                            else:
                                tr.wait(e, E, ("zev", u, fc - 2))
                        tr.minc(e.matmul(
                            pb[2 + fc % 2][:, cs],
                            selB[:, fc * 128:(fc + 1) * 128], rqk[:, cs],
                            start=True, stop=True),
                            "t", key=("rbc", u, fc, hh)
                            if hh == 1 else None)
                for fc in range(8):
                    tr.wait(e, E, ("zev", u, fc), "c_o1")
                    if fc == 0:
                        if u == 0:
                            tr.wait(e, E, ("p1a", NT1 - 1, 1, 0))
                        else:
                            tr.wait(e, E, ("t1", u - 1, 7))
                    tr.inc(e.matmul(
                        pb[6][0:1, :], ones1[:, :], vbf[:, fc, :],
                        start=(fc == 0), stop=(fc == 7)),
                        "t", key=("smp", u, fc))
                    tr.wait(e, E, ("zsq", u, fc))
                    if fc == 0:
                        if u == 0:
                            tr.wait(e, E, ("p1a", NT1 - 1, 1, 1))
                        else:
                            tr.wait(e, E, ("t2", u - 1, 7))
                    tr.inc(e.matmul(
                        pb[7][0:1, :], ones1[:, :], prodp[:, fc % 2, :],
                        start=(fc == 0), stop=(fc == 7)),
                        "t", key=("sqp", u, fc))
                if u + 1 < NT2:
                    s2_q(u + 1, 0, 2)
                tr.wait(e, E, ("rstdbf", u), ("mu", u), "c_or")
                tr.inc(e.matmul(pb[6][:, :], onesr[0:1, :], rstdb[0:1, :],
                                start=True, stop=True),
                       "t", key=("b1m", u))
                tr.wait(e, E, ("mwr0", u), ("var", u))
                tr.inc(e.matmul(pb[7][:, :], onesr[0:1, :], mwr[0:1, :],
                                start=True, stop=True),
                       "t", key=("b2m2", u))
                for g in range(8):
                    s, oh = g // 2, g % 2
                    if g == 0:
                        tr.wait(e, E, "wl_wo")
                    if g < 2:
                        tr.wait(e, E, ("zev", u, 6 + g))
                    else:
                        tr.wait(e, E, ("oev", u, g - 2))
                    for fc in range(8):
                        if g == 0:
                            tr.wait(e, E, ("zn", u, fc))
                        tr.minc(e.matmul(
                            pb[2 + g % 2][:, :],
                            znb[:, fc, s * 128:(s + 1) * 128],
                            wo_s(fc, oh * 512, (oh + 1) * 512),
                            start=(fc == 0), stop=(fc == 7)),
                            "t", key=("womm", u, g) if fc == 7 else None)
                if u + 1 < NT2:
                    s2_q(u + 1, 2, 8)
                    s2_v(u + 1)
                    s2_g(u + 1)

        bodies = [("sync", body_sync), ("g", body_gpsimd),
                  ("v", body_vector), ("s", body_scalar),
                  ("t", body_tensor)]

        fake = _FakeEng()
        for _, b in bodies:
            b(fake)
        tr.begin_emit()
        block = ctx.enter_context(nc.Block())
        block.sync(body_sync)
        block.gpsimd(body_gpsimd)
        block.vector(body_vector)
        block.scalar(body_scalar)
        block.tensor(body_tensor)
        tr.check()
    return nc


def _rearr_w(w):
    M = w.shape[1]
    return np.ascontiguousarray(
        w.reshape(8, 128, M).transpose(1, 0, 2).reshape(128, 8 * M))


def _rearr_w32(w):
    M = w.shape[1]
    return np.ascontiguousarray(
        w.reshape(32, 128, M).transpose(1, 0, 2).reshape(128, 32 * M))


_E4PERM = (np.arange(1024).reshape(256, 4).T.reshape(-1))


def xl16_host(lt):
    rows = []
    for lti in [0, 1, 6, 7, 2, 3, 8, 9, 10, 11, 4, 5, 12, 13, 14, 15]:
        r = lt[lti]
        if lti >= 8 and lti % 2 == 0:
            r = r[_E4PERM]
        rows.append(r)
    return np.ascontiguousarray(np.stack(rows))


def _consts():
    ident = np.eye(128, dtype=ml_dtypes.bfloat16)
    selA = np.zeros((128, 8 * 16), dtype=ml_dtypes.bfloat16)
    selB = np.zeros((32, 8 * 128), dtype=ml_dtypes.bfloat16)
    for fc in range(8):
        for p in range(128):
            h = 2 * fc + p // 64
            selA[p, fc * 16 + h] = 1
            selB[h, fc * 128 + p] = 1
    ones1 = np.ones((128, 1), dtype=ml_dtypes.bfloat16)
    onesr = np.ones((1, 128), dtype=ml_dtypes.bfloat16)
    return ident, selA, selB, ones1, onesr


def prepare(L0_attn_hidden_states, L0_attn_Wq, L0_attn_Wk, L0_attn_Wv,
            L0_attn_Wg, L0_attn_Wo, L0_attn_gamma, L0_attn_beta,
            L0_ffn_W1, L0_ffn_b1, L0_ffn_W2, L0_ffn_b2, L0_ffn_scale):
    x = np.asarray(L0_attn_hidden_states)
    B, S, _ = x.shape
    t_core = B * S // NCORES
    cores_per_b = NCORES // B

    wq = _rearr_w(np.asarray(L0_attn_Wq))
    wk = _rearr_w(np.asarray(L0_attn_Wk))
    wv = _rearr_w(np.asarray(L0_attn_Wv))
    wg = _rearr_w(np.asarray(L0_attn_Wg))
    wo = _rearr_w(np.asarray(L0_attn_Wo))
    w1s = _rearr_w(np.ascontiguousarray(np.asarray(L0_ffn_W1).T))
    w2s = _rearr_w32(np.ascontiguousarray(np.asarray(L0_ffn_W2).T))
    b1c = np.ascontiguousarray(np.asarray(L0_ffn_b1).reshape(32, 128).T)
    b2r = np.ascontiguousarray(np.asarray(L0_ffn_b2).reshape(1, D))
    gbcol = np.ascontiguousarray(np.concatenate(
        [np.asarray(L0_attn_gamma).reshape(8, 128).T,
         np.asarray(L0_attn_beta).reshape(8, 128).T], axis=1))
    scl = np.ascontiguousarray(
        np.broadcast_to(np.asarray(L0_ffn_scale).reshape(1, 1), (128, 1))
    ).astype(np.float32)
    ident, selA, selB, ones1, onesr = _consts()

    nc = build_program(t_core)
    in_maps = []
    for c in range(NCORES):
        b = c // cores_per_b
        q = c % cores_per_b
        mkb = np.zeros((128, 2), dtype=np.float32)
        mkb[:, b] = 1.0
        in_maps.append(dict(
            x_in=np.ascontiguousarray(x[b, q * t_core:(q + 1) * t_core]),
            xl16=xl16_host(x[b, 1:17]),
            wq=wq, wk=wk, wv=wv, wg=wg, wo=wo, w1s=w1s, w2s=w2s,
            b1c=b1c, b2r=b2r, gbcol=gbcol, scl=scl, mkb=mkb,
            idn=ident, selA=selA, selB=selB, ones128=ones1, onesrow=onesr,
        ))
    return nc, in_maps


def assemble(results, B, S):
    t_core = B * S // NCORES
    cores_per_b = NCORES // B
    out = np.empty((B, S, D), dtype=np.float32)
    for c in range(NCORES):
        b = c // cores_per_b
        q = c % cores_per_b
        out[b, q * t_core:(q + 1) * t_core] = results[c]["out"]
    return out


def kernel(L0_attn_hidden_states, **kw):
    x = np.asarray(L0_attn_hidden_states)
    nc, in_maps = prepare(L0_attn_hidden_states, **kw)
    res = run_bass_kernel_spmd(nc, in_maps, core_ids=list(range(NCORES)))
    return assemble(res.results, x.shape[0], x.shape[1])


# version 20
# speedup vs baseline: 1.0718x; 1.0120x over previous
import os
import numpy as np
import ml_dtypes
import concourse.bass as bass
import concourse.mybir as mybir
from concourse.bass_utils import run_bass_kernel_spmd
from contextlib import ExitStack

F32 = mybir.dt.float32
BF16 = mybir.dt.bfloat16
AF = mybir.ActivationFunctionType
ALU = mybir.AluOpType
AX = mybir.AxisListType

D = 1024
HF = 4096
NCORES = 8
PHASE = int(os.environ.get("KPHASE", "4"))
PROLOG = int(os.environ.get("KPROLOG", "3"))


class _FakeInstr:
    def then_inc(self, sem, n=1):
        return self


class _FakeEng:
    def __getattr__(self, name):
        return lambda *a, **k: _FakeInstr()


class _Tr:

    def __init__(self, sems):
        self.sems = sems
        self.plan = True
        self.count = {k: 0 for k in sems}
        self.evt = {}
        self.waited = {}

    def begin_emit(self):
        assert self.plan
        self._plan_count = dict(self.count)
        self.plan = False
        self.count = {k: 0 for k in self.sems}
        self.waited = {}
        for q in getattr(self, "queues", []):
            q.cnt = 0

    def wait_val(self, eng, ename, sem, val):
        if self.plan or val <= 0:
            return
        w = self.waited.setdefault(ename, {})
        if w.get(sem, 0) >= val:
            return
        eng.wait_ge(self.sems[sem], val)
        w[sem] = val

    def check(self):
        assert self.count == self._plan_count, (self.count, self._plan_count)

    def inc(self, instr, sem, n=1, key=None):
        self.count[sem] += n
        tok = (sem, self.count[sem])
        if self.plan:
            if key is not None:
                assert key not in self.evt, key
                self.evt[key] = tok
        else:
            instr.then_inc(self.sems[sem], n)
            if key is not None:
                assert self.evt[key] == tok, (key, self.evt[key], tok)
        return tok

    def minc(self, instr, sem, key=None):
        if key is None:
            return None
        return self.inc(instr, sem, 1, key=key)

    def wait(self, eng, ename, *keys):
        if self.plan:
            return
        w = self.waited.setdefault(ename, {})
        for key in keys:
            if key is None:
                continue
            sem, val = self.evt[key]
            if val <= 0 or w.get(sem, 0) >= val:
                continue
            eng.wait_ge(self.sems[sem], val)
            w[sem] = val

    def wait_opt(self, eng, ename, *keys):
        if self.plan:
            return
        self.wait(eng, ename, *[k for k in keys if k in self.evt])


class _DmaQ:

    def __init__(self, tr, base, n):
        self.tr, self.n = tr, n
        self.names = [f"{base}{i}" for i in range(n)]
        self.cnt = 0
        tr.queues = getattr(tr, "queues", []) + [self]

    def issue(self, eng, ename, key, fn):
        i = self.cnt
        self.cnt += 1
        nm = self.names[i % self.n]
        self.tr.wait_val(eng, ename, nm, 16 * (i // self.n))
        self.tr.inc(fn(), nm, 16, key=key)


def build_program(t_core):
    NT1 = t_core // 256
    NT2 = t_core // 512
    assert NT1 * 256 == t_core and NT2 * 512 == t_core

    nc = bass.Bass("TRN2", target_bir_lowering=False)

    x_in = nc.dram_tensor("x_in", [t_core, D], F32, kind="ExternalInput").ap()
    xl16_d = nc.dram_tensor("xl16", [16, D], F32, kind="ExternalInput").ap()
    wq_d = nc.dram_tensor("wq", [128, 8 * D], F32, kind="ExternalInput").ap()
    wk_d = nc.dram_tensor("wk", [128, 8 * D], F32, kind="ExternalInput").ap()
    wv_d = nc.dram_tensor("wv", [128, 8 * D], F32, kind="ExternalInput").ap()
    wg_d = nc.dram_tensor("wg", [128, 8 * D], F32, kind="ExternalInput").ap()
    wo_d = nc.dram_tensor("wo", [128, 8 * D], F32, kind="ExternalInput").ap()
    w1_d = nc.dram_tensor("w1s", [128, 8 * HF], F32, kind="ExternalInput").ap()
    w2_d = nc.dram_tensor("w2s", [128, 32 * D], F32, kind="ExternalInput").ap()
    b1_d = nc.dram_tensor("b1c", [128, 32], F32, kind="ExternalInput").ap()
    b2_d = nc.dram_tensor("b2r", [1, D], F32, kind="ExternalInput").ap()
    gb_d = nc.dram_tensor("gbcol", [128, 16], F32, kind="ExternalInput").ap()
    sc_d = nc.dram_tensor("scl", [128, 1], F32, kind="ExternalInput").ap()
    mb_d = nc.dram_tensor("mkb", [128, 2], F32, kind="ExternalInput").ap()
    id_d = nc.dram_tensor("idn", [128, 128], BF16, kind="ExternalInput").ap()
    sa_d = nc.dram_tensor("selA", [128, 8 * 16], BF16, kind="ExternalInput").ap()
    sb_d = nc.dram_tensor("selB", [32, 8 * 128], BF16, kind="ExternalInput").ap()
    o1_d = nc.dram_tensor("ones128", [128, 1], BF16, kind="ExternalInput").ap()
    or_d = nc.dram_tensor("onesrow", [1, 128], BF16, kind="ExternalInput").ap()

    out_d = nc.dram_tensor("out", [t_core, D], F32, kind="ExternalOutput").ap()

    xnt_scr = nc.dram_tensor("xnt_scr", [NT2, 128, 8, 512], BF16).ap()
    k_scr = nc.dram_tensor("k_scr", [NT2, 128, 8, 512], BF16).ap()
    p1_scr = nc.dram_tensor("p1_scr", [NT2, 128, 4, D], F32).ap()
    cc_in = nc.dram_tensor("cc_in", [2, 128, 8], F32).ap()
    cc_out = nc.dram_tensor("cc_out", [2, 128, 8], F32).ap()

    ctx = ExitStack()
    with ctx:
        def sbuf(name, shape, dtype):
            return ctx.enter_context(nc.sbuf_tensor(name, shape, dtype))

        wA = sbuf("wA", [128, 73728], BF16)

        def w1eff(dc, lo, hi):
            return wA[:, dc * HF + lo: dc * HF + hi]

        def w2eff(cc, lo, hi):
            return wA[:, 32768 + cc * D + lo: 32768 + cc * D + hi]

        def wk_s(dc, oc):
            b = 65536 + dc * D + oc * 128
            return wA[:, b: b + 128]

        def wq_s(dc, oc):
            b = 65536 + dc * D + oc * 128
            return wA[:, b: b + 128]

        def wv_s(dc, oc):
            b = 49152 + dc * D + oc * 128
            return wA[:, b: b + 128]

        def wg_s(dc, oc):
            b = 57344 + dc * D + oc * 128
            return wA[:, b: b + 128]

        def wo_s(fc, lo, hi):
            return wA[:, fc * D + lo: fc * D + hi]

        xnT2v = wA[:, 32768:36864].rearrange("p (c n) -> p c n", c=8)
        k2v = wA[:, 36864:40960].rearrange("p (c n) -> p c n", c=8)
        qbf = wA[:, 40960:45056].rearrange("p (c n) -> p c n", c=8)
        vbf = wA[:, 45056:49152].rearrange("p (c n) -> p c n", c=8)
        sgbf = wA[:, 8192:12288].rearrange("p (c n) -> p c n", c=8)
        znb = wA[:, 12288:16384].rearrange("p (c n) -> p c n", c=8)

        aB = sbuf("aB", [128, 9728], BF16)
        xn_tm = aB[:, 0:2048].rearrange("p (s d) -> p s d", s=2)
        xnT1 = aB[:, 2048:4096].rearrange("p (c n) -> p c n", c=8)
        hc = aB[:, 4096:4608].rearrange("p (b n) -> p b n", b=2)
        kbf = aB[:, 4608:6656].rearrange("p (c n) -> p c n", c=8)
        mn1 = aB[:, 6656:7168].rearrange("p (b n) -> p b n", b=2)
        ex1 = aB[:, 9216:9728].rearrange("p (b n) -> p b n", b=2)
        xsq = aB[:, 7168:9216].rearrange("p (s d) -> p s d", s=2)
        prodp = aB[:, 0:1024].rearrange("p (b n) -> p b n", b=2)
        nprodp = aB[:, 1024:2048].rearrange("p (b n) -> p b n", b=2)
        t1p = aB[:, 2048:3072].rearrange("p (b n) -> p b n", b=2)
        mn2 = aB[:, 3072:4096].rearrange("p (b n) -> p b n", b=2)
        ex2 = aB[:, 5632:6656].rearrange("p (b n) -> p b n", b=2)
        xsq16 = aB[0:16, 7168:8192]

        aF = sbuf("aF", [128, 6656], F32)
        xF = aF[:, 0:4096].rearrange("p (b s d) -> p b s d", b=2, s=2)
        p1t = aF[:, 4096:6144].rearrange("p (s d) -> p s d", s=2)

        st1 = aF[:, 6144:6160]
        p1in = aF[:, 0:4096].rearrange("p (s d) -> p s d", s=4)
        recf = aF[0:16, 4096:4608]
        stat2 = aF[0:1, 4608:6656]
        xl16 = aF[0:16, 4096:5120]

        st16 = sbuf("st16_s", [16, 8], F32)
        xl8b = sbuf("xl8b_s", [128, 6144], BF16)
        b2bf = aB[0:1, 0:1024]
        rqk = aB[0:32, 4096:4608]
        mwr = aB[0:2, 4608:5120]
        rstdb = aB[0:1, 5120:5632]
        ident = sbuf("ident_s", [128, 128], BF16)
        selA = sbuf("selA_s", [128, 8 * 16], BF16)
        selB = sbuf("selB_s", [32, 8 * 128], BF16)
        ones1 = sbuf("ones1_s", [128, 1], BF16)
        onesr = sbuf("onesr_s", [1, 128], BF16)
        b1c = sbuf("b1c_s", [128, 32], F32)
        b2rep = sbuf("b2rep_s", [128, D], BF16)
        gbcol = sbuf("gbcol_s", [128, 16], F32)
        scl = sbuf("scl_s", [128, 1], F32)
        mkb = sbuf("mkb_s", [128, 2], F32)
        ksum_r = sbuf("ksum_r_s", [128, 8], F32)
        kredb = sbuf("kredb_s", [128, 8], F32)
        ccin_s = sbuf("ccin_s_s", [128, 16], F32)
        ccout_s = sbuf("ccout_s_s", [128, 16], F32)
        ksum_g = sbuf("ksum_g_s", [128, 8], F32)
        selAk = sbuf("selAk_s", [128, 8 * 16], BF16)

        pb = [ctx.enter_context(nc.psum_tensor(f"pb{i}", [128, 512], F32))
              for i in range(8)]

        sems = {}
        for s_ in (["cc", "v", "s", "t", "g"]
                   + [f"ld{i}" for i in range(4)]
                   + [f"st{i}" for i in range(4)]
                   + [f"wl{i}" for i in range(2)]):
            sems[s_] = ctx.enter_context(nc.semaphore("sem_" + s_))
        tr = _Tr(sems)
        Qld = _DmaQ(tr, "ld", 4)
        Qst = _DmaQ(tr, "st", 4)
        Qwl = _DmaQ(tr, "wl", 2)

        _PAIRMAP = {0: (0, 0), 1: (32, 0), 2: (64, 0), 3: (0, 1),
                    4: (32, 1), 5: (32, 2), 6: (64, 1), 7: (64, 2)}

        def xl8(j, lo, hi):
            pair = (j // 2) if j < 8 else (4 + (j - 8) // 2)
            base, blk = _PAIRMAP[pair]
            c0 = blk * 2048 + (j % 2) * 1024
            return xl8b[base:base + 1, c0 + lo: c0 + hi]

        RE = lambda i: 1.0 / i

        def body_sync(e):
            E = "sync"
            Qld.issue(e, E, "ld_xl",
                      lambda: e.dma_start(out=xl16, in_=xl16_d))
            for t in range(min(2, NT1) if PHASE >= 1 else 0):
                Qld.issue(e, E, ("xld", t), lambda t=t: e.dma_start(
                    out=xF[:, t % 2, :, :],
                    in_=x_in[t * 256:(t + 1) * 256, :].rearrange(
                        "(s p) d -> p s d", p=128)))
            for key_, ap_d, ap_s in [
                    ("c_id", id_d, ident[:, :]), ("c_sa", sa_d, selA[:, :]),
                    ("c_sb", sb_d, selB[:, :]), ("c_o1", o1_d, ones1[:, :]),
                    ("c_or", or_d, onesr[:, :]), ("c_b1", b1_d, b1c[:, :]),
                    ("c_sc", sc_d, scl[:, :]), ("c_mb", mb_d, mkb[:, :])]:
                Qld.issue(e, E, key_,
                          lambda a=ap_s, b=ap_d: e.dma_start(out=a, in_=b))
            tr.wait(e, E, "xnl16")
            Qld.issue(e, E, ("xl8", 0), lambda: e.dma_start(
                out=xl8b[0:1, 0:4096], in_=xsq16[0:4, :]))
            Qld.issue(e, E, ("xl8", 1), lambda: e.dma_start(
                out=xl8b[32:33, 0:6144], in_=xsq16[4:10, :]))
            Qld.issue(e, E, ("xl8", 2), lambda: e.dma_start(
                out=xl8b[64:65, 0:6144], in_=xsq16[10:16, :]))
            if PHASE < 1:
                if PROLOG >= 3:
                    tr.wait(e, E, ("w2e", 31, 1))
                elif PROLOG >= 2:
                    tr.wait(e, E, ("w1e", 7, 7))
                elif PROLOG >= 1:
                    tr.wait(e, E, ("b2e", 1))
                else:
                    tr.wait(e, E, "xnl16")
                return
            for t in range(2, NT1):
                if PHASE >= 2:
                    tr.wait_opt(e, E, ("p1x", t - 2, 1, 1))
                else:
                    tr.wait_opt(e, E, ("xn", t - 2, 1))
                Qld.issue(e, E, ("xld", t), lambda t=t: e.dma_start(
                    out=xF[:, t % 2, :, :],
                    in_=x_in[t * 256:(t + 1) * 256, :].rearrange(
                        "(s p) d -> p s d", p=128)))
            if PHASE < 3:
                tr.wait(e, E, ("xnTst", NT1 - 1), ("kst", NT1 - 1),
                        ("kred", NT1 - 1))
                if PHASE >= 2:
                    tr.wait(e, E, *[("p1st", t)
                                    for t in range(max(0, NT1 - 4), NT1)])
                    tr.wait(e, E, *[("xnTst", t)
                                    for t in range(max(0, NT1 - 4), NT1)])
                    tr.wait(e, E, *[("kst", t)
                                    for t in range(max(0, NT1 - 4), NT1)])
                return
            tr.wait(e, E, "ccin_v")
            Qld.issue(e, E, "ccin_d", lambda: e.dma_start(
                out=cc_in.rearrange("b p c -> p b c"),
                in_=ccin_s[:, :].rearrange("p (b c) -> p b c", b=2)))
            tr.wait(e, E, "cc_done")
            Qld.issue(e, E, "ccout_d", lambda: e.dma_start(
                out=ccout_s[:, :].rearrange("p (b c) -> p b c", b=2),
                in_=cc_out.rearrange("b p c -> p b c")))
            if PHASE < 4:
                tr.wait(e, E, "ccout_d")
                return
            for u in range(NT2):
                if u == 0:
                    tr.wait(e, E, ("fgrp", NT1 - 1, 3))
                else:
                    tr.wait(e, E, ("gmm", u - 1, 7))
                tr.wait(e, E, ("xnTst", 2 * u), ("xnTst", 2 * u + 1))
                Qld.issue(e, E, ("xnld", u), lambda u=u: e.dma_start(
                    out=xnT2v, in_=xnt_scr[u]))
                if u > 0:
                    tr.wait(e, E, ("prod", u - 1, 7))
                if u == 0:
                    tr.wait(e, E, ("fgrp", NT1 - 1, 7))
                tr.wait(e, E, ("kst", 2 * u), ("kst", 2 * u + 1))
                Qld.issue(e, E, ("kld", u), lambda u=u: e.dma_start(
                    out=k2v, in_=k_scr[u]))
                if u > 0:
                    tr.wait(e, E, ("ost", u - 1))
                else:
                    tr.wait(e, E, ("p1x", NT1 - 1, 1, 1))
                tr.wait(e, E, ("p1st", 2 * u), ("p1st", 2 * u + 1))
                Qld.issue(e, E, ("p1ld", u), lambda u=u: e.dma_start(
                    out=p1in, in_=p1_scr[u]))
            tr.wait(e, E, *[("ost", u)
                            for u in range(max(0, NT2 - 4), NT2)])

        def body_gpsimd(e):
            E = "g"
            Qwl.issue(e, E, "gbcol",
                      lambda: e.dma_start(out=gbcol[:, :], in_=gb_d))
            Qwl.issue(e, E, "b2bf",
                      lambda: e.dma_start(out=b2bf[:, :], in_=b2_d))
            tr.wait(e, E, "ld_xl")
            Qwl.issue(e, E, "wl_wk",
                      lambda: e.dma_start(out=wA[:, 65536:73728], in_=wk_d))
            if PHASE >= 1:
                tr.wait(e, E, ("xl8", 2), ("xld", 1))
            for dc in range(8):
                Qwl.issue(e, E, ("wl_w1", dc), lambda dc=dc: e.dma_start(
                    out=wA[:, dc * HF:(dc + 1) * HF],
                    in_=w1_d[:, dc * HF:(dc + 1) * HF]))
            for j in range(8):
                Qwl.issue(e, E, ("wl_w2", j), lambda j=j: e.dma_start(
                    out=wA[:, 32768 + j * HF: 32768 + (j + 1) * HF],
                    in_=w2_d[:, j * HF:(j + 1) * HF]))
            if PHASE < 3:
                return
            for t in range(NT1 if PHASE >= 1 else 0):
                pass
            if PHASE < 3:
                return
            tr.wait(e, E, "ccin_d")
            tr.inc(e.collective_compute(
                "AllReduce", ALU.add, replica_groups=[list(range(NCORES))],
                ins=[cc_in[:, :, :]], outs=[cc_out[:, :, :]]),
                "cc", 1, key="cc_done")
            if PHASE < 4:
                return
            tr.wait(e, E, ("kmm", NT1 - 1, 7))
            Qwl.issue(e, E, "wl_wq", lambda: e.dma_start(
                out=wA[:, 65536:73728], in_=wq_d))
            tr.wait(e, E, ("fgrp", NT1 - 1, 23))
            Qwl.issue(e, E, "wl_wv", lambda: e.dma_start(
                out=wA[:, 49152:57344], in_=wv_d))
            tr.wait(e, E, ("fgrp", NT1 - 1, 31))
            Qwl.issue(e, E, "wl_wg", lambda: e.dma_start(
                out=wA[:, 57344:65536], in_=wg_d))
            tr.wait(e, E, ("h", NT1 - 1, 31))
            Qwl.issue(e, E, "wl_wo", lambda: e.dma_start(
                out=wA[:, 0:8192], in_=wo_d))
            for u in range(NT2):

                for fc in range(8):
                    tr.wait(e, E, ("t3", u, fc), ("sgev", u, fc))
                    if u == 0 and fc < 2:
                        tr.wait(e, E, ("h", NT1 - 1, 31))
                    tr.wait_opt(e, E, ("womm", u - 1, 7))
                    tr.inc(e.tensor_tensor(
                        out=znb[:, fc, :], in0=t1p[:, fc % 2, :],
                        in1=sgbf[:, fc, :], op=ALU.mult),
                        "g", key=("zn", u, fc))

        def body_vector(e):
            E = "v"
            tr.wait(e, E, "ld_xl")
            tr.inc(e.tensor_tensor(out=xsq16, in0=xl16, in1=xl16,
                                   op=ALU.mult), "v", key=("pl", 0))
            tr.inc(e.reduce_sum(out=st16[:, 0:1], in_=xl16, axis=AX.X),
                   "v", key=("pl", 1))
            tr.wait(e, E, ("pl", 0))
            tr.inc(e.reduce_sum(out=st16[:, 1:2], in_=xsq16, axis=AX.X),
                   "v", key=("pl", 2))
            tr.wait(e, E, ("pl", 1), ("pl", 2))
            tr.inc(e.scalar_tensor_tensor(
                out=st16[:, 2:3], in0=st16[:, 0:1], scalar=RE(D),
                in1=st16[:, 0:1], op0=ALU.mult, op1=ALU.mult),
                "v", key=("pl", 3))
            tr.wait(e, E, ("pl", 3))
            tr.inc(e.tensor_sub(st16[:, 3:4], st16[:, 1:2], st16[:, 2:3]),
                   "v", key=("pl", 4))
            tr.wait(e, E, ("pl", 4))
            tr.inc(e.tensor_scalar_add(st16[:, 4:5], st16[:, 3:4], D * 1e-5),
                   "v", key=("pl", 5))
            tr.wait(e, E, ("pl", 5))
            tr.inc(e.reciprocal(st16[:, 5:6], st16[:, 4:5]),
                   "v", key=("pl", 6))
            tr.wait(e, E, "pl_sqrt")
            tr.inc(e.scalar_tensor_tensor(
                out=st16[:, 7:8], in0=st16[:, 0:1], scalar=-RE(D),
                in1=st16[:, 6:7], op0=ALU.mult, op1=ALU.mult),
                "v", key=("pl", 7))
            tr.wait(e, E, ("pl", 7))
            tr.inc(e.tensor_scalar(
                out=xsq16, in0=xl16, scalar1=st16[:, 6:7],
                scalar2=st16[:, 7:8], op0=ALU.mult, op1=ALU.add),
                "v", key="xnl16")
            tr.inc(e.memset(ksum_r[:, :], 0.0), "v", key="ksum0")
            for oh in range(2 if PROLOG >= 1 else 0):
                tr.wait(e, E, ("b2m", oh))
                tr.inc(e.tensor_copy(out=b2rep[:, oh * 512:(oh + 1) * 512],
                                     in_=pb[oh][:, :]),
                       "v", key=("b2e", oh))
            def v_folds():
                for dc in range(8 if PROLOG >= 2 else 0):
                    for hs in range(8):
                        tr.wait(e, E, ("w1m", dc, hs), ("wl_w1", dc), "c_sc")
                        sl_ = w1eff(dc, hs * 512, (hs + 1) * 512)
                        tr.inc(e.scalar_tensor_tensor(
                            out=sl_, in0=pb[4 + hs % 4][:, :],
                            scalar=scl[:, 0:1], in1=sl_,
                            op0=ALU.mult, op1=ALU.add),
                            "v", key=("w1e", dc, hs))
                for cc in range(32 if PROLOG >= 3 else 0):
                    pair = (4, 5) if cc % 2 == 0 else (6, 7)
                    for oh in range(2):
                        tr.wait(e, E, ("w2m", cc, oh), ("wl_w2", cc // 4),
                                "c_sc")
                        sl_ = w2eff(cc, oh * 512, (oh + 1) * 512)
                        tr.inc(e.scalar_tensor_tensor(
                            out=sl_, in0=pb[pair[oh]][:, :],
                            scalar=scl[:, 0:1], in1=sl_,
                            op0=ALU.mult, op1=ALU.add),
                            "v", key=("w2e", cc, oh))

            if PHASE < 1:
                v_folds()
            def ln_stats(t, xb):
                tr.wait(e, E, ("xld", t))
                tr.wait_opt(e, E, ("xn", t - 1, 1))
                tr.inc(e.reduce_sum(out=st1[:, 0:2], in_=xb, axis=AX.X),
                       "v", key=("mus", t))
                for s_ in range(2):
                    tr.wait(e, E, ("xsqs", t, s_))
                    tr.inc(e.reduce_sum(out=st1[:, 2 + s_:3 + s_],
                                        in_=xsq[:, s_, :],
                                        axis=AX.X), "v", key=("sqs", t, s_))
                tr.wait(e, E, ("mus", t), ("sqs", t, 1))
                tr.inc(e.scalar_tensor_tensor(
                    out=st1[:, 4:6], in0=st1[:, 0:2], scalar=RE(D),
                    in1=st1[:, 0:2], op0=ALU.mult, op1=ALU.mult),
                    "v", key=("lnb", t))
                tr.wait(e, E, ("lnb", t))
                tr.inc(e.tensor_sub(st1[:, 6:8], st1[:, 2:4], st1[:, 4:6]),
                       "v", key=("lnc", t))
                tr.wait(e, E, ("lnc", t))
                tr.inc(e.tensor_scalar_add(st1[:, 8:10], st1[:, 6:8],
                                           D * 1e-5), "v", key=("lne", t))
                tr.wait(e, E, ("lne", t))
                tr.inc(e.reciprocal(st1[:, 10:12], st1[:, 8:10]),
                       "v", key=("rec1", t))

            def ln_apply(t, xb):
                tr.wait(e, E, ("rstd1", t))
                tr.inc(e.scalar_tensor_tensor(
                    out=st1[:, 14:16], in0=st1[:, 0:2], scalar=-RE(D),
                    in1=st1[:, 12:14], op0=ALU.mult, op1=ALU.mult),
                    "v", key=("nmr1", t))
                for s in range(2):
                    tr.wait(e, E, ("nmr1", t))
                    tr.wait_opt(e, E, ("tT", t - 1, 2 * s + 1))
                    if t == 0:
                        tr.wait(e, E, "xnl16", ("b2m", 1))
                    tr.inc(e.tensor_scalar(
                        out=xn_tm[:, s, :], in0=xb[:, s, :],
                        scalar1=st1[:, 12 + s:13 + s],
                        scalar2=st1[:, 14 + s:15 + s],
                        op0=ALU.mult, op1=ALU.add),
                        "v", key=("xn", t, s))

            def ln_block(t, xb):
                ln_stats(t, xb)
                ln_apply(t, xb)

            def p1x_block(t):
                xb_ = xF[:, t % 2, :, :]
                for s in range(2 if PHASE >= 2 else 0):
                    for oh in range(2):
                        sl_ = p1t[:, s, oh * 512:(oh + 1) * 512]
                        tr.wait(e, E, ("b2e", 1))
                        tr.wait_opt(e, E, ("p1st", t - 1))
                        if t == 0:
                            tr.wait(e, E, "xnl16")
                        tr.inc(e.tensor_add(
                            sl_, xb_[:, s, oh * 512:(oh + 1) * 512],
                            b2rep[:, oh * 512:(oh + 1) * 512]),
                            "v", key=("p1x", t, s, oh))

            def p1_block(t):
                for s in range(2 if PHASE >= 2 else 0):
                    for oh in range(2):
                        sl_ = p1t[:, s, oh * 512:(oh + 1) * 512]
                        tr.wait(e, E, ("fgrp", t, 31), ("p1x", t, s, oh))
                        tr.inc(e.tensor_add(
                            sl_, sl_, pb[4 + s * 2 + oh][:, :]),
                            "v", key=("p1a", t, s, oh))

            for t in range(NT1 if PHASE >= 1 else 0):
                xb = xF[:, t % 2, :, :]
                if t == 0:
                    ln_block(0, xb)
                def k_evac(oc):
                    bank = pb[oc % 4]
                    tr.wait(e, E, ("kmm", t, oc), ("kex", t, oc))
                    tr.wait_opt(e, E, ("kred", t - 1), ("kst", t - 1))
                    tr.inc(e.tensor_scalar_max(kbf[:, oc, :],
                                               bank[:, 0:256], 0.0),
                           "v", key=("krl", t, oc))
                    tr.wait(e, E, ("krl", t, oc))
                    tr.inc(e.scalar_tensor_tensor(
                        out=kbf[:, oc, :], in0=ex1[:, oc % 2, :], scalar=1.0,
                        in1=kbf[:, oc, :], op0=ALU.min, op1=ALU.add),
                        "v", key=("kst3", t, oc))

                k_evac(0)
                k_evac(1)
                if t + 1 < NT1:
                    ln_stats(t + 1, xF[:, (t + 1) % 2, :, :])
                for oc in range(2, 8):
                    k_evac(oc)

                if t == 0 and PHASE >= 1:
                    v_folds()
                if t + 1 < NT1:
                    ln_apply(t + 1, xF[:, (t + 1) % 2, :, :])
                p1x_block(t)
                tr.wait(e, E, *[("kst3", t, oc) for oc in range(8)])
                tr.wait_opt(e, E, ("ksum", t - 1))
                tr.inc(e.reduce_sum(out=kredb[:, :], in_=kbf, axis=AX.X),
                       "v", key=("kred", t))
                tr.wait(e, E, "ksum0", ("kred", t))
                tr.inc(e.tensor_tensor(out=ksum_r[:, :], in0=ksum_r[:, :],
                                       in1=kredb[:, :], op=ALU.add),
                       "v", key=("ksum", t))
                if t + 1 >= NT1 and PHASE >= 3:
                    tr.wait(e, E, ("ksum", NT1 - 1), "c_mb")
                    tr.inc(e.tensor_scalar_mul(ccin_s[:, 0:8], ksum_r[:, :],
                                               mkb[:, 0:1]), "v",
                           key=("ccm", 0))
                    tr.wait(e, E, ("ccm", 0))
                    tr.inc(e.tensor_scalar_mul(ccin_s[:, 8:16], ksum_r[:, :],
                                               mkb[:, 1:2]), "v",
                           key="ccin_v")
                p1_block(t)
            if PHASE < 3:
                return
            tr.wait(e, E, ("fgrp", NT1 - 1, 31), ("kst", NT1 - 1),
                    ("kred", NT1 - 1))
            tr.inc(e.memset(rqk[:, :], 0.0), "v", key="rqk0")
            tr.inc(e.memset(mwr[:, :], 1.0), "v", key="mwr1")
            tr.wait(e, E, "ccout_d")
            tr.inc(e.tensor_scalar_mul(ksum_g[:, :], ccout_s[:, 0:8],
                                       mkb[:, 0:1]), "v", key=("ccg", 0))
            tr.wait(e, E, ("ccg", 0))
            tr.inc(e.scalar_tensor_tensor(
                out=ksum_g[:, :], in0=ccout_s[:, 8:16], scalar=mkb[:, 1:2],
                in1=ksum_g[:, :], op0=ALU.mult, op1=ALU.add),
                "v", key="ksumg")
            for fc in range(8):
                if fc == 0:
                    tr.wait(e, E, "ksumg", "c_sa")
                tr.inc(e.tensor_scalar_mul(
                    selAk[:, fc * 16:(fc + 1) * 16],
                    selA[:, fc * 16:(fc + 1) * 16], ksum_g[:, fc:fc + 1]),
                    "v", key=("sak", fc))
            def v_qrl(u, lo=0, hi=8):
                for oc in range(lo, hi):
                    tr.wait(e, E, ("qmm", u, oc), ("qex", u, oc))
                    if u == 0:
                        tr.wait(e, E, ("fgrp", NT1 - 1, 11))
                    tr.inc(e.tensor_scalar_max(qbf[:, oc, :],
                                               pb[oc % 2][:, :], 0.0),
                           "v", key=("qrl", u, oc))
                    tr.wait(e, E, ("qrl", u, oc), ("qex", u, oc))
                    tr.inc(e.scalar_tensor_tensor(
                        out=qbf[:, oc, :], in0=ex2[:, oc % 2, :], scalar=1.0,
                        in1=qbf[:, oc, :], op0=ALU.min, op1=ALU.add),
                        "v", key=("qst3", u, oc))

            for u in range(NT2 if PHASE >= 4 else 0):
                if u == 0:
                    v_qrl(0)
                for fc in range(8):
                    tr.wait(e, E, ("qst3", u, fc), ("kld", u))
                    tr.wait_opt(e, E, ("qkp", u, fc - 2),
                                ("sqp", u - 1, 6 + fc % 2),
                                ("tT", NT1 - 1, 3) if u == 0 else None)
                    tr.inc(e.tensor_tensor(
                        out=prodp[:, fc % 2, :], in0=qbf[:, fc, :],
                        in1=k2v[:, fc, :], op=ALU.mult),
                        "v", key=("prod", u, fc))
                for hh in range(2):
                    cs = slice(hh * 256, (hh + 1) * 256)
                    if hh == 0:
                        tr.wait(e, E, ("nmp", u, 7))
                        tr.wait_opt(e, E, ("rmul", u - 1, 1),
                                    ("p1st", NT1 - 1) if u == 0 else None)
                    tr.inc(e.tensor_scalar_add(recf[:, cs], pb[5][0:16, cs],
                                               1e-6),
                           "v", key=("rrec1", u, hh))
                    tr.wait(e, E, ("rrec1", u, hh))
                    tr.inc(e.reciprocal(recf[:, cs], recf[:, cs]),
                           "v", key=("rrec2", u, hh))
                    tr.wait(e, E, ("qkp", u, 7), ("rrec2", u, hh), "rqk0")
                    if hh == 0:
                        tr.wait_opt(e, E, ("rbc", u - 1, 7, 1))
                    tr.inc(e.tensor_tensor(out=rqk[0:16, cs],
                                           in0=pb[4][0:16, cs],
                                           in1=recf[:, cs], op=ALU.mult),
                           "v", key=("rmul", u, hh))
                for fc in range(8):
                    tr.wait(e, E, ("rbc", u, fc, 1), ("vev", u, fc))
                    tr.inc(e.tensor_tensor(
                        out=vbf[:, fc, :], in0=vbf[:, fc, :],
                        in1=pb[2 + fc % 2][:, :], op=ALU.mult),
                        "v", key=("zev", u, fc))
                tr.wait(e, E, ("smp", u, 7))
                tr.wait_opt(e, E, ("p1st", NT1 - 1) if u == 0 else
                            ("mwr0", u - 1))
                tr.inc(e.tensor_scalar_mul(stat2[0:1, 0:512], pb[6][0:1, :],
                                           RE(D)), "v", key=("mu", u))
                tr.wait(e, E, ("mu", u))
                tr.inc(e.tensor_tensor(out=stat2[0:1, 512:1024],
                                       in0=stat2[0:1, 0:512],
                                       in1=stat2[0:1, 0:512], op=ALU.mult),
                       "v", key=("musq", u))
                tr.wait(e, E, ("sqp", u, 7), ("musq", u))
                tr.inc(e.scalar_tensor_tensor(
                    out=stat2[0:1, 1024:1536], in0=pb[7][0:1, :],
                    scalar=RE(D), in1=stat2[0:1, 512:1024],
                    op0=ALU.mult, op1=ALU.subtract),
                    "v", key=("var", u))
                tr.wait(e, E, ("var", u))
                tr.inc(e.tensor_scalar_add(stat2[0:1, 1024:1536],
                                           stat2[0:1, 1024:1536], 1e-5),
                       "v", key=("vare", u))
                tr.wait(e, E, ("vare", u))
                if u == 0:
                    tr.wait(e, E, ("rstd1", NT1 - 1))
                tr.inc(e.reciprocal(stat2[0:1, 1536:2048],
                                    stat2[0:1, 1024:1536]),
                       "v", key=("rec2", u))
                tr.wait(e, E, ("rstds", u))
                if u == 0:
                    tr.wait(e, E, ("kst", NT1 - 1))
                tr.wait_opt(e, E, ("b1m", u - 1))
                tr.inc(e.tensor_copy(out=rstdb[0:1, :],
                                     in_=stat2[0:1, 512:1024]),
                       "v", key=("rstdbf", u))
                tr.wait(e, E, "mwr1")
                tr.wait_opt(e, E, ("b2m2", u - 1))
                tr.inc(e.scalar_tensor_tensor(
                    out=mwr[0:1, :], in0=stat2[0:1, 0:512], scalar=-1.0,
                    in1=stat2[0:1, 512:1024], op0=ALU.mult, op1=ALU.mult),
                    "v", key=("mwr0", u))
                for fc in range(8):
                    tr.wait(e, E, ("b1m", u), ("zev", u, fc))
                    if u == 0:
                        tr.wait(e, E, ("xnTst", NT1 - 1))
                    tr.wait_opt(e, E, ("zn", u, fc - 2))
                    tr.inc(e.tensor_tensor(
                        out=t1p[:, fc % 2, :], in0=vbf[:, fc, :],
                        in1=pb[6][:, :], op=ALU.mult),
                        "v", key=("t1", u, fc))
                    tr.wait(e, E, ("b2m2", u), ("t1", u, fc))
                    tr.inc(e.tensor_tensor(
                        out=t1p[:, fc % 2, :], in0=t1p[:, fc % 2, :],
                        in1=pb[7][:, :], op=ALU.add),
                        "v", key=("t2", u, fc))

                if u + 1 < NT2:
                    v_qrl(u + 1, 0, 2)
                for g in range(8):
                    s, oh = g // 2, g % 2
                    tr.wait(e, E, ("womm", u, g), ("p1ld", u))
                    tr.inc(e.tensor_add(
                        p1in[:, s, oh * 512:(oh + 1) * 512],
                        p1in[:, s, oh * 512:(oh + 1) * 512],
                        pb[(2, 3, 4, 5)[g % 4]][:, :]),
                        "v", key=("oev", u, g))
                if u + 1 < NT2:
                    v_qrl(u + 1, 2, 8)

        def body_scalar(e):
            E = "s"
            tr.wait(e, E, ("pl", 6))
            tr.inc(e.activation(out=st16[:, 6:7], in_=st16[:, 5:6],
                                func=AF.Sqrt, scale=float(D)),
                   "s", key="pl_sqrt")
            def rstd_block(t):
                tr.wait(e, E, ("rec1", t))
                tr.inc(e.activation(out=st1[:, 12:14], in_=st1[:, 10:12],
                                    func=AF.Sqrt, scale=float(D)),
                       "s", key=("rstd1", t))

            for t in range(NT1 if PHASE >= 1 else 0):
                xb = xF[:, t % 2, :, :]
                if t == 0:
                    for s_ in range(2):
                        tr.wait(e, E, ("xld", 0))
                        if s_ == 0:
                            tr.wait(e, E, ("xl8", 0), ("xl8", 1),
                                    ("xl8", 2))
                        else:
                            tr.wait(e, E, ("sqs", 0, 0))
                        tr.inc(e.activation(out=xsq[:, s_, :],
                                            in_=xb[:, s_, :],
                                            func=AF.Square),
                               "s", key=("xsqs", 0, s_))
                    rstd_block(0)
                for g in range(4):
                    s_, half = g // 2, g % 2
                    tr.wait(e, E, ("tT", t, g))
                    tr.wait_opt(e, E, ("h", t - 1, 31), ("xnTst", t - 1))
                    dst = xnT1[:, half * 4:(half + 1) * 4,
                               s_ * 128:(s_ + 1) * 128]
                    tr.inc(e.activation(
                        out=dst,
                        in_=pb[g % 2][:, :].rearrange("p (q n) -> p q n",
                                                      q=4),
                        func=AF.Identity),
                        "s", key=("tTe", t, g))
                tr.wait(e, E, *[("tTe", t, g) for g in range(4)])
                Qst.issue(e, E, ("xnTst", t), lambda t=t: e.dma_start(
                    out=xnt_scr[t // 2, :, :,
                                (t % 2) * 256:(t % 2) * 256 + 256],
                    in_=xnT1))
                for oc in range(8):
                    tr.wait(e, E, ("kmm", t, oc))
                    tr.wait_opt(e, E, ("kst3", t, oc - 2))
                    tr.inc(e.activation(out=ex1[:, oc % 2, :],
                                        in_=pb[oc % 4][:, 0:256],
                                        func=AF.Exp),
                           "s", key=("kex", t, oc))
                    if t + 1 < NT1 and oc in (0, 2):
                        s_ = oc // 2
                        tr.wait(e, E, ("xld", t + 1))
                        tr.wait_opt(e, E, ("sqs", t, s_))
                        tr.inc(e.activation(out=xsq[:, s_, :],
                                            in_=xF[:, (t + 1) % 2, s_, :],
                                            func=AF.Square),
                               "s", key=("xsqs", t + 1, s_))
                if t > 0 and PHASE >= 2:
                    tr.wait(e, E, *[("p1a", t - 1, s_, oh)
                                    for s_ in range(2) for oh in range(2)])
                    Qst.issue(e, E, ("p1st", t - 1),
                              lambda t=t: e.dma_start(
                        out=p1_scr[(t - 1) // 2, :,
                                   ((t - 1) % 2) * 2:((t - 1) % 2) * 2 + 2,
                                   :],
                        in_=p1t))
                if t + 1 < NT1:
                    rstd_block(t + 1)
                tr.wait(e, E, *[("kst3", t, oc) for oc in range(8)])
                Qst.issue(e, E, ("kst", t), lambda t=t: e.dma_start(
                    out=k_scr[t // 2, :, :,
                              (t % 2) * 256:(t % 2) * 256 + 256],
                    in_=kbf))
                for cc in range(32 if PHASE >= 2 else 0):
                    tr.wait(e, E, ("h", t, cc), "c_b1")
                    tr.wait_opt(e, E, ("fgrp", t, cc - 2))
                    tr.inc(e.activation(out=hc[:, cc % 2, :],
                                        in_=pb[2 + cc % 2][:, 0:256],
                                        func=AF.Gelu, bias=b1c[:, cc:cc + 1]),
                           "s", key=("ge", t, cc))

                if PHASE >= 2 and t == NT1 - 1:
                    tr.wait(e, E, *[("p1a", t, s, oh)
                                    for s in range(2) for oh in range(2)])
                    Qst.issue(e, E, ("p1st", t), lambda t=t: e.dma_start(
                        out=p1_scr[t // 2, :,
                                   (t % 2) * 2:(t % 2) * 2 + 2, :],
                        in_=p1t))
            def ost_block(u):
                tr.wait(e, E, *[("oev", u, g) for g in range(8)])
                Qst.issue(e, E, ("ost", u), lambda u=u: e.dma_start(
                    out=out_d[u * 512:(u + 1) * 512, :].rearrange(
                        "(s p) d -> p s d", p=128),
                    in_=p1in))

            for u in range(NT2 if PHASE >= 4 else 0):
                for oc in range(8):
                    tr.wait(e, E, ("qmm", u, oc))
                    if u == 0 and oc < 2:
                        tr.wait(e, E, ("kst", NT1 - 1))
                    tr.wait_opt(e, E, ("qst3", u, oc - 2))
                    tr.inc(e.activation(out=ex2[:, oc % 2, :],
                                        in_=pb[oc % 2][:, :], func=AF.Exp),
                           "s", key=("qex", u, oc))
                if u > 0:
                    ost_block(u - 1)
                for oc in range(8):
                    tr.wait(e, E, ("vmm", u, oc))
                    if u == 0:
                        tr.wait(e, E, ("fgrp", NT1 - 1, 15))
                    tr.wait_opt(e, E, ("smp", u - 1, oc), ("t1", u - 1, oc))
                    tr.inc(e.activation(out=vbf[:, oc, :],
                                        in_=pb[2 + oc % 2][:, :],
                                        func=AF.Identity),
                           "s", key=("vev", u, oc))
                for oc in range(8):
                    tr.wait(e, E, ("gmm", u, oc))
                    if u == 0:
                        tr.wait(e, E, ("h", NT1 - 1, 31))
                    else:
                        tr.wait_opt(e, E, ("zn", u - 1, oc))
                    tr.inc(e.activation(out=sgbf[:, oc, :],
                                        in_=pb[oc % 2][:, :], func=AF.Silu),
                           "s", key=("sgev", u, oc))
                for fc in range(8):
                    tr.wait(e, E, ("zev", u, fc))
                    tr.wait_opt(e, E, ("qkp", u, 7), ("sqp", u, fc - 2))
                    tr.inc(e.activation(out=prodp[:, fc % 2, :],
                                        in_=vbf[:, fc, :], func=AF.Square),
                           "s", key=("zsq", u, fc))
                tr.wait(e, E, ("rec2", u), ("var", u))
                tr.inc(e.activation(out=stat2[0:1, 512:1024],
                                    in_=stat2[0:1, 1536:2048], func=AF.Sqrt),
                       "s", key=("rstds", u))
                for fc in range(8):
                    tr.wait(e, E, ("t2", u, fc), "gbcol")
                    tr.wait_opt(e, E, ("zn", u, fc - 2))
                    tr.inc(e.activation(out=t1p[:, fc % 2, :],
                                        in_=t1p[:, fc % 2, :],
                                        func=AF.Identity,
                                        bias=gbcol[:, 8 + fc:9 + fc],
                                        scale=gbcol[:, fc:fc + 1]),
                           "s", key=("t3", u, fc))
                if u == NT2 - 1:
                    ost_block(u)

        def body_tensor(e):
            E = "t"
            for oh in range(2 if PROLOG >= 1 else 0):
                tr.wait(e, E, "b2bf", "c_or")
                tr.minc(e.matmul(pb[oh][:, :], onesr[0:1, :],
                                b2bf[0:1, oh * 512:(oh + 1) * 512],
                                start=True, stop=True),
                       "t", key=("b2m", oh))
            def w1fold():
              for dc in range(8 if PROLOG >= 2 else 0):
                for hs in range(8):
                    i, rsl = hs // 2, hs % 2
                    tr.wait(e, E, ("xl8", 0), ("xl8", 1), ("xl8", 2))
                    if hs >= 4:
                        tr.wait(e, E, ("w1e", dc, hs - 4))
                    elif dc > 0:
                        tr.wait(e, E, ("w1e", dc - 1, hs + 4))
                    tr.minc(e.matmul(
                        pb[4 + hs % 4][:, :],
                        xl8(2 * i + 1, dc * 128, (dc + 1) * 128),
                        xl8(2 * i, rsl * 512, (rsl + 1) * 512),
                        start=True, stop=True),
                        "t", key=("w1m", dc, hs))
            def w2fold():
              for cc in range(32 if PROLOG >= 3 else 0):
                j, c0 = cc // 8, (cc % 8) * 128
                pair = (4, 5) if cc % 2 == 0 else (6, 7)
                for i in range(4):
                    bank = pair[i // 2]
                    off = (i % 2) * 256
                    tr.wait(e, E, ("xl8", 0), ("xl8", 1), ("xl8", 2))
                    if cc < 2:
                        tr.wait(e, E, ("w1e", 7, bank - 4), ("w1e", 7, bank))
                    else:
                        tr.wait(e, E, ("w2e", cc - 2, i // 2))
                    rhs = xl8(2 * i + 8, j * 256, (j + 1) * 256)
                    tr.minc(e.matmul(
                        pb[bank][:, off:off + 256],
                        xl8(2 * i + 9, c0, c0 + 128), rhs,
                        start=True, stop=True),
                        "t", key=("w2m", cc, i // 2) if i % 2 == 1 else None)
            if PHASE < 1:
                w1fold()
                w2fold()
            for t in range(NT1 if PHASE >= 1 else 0):
                for g in range(4):
                    s, half = g // 2, g % 2
                    tr.wait(e, E, ("xn", t, s), "c_id")
                    if g >= 2:
                        tr.wait(e, E, ("tTe", t, g - 2))
                    elif t > 0:
                        tr.wait(e, E, ("krl", t - 1, 4 + g),
                                ("kex", t - 1, 4 + g))
                    else:
                        tr.wait(e, E, ("b2e", g))
                    for q in range(4):
                        fc = half * 4 + q
                        tr.minc(e.matmul(
                            pb[g % 2][:, q * 128:(q + 1) * 128],
                            xn_tm[:, s, fc * 128:(fc + 1) * 128],
                            ident[:, :], start=True, stop=True),
                            "t", key=("tT", t, g) if q == 3 else None)
                for oc in range(8):
                    bank = pb[oc % 4]
                    tr.wait(e, E, ("tTe", t, 3), "wl_wk")
                    if oc < 2:
                        tr.wait(e, E, ("tTe", t, 2 + oc))
                    elif oc < 4:
                        tr.wait_opt(e, E, ("ge", t - 1, 28 + oc)
                                    if PHASE >= 2 else ("krl", t - 1, 4 + oc))
                    else:
                        tr.wait(e, E, ("krl", t, oc - 4), ("kex", t, oc - 4))
                    for dc in range(8):
                        tr.minc(e.matmul(
                            bank[:, 0:256], wk_s(dc, oc), xnT1[:, dc, :],
                            start=(dc == 0), stop=(dc == 7)),
                            "t", key=("kmm", t, oc) if dc == 7 else None)
                if t == 0:
                    w1fold()
                    w2fold()
                for cc in range(33 if PHASE >= 2 else 0):
                    if cc < 32:
                        tr.wait(e, E, ("w1e", 7, 7))
                        if cc < 2:
                            tr.wait(e, E, ("krl", t, 6 + cc),
                                    ("kex", t, 6 + cc))
                        else:
                            tr.wait(e, E, ("ge", t, cc - 2))
                        for dc in range(8):
                            tr.minc(e.matmul(
                                pb[2 + cc % 2][:, 0:256],
                                w1eff(dc, cc * 128, (cc + 1) * 128),
                                xnT1[:, dc, :],
                                start=(dc == 0), stop=(dc == 7)),
                                "t", key=("h", t, cc) if dc == 7 else None)
                    if cc >= 1:
                        ccw = cc - 1
                        tr.wait(e, E, ("ge", t, ccw), ("w2e", 31, 1))
                        if ccw == 0:
                            if t == 0:
                                tr.wait(e, E, ("w2e", 30, 0), ("w2e", 30, 1),
                                        ("w2e", 31, 0), ("w2e", 31, 1))
                            else:
                                tr.wait(e, E, *[("p1a", t - 1, s, oh)
                                                for s in range(2)
                                                for oh in range(2)])
                        nmm = 0
                        for s in range(2):
                            for oh in range(2):
                                nmm += 1
                                tr.minc(e.matmul(
                                    pb[4 + s * 2 + oh][:, :],
                                    hc[:, ccw % 2, s * 128:(s + 1) * 128],
                                    w2eff(ccw, oh * 512, (oh + 1) * 512),
                                    start=(ccw == 0), stop=(ccw == 31)),
                                    "t", key=("fgrp", t, ccw)
                                    if nmm == 4 else None)
            def s2_q(u, lo=0, hi=8):
                for oc in range(lo, hi):
                    tr.wait(e, E, ("xnld", u), "wl_wq")
                    if oc < 2:
                        if u == 0:
                            tr.wait(e, E, ("tTe", NT1 - 1, 2 + oc))
                        else:
                            tr.wait(e, E, ("sgev", u - 1, 6 + oc))
                    else:
                        tr.wait(e, E, ("qrl", u, oc - 2), ("qex", u, oc - 2))
                    for dc in range(8):
                        tr.minc(e.matmul(
                            pb[oc % 2][:, :], wq_s(dc, oc), xnT2v[:, dc, :],
                            start=(dc == 0), stop=(dc == 7)),
                            "t", key=("qmm", u, oc) if dc == 7 else None)

            def s2_v(u):
                for oc in range(8):
                    tr.wait(e, E, "wl_wv")
                    if oc < 2:
                        if u == 0:
                            tr.wait(e, E, ("ge", NT1 - 1, 30 + oc))
                        else:
                            tr.wait(e, E, ("oev", u - 1, 4 + oc))
                    else:
                        tr.wait(e, E, ("vev", u, oc - 2))
                    for dc in range(8):
                        tr.minc(e.matmul(
                            pb[2 + oc % 2][:, :], wv_s(dc, oc),
                            xnT2v[:, dc, :],
                            start=(dc == 0), stop=(dc == 7)),
                            "t", key=("vmm", u, oc) if dc == 7 else None)

            def s2_g(u):
                for oc in range(8):
                    tr.wait(e, E, "wl_wg")
                    if oc < 2:
                        tr.wait(e, E, ("qrl", u, 6 + oc), ("qex", u, 6 + oc))
                    else:
                        tr.wait(e, E, ("sgev", u, oc - 2))
                    for dc in range(8):
                        tr.minc(e.matmul(
                            pb[oc % 2][:, :], wg_s(dc, oc), xnT2v[:, dc, :],
                            start=(dc == 0), stop=(dc == 7)),
                            "t", key=("gmm", u, oc) if dc == 7 else None)

            for u in range(NT2 if PHASE >= 4 else 0):
                if u == 0:
                    s2_q(0)
                    s2_v(0)
                    s2_g(0)
                for fc in range(8):
                    tr.wait(e, E, ("prod", u, fc), "c_sa")
                    if fc == 0:
                        if u == 0:
                            tr.wait(e, E, ("p1a", NT1 - 1, 0, 0))
                        else:
                            tr.wait(e, E, ("oev", u - 1, 6))
                    tr.inc(e.matmul(
                        pb[4][0:16, :], selA[:, fc * 16:(fc + 1) * 16],
                        prodp[:, fc % 2, :],
                        start=(fc == 0), stop=(fc == 7)),
                        "t", key=("qkp", u, fc))
                    tr.wait(e, E, ("qst3", u, fc))
                    if fc == 0:
                        if u == 0:
                            tr.wait(e, E, ("p1a", NT1 - 1, 0, 1), ("sak", 7))
                        else:
                            tr.wait(e, E, ("oev", u - 1, 7))
                    tr.inc(e.matmul(
                        pb[5][0:16, :], selAk[:, fc * 16:(fc + 1) * 16],
                        qbf[:, fc, :],
                        start=(fc == 0), stop=(fc == 7)),
                        "t", key=("nmp", u, fc))
                for fc in range(8):
                    for hh in range(2):
                        cs = slice(hh * 256, (hh + 1) * 256)
                        tr.wait(e, E, ("rmul", u, hh), "c_sb")
                        if hh == 0:
                            if fc < 2:
                                tr.wait(e, E, ("vev", u, 6 + fc))
                            else:
                                tr.wait(e, E, ("zev", u, fc - 2))
                        tr.minc(e.matmul(
                            pb[2 + fc % 2][:, cs],
                            selB[:, fc * 128:(fc + 1) * 128], rqk[:, cs],
                            start=True, stop=True),
                            "t", key=("rbc", u, fc, hh)
                            if hh == 1 else None)
                for fc in range(8):
                    tr.wait(e, E, ("zev", u, fc), "c_o1")
                    if fc == 0:
                        if u == 0:
                            tr.wait(e, E, ("p1a", NT1 - 1, 1, 0))
                        else:
                            tr.wait(e, E, ("t1", u - 1, 7))
                    tr.inc(e.matmul(
                        pb[6][0:1, :], ones1[:, :], vbf[:, fc, :],
                        start=(fc == 0), stop=(fc == 7)),
                        "t", key=("smp", u, fc))
                    tr.wait(e, E, ("zsq", u, fc))
                    if fc == 0:
                        if u == 0:
                            tr.wait(e, E, ("p1a", NT1 - 1, 1, 1))
                        else:
                            tr.wait(e, E, ("t2", u - 1, 7))
                    tr.inc(e.matmul(
                        pb[7][0:1, :], ones1[:, :], prodp[:, fc % 2, :],
                        start=(fc == 0), stop=(fc == 7)),
                        "t", key=("sqp", u, fc))
                if u + 1 < NT2:
                    s2_q(u + 1, 0, 2)
                tr.wait(e, E, ("rstdbf", u), ("mu", u), "c_or")
                tr.inc(e.matmul(pb[6][:, :], onesr[0:1, :], rstdb[0:1, :],
                                start=True, stop=True),
                       "t", key=("b1m", u))
                tr.wait(e, E, ("mwr0", u), ("var", u))
                tr.inc(e.matmul(pb[7][:, :], onesr[0:1, :], mwr[0:1, :],
                                start=True, stop=True),
                       "t", key=("b2m2", u))
                for half in range(2):
                    for fc in range(8):
                        for gi in range(4):
                            g = half * 4 + gi
                            s, oh = g // 2, g % 2
                            if fc == 0:
                                if g == 0:
                                    tr.wait(e, E, "wl_wo")
                                if half == 0:
                                    if g < 2:
                                        tr.wait(e, E, ("zev", u, 6 + g))
                                    elif g == 2:
                                        tr.wait(e, E, ("rmul", u, 1))
                                    else:
                                        tr.wait(e, E, ("rrec1", u, 1))
                                else:
                                    tr.wait(e, E, ("oev", u, g - 4))
                            if half == 0 and gi == 0:
                                tr.wait(e, E, ("zn", u, fc))
                            tr.minc(e.matmul(
                                pb[(2, 3, 4, 5)[gi]][:, :],
                                znb[:, fc, s * 128:(s + 1) * 128],
                                wo_s(fc, oh * 512, (oh + 1) * 512),
                                start=(fc == 0), stop=(fc == 7)),
                                "t", key=("womm", u, g) if fc == 7 else None)
                if u + 1 < NT2:
                    s2_q(u + 1, 2, 8)
                    s2_v(u + 1)
                    s2_g(u + 1)

        bodies = [("sync", body_sync), ("g", body_gpsimd),
                  ("v", body_vector), ("s", body_scalar),
                  ("t", body_tensor)]

        fake = _FakeEng()
        for _, b in bodies:
            b(fake)
        tr.begin_emit()
        block = ctx.enter_context(nc.Block())
        block.sync(body_sync)
        block.gpsimd(body_gpsimd)
        block.vector(body_vector)
        block.scalar(body_scalar)
        block.tensor(body_tensor)
        tr.check()
    return nc


def _rearr_w(w):
    M = w.shape[1]
    return np.ascontiguousarray(
        w.reshape(8, 128, M).transpose(1, 0, 2).reshape(128, 8 * M))


def _rearr_w32(w):
    M = w.shape[1]
    return np.ascontiguousarray(
        w.reshape(32, 128, M).transpose(1, 0, 2).reshape(128, 32 * M))


_E4PERM = (np.arange(1024).reshape(256, 4).T.reshape(-1))


def xl16_host(lt):
    rows = []
    for lti in [0, 1, 6, 7, 2, 3, 8, 9, 10, 11, 4, 5, 12, 13, 14, 15]:
        r = lt[lti]
        if lti >= 8 and lti % 2 == 0:
            r = r[_E4PERM]
        rows.append(r)
    return np.ascontiguousarray(np.stack(rows))


def _consts():
    ident = np.eye(128, dtype=ml_dtypes.bfloat16)
    selA = np.zeros((128, 8 * 16), dtype=ml_dtypes.bfloat16)
    selB = np.zeros((32, 8 * 128), dtype=ml_dtypes.bfloat16)
    for fc in range(8):
        for p in range(128):
            h = 2 * fc + p // 64
            selA[p, fc * 16 + h] = 1
            selB[h, fc * 128 + p] = 1
    ones1 = np.ones((128, 1), dtype=ml_dtypes.bfloat16)
    onesr = np.ones((1, 128), dtype=ml_dtypes.bfloat16)
    return ident, selA, selB, ones1, onesr


def prepare(L0_attn_hidden_states, L0_attn_Wq, L0_attn_Wk, L0_attn_Wv,
            L0_attn_Wg, L0_attn_Wo, L0_attn_gamma, L0_attn_beta,
            L0_ffn_W1, L0_ffn_b1, L0_ffn_W2, L0_ffn_b2, L0_ffn_scale):
    x = np.asarray(L0_attn_hidden_states)
    B, S, _ = x.shape
    t_core = B * S // NCORES
    cores_per_b = NCORES // B

    wq = _rearr_w(np.asarray(L0_attn_Wq))
    wk = _rearr_w(np.asarray(L0_attn_Wk))
    wv = _rearr_w(np.asarray(L0_attn_Wv))
    wg = _rearr_w(np.asarray(L0_attn_Wg))
    wo = _rearr_w(np.asarray(L0_attn_Wo))
    w1s = _rearr_w(np.ascontiguousarray(np.asarray(L0_ffn_W1).T))
    w2s = _rearr_w32(np.ascontiguousarray(np.asarray(L0_ffn_W2).T))
    b1c = np.ascontiguousarray(np.asarray(L0_ffn_b1).reshape(32, 128).T)
    b2r = np.ascontiguousarray(np.asarray(L0_ffn_b2).reshape(1, D))
    gbcol = np.ascontiguousarray(np.concatenate(
        [np.asarray(L0_attn_gamma).reshape(8, 128).T,
         np.asarray(L0_attn_beta).reshape(8, 128).T], axis=1))
    scl = np.ascontiguousarray(
        np.broadcast_to(np.asarray(L0_ffn_scale).reshape(1, 1), (128, 1))
    ).astype(np.float32)
    ident, selA, selB, ones1, onesr = _consts()

    nc = build_program(t_core)
    in_maps = []
    for c in range(NCORES):
        b = c // cores_per_b
        q = c % cores_per_b
        mkb = np.zeros((128, 2), dtype=np.float32)
        mkb[:, b] = 1.0
        in_maps.append(dict(
            x_in=np.ascontiguousarray(x[b, q * t_core:(q + 1) * t_core]),
            xl16=xl16_host(x[b, 1:17]),
            wq=wq, wk=wk, wv=wv, wg=wg, wo=wo, w1s=w1s, w2s=w2s,
            b1c=b1c, b2r=b2r, gbcol=gbcol, scl=scl, mkb=mkb,
            idn=ident, selA=selA, selB=selB, ones128=ones1, onesrow=onesr,
        ))
    return nc, in_maps


def assemble(results, B, S):
    t_core = B * S // NCORES
    cores_per_b = NCORES // B
    out = np.empty((B, S, D), dtype=np.float32)
    for c in range(NCORES):
        b = c // cores_per_b
        q = c % cores_per_b
        out[b, q * t_core:(q + 1) * t_core] = results[c]["out"]
    return out


def kernel(L0_attn_hidden_states, **kw):
    x = np.asarray(L0_attn_hidden_states)
    nc, in_maps = prepare(L0_attn_hidden_states, **kw)
    res = run_bass_kernel_spmd(nc, in_maps, core_ids=list(range(NCORES)))
    return assemble(res.results, x.shape[0], x.shape[1])


# version 21
# speedup vs baseline: 1.0780x; 1.0058x over previous
import os
import numpy as np
import ml_dtypes
import concourse.bass as bass
import concourse.mybir as mybir
from concourse.bass_utils import run_bass_kernel_spmd
from contextlib import ExitStack

F32 = mybir.dt.float32
BF16 = mybir.dt.bfloat16
AF = mybir.ActivationFunctionType
ALU = mybir.AluOpType
AX = mybir.AxisListType

D = 1024
HF = 4096
NCORES = 8
PHASE = int(os.environ.get("KPHASE", "4"))
PROLOG = int(os.environ.get("KPROLOG", "3"))


class _FakeInstr:
    def then_inc(self, sem, n=1):
        return self


class _FakeEng:
    def __getattr__(self, name):
        return lambda *a, **k: _FakeInstr()


class _Tr:

    def __init__(self, sems):
        self.sems = sems
        self.plan = True
        self.count = {k: 0 for k in sems}
        self.evt = {}
        self.waited = {}

    def begin_emit(self):
        assert self.plan
        self._plan_count = dict(self.count)
        self.plan = False
        self.count = {k: 0 for k in self.sems}
        self.waited = {}
        for q in getattr(self, "queues", []):
            q.cnt = 0

    def wait_val(self, eng, ename, sem, val):
        if self.plan or val <= 0:
            return
        w = self.waited.setdefault(ename, {})
        if w.get(sem, 0) >= val:
            return
        eng.wait_ge(self.sems[sem], val)
        w[sem] = val

    def check(self):
        assert self.count == self._plan_count, (self.count, self._plan_count)

    def inc(self, instr, sem, n=1, key=None):
        self.count[sem] += n
        tok = (sem, self.count[sem])
        if self.plan:
            if key is not None:
                assert key not in self.evt, key
                self.evt[key] = tok
        else:
            instr.then_inc(self.sems[sem], n)
            if key is not None:
                assert self.evt[key] == tok, (key, self.evt[key], tok)
        return tok

    def minc(self, instr, sem, key=None):
        if key is None:
            return None
        return self.inc(instr, sem, 1, key=key)

    def wait(self, eng, ename, *keys):
        if self.plan:
            return
        w = self.waited.setdefault(ename, {})
        for key in keys:
            if key is None:
                continue
            sem, val = self.evt[key]
            if val <= 0 or w.get(sem, 0) >= val:
                continue
            eng.wait_ge(self.sems[sem], val)
            w[sem] = val

    def wait_opt(self, eng, ename, *keys):
        if self.plan:
            return
        self.wait(eng, ename, *[k for k in keys if k in self.evt])


class _DmaQ:

    def __init__(self, tr, base, n):
        self.tr, self.n = tr, n
        self.names = [f"{base}{i}" for i in range(n)]
        self.cnt = 0
        tr.queues = getattr(tr, "queues", []) + [self]

    def issue(self, eng, ename, key, fn):
        i = self.cnt
        self.cnt += 1
        nm = self.names[i % self.n]
        self.tr.wait_val(eng, ename, nm, 16 * (i // self.n))
        self.tr.inc(fn(), nm, 16, key=key)


def build_program(t_core):
    NT1 = t_core // 256
    NT2 = t_core // 512
    assert NT1 * 256 == t_core and NT2 * 512 == t_core

    nc = bass.Bass("TRN2", target_bir_lowering=False)

    x_in = nc.dram_tensor("x_in", [t_core, D], F32, kind="ExternalInput").ap()
    xl16_d = nc.dram_tensor("xl16", [16, D], F32, kind="ExternalInput").ap()
    wq_d = nc.dram_tensor("wq", [128, 8 * D], F32, kind="ExternalInput").ap()
    wk_d = nc.dram_tensor("wk", [128, 8 * D], F32, kind="ExternalInput").ap()
    wv_d = nc.dram_tensor("wv", [128, 8 * D], F32, kind="ExternalInput").ap()
    wg_d = nc.dram_tensor("wg", [128, 8 * D], F32, kind="ExternalInput").ap()
    wo_d = nc.dram_tensor("wo", [128, 8 * D], F32, kind="ExternalInput").ap()
    w1_d = nc.dram_tensor("w1s", [128, 8 * HF], F32, kind="ExternalInput").ap()
    w2_d = nc.dram_tensor("w2s", [128, 32 * D], F32, kind="ExternalInput").ap()
    b1_d = nc.dram_tensor("b1c", [128, 32], F32, kind="ExternalInput").ap()
    b2_d = nc.dram_tensor("b2r", [1, D], F32, kind="ExternalInput").ap()
    gb_d = nc.dram_tensor("gbcol", [128, 16], F32, kind="ExternalInput").ap()
    sc_d = nc.dram_tensor("scl", [128, 1], F32, kind="ExternalInput").ap()
    mb_d = nc.dram_tensor("mkb", [128, 2], F32, kind="ExternalInput").ap()
    id_d = nc.dram_tensor("idn", [128, 128], BF16, kind="ExternalInput").ap()
    sa_d = nc.dram_tensor("selA", [128, 8 * 16], BF16, kind="ExternalInput").ap()
    sb_d = nc.dram_tensor("selB", [32, 8 * 128], BF16, kind="ExternalInput").ap()
    o1_d = nc.dram_tensor("ones128", [128, 1], BF16, kind="ExternalInput").ap()
    or_d = nc.dram_tensor("onesrow", [1, 128], BF16, kind="ExternalInput").ap()

    out_d = nc.dram_tensor("out", [t_core, D], F32, kind="ExternalOutput").ap()

    xnt_scr = nc.dram_tensor("xnt_scr", [NT2, 128, 8, 512], BF16).ap()
    k_scr = nc.dram_tensor("k_scr", [NT2, 128, 8, 512], BF16).ap()
    p1_scr = nc.dram_tensor("p1_scr", [NT2, 128, 4, D], F32).ap()
    cc_in = nc.dram_tensor("cc_in", [2, 128, 8], F32).ap()
    cc_out = nc.dram_tensor("cc_out", [2, 128, 8], F32).ap()

    ctx = ExitStack()
    with ctx:
        def sbuf(name, shape, dtype):
            return ctx.enter_context(nc.sbuf_tensor(name, shape, dtype))

        wA = sbuf("wA", [128, 73728], BF16)

        def w1eff(dc, lo, hi):
            return wA[:, dc * HF + lo: dc * HF + hi]

        def w2eff(cc, lo, hi):
            return wA[:, 32768 + cc * D + lo: 32768 + cc * D + hi]

        def wk_s(dc, oc):
            b = 65536 + dc * D + oc * 128
            return wA[:, b: b + 128]

        def wq_s(dc, oc):
            b = 65536 + dc * D + oc * 128
            return wA[:, b: b + 128]

        def wv_s(dc, oc):
            b = 49152 + dc * D + oc * 128
            return wA[:, b: b + 128]

        def wg_s(dc, oc):
            b = 57344 + dc * D + oc * 128
            return wA[:, b: b + 128]

        def wo_s(fc, lo, hi):
            return wA[:, fc * D + lo: fc * D + hi]

        xnT2v = wA[:, 32768:36864].rearrange("p (c n) -> p c n", c=8)
        k2v = wA[:, 36864:40960].rearrange("p (c n) -> p c n", c=8)
        qbf = wA[:, 40960:45056].rearrange("p (c n) -> p c n", c=8)
        vbf = wA[:, 45056:49152].rearrange("p (c n) -> p c n", c=8)
        sgbf = wA[:, 8192:12288].rearrange("p (c n) -> p c n", c=8)
        znb = wA[:, 12288:16384].rearrange("p (c n) -> p c n", c=8)

        aB = sbuf("aB", [128, 9728], BF16)
        xn_tm = aB[:, 0:2048].rearrange("p (s d) -> p s d", s=2)
        xnT1 = aB[:, 2048:4096].rearrange("p (c n) -> p c n", c=8)
        hc = aB[:, 4096:4608].rearrange("p (b n) -> p b n", b=2)
        kbf = aB[:, 4608:6656].rearrange("p (c n) -> p c n", c=8)
        mn1 = aB[:, 6656:7168].rearrange("p (b n) -> p b n", b=2)
        ex1 = aB[:, 9216:9728].rearrange("p (b n) -> p b n", b=2)
        xsq = aB[:, 7168:9216].rearrange("p (s d) -> p s d", s=2)
        prodp = aB[:, 0:1024].rearrange("p (b n) -> p b n", b=2)
        nprodp = aB[:, 1024:2048].rearrange("p (b n) -> p b n", b=2)
        t1p = aB[:, 2048:3072].rearrange("p (b n) -> p b n", b=2)
        mn2 = aB[:, 3072:4096].rearrange("p (b n) -> p b n", b=2)
        ex2 = aB[:, 5632:6656].rearrange("p (b n) -> p b n", b=2)
        xsq16 = aB[0:16, 7168:8192]

        aF = sbuf("aF", [128, 6656], F32)
        xF = aF[:, 0:4096].rearrange("p (b s d) -> p b s d", b=2, s=2)
        p1t = aF[:, 4096:6144].rearrange("p (s d) -> p s d", s=2)

        st1 = aF[:, 6144:6160]
        p1in = aF[:, 0:4096].rearrange("p (s d) -> p s d", s=4)
        recf = aF[0:16, 4096:4608]
        stat2 = aF[0:1, 4608:6656]
        xl16 = aF[0:16, 4096:5120]

        st16 = sbuf("st16_s", [16, 8], F32)
        xl8b = sbuf("xl8b_s", [128, 6144], BF16)
        b2bf = aB[0:1, 0:1024]
        rqk = aB[0:32, 4096:4608]
        mwr = aB[0:2, 4608:5120]
        rstdb = aB[0:1, 5120:5632]
        ident = sbuf("ident_s", [128, 128], BF16)
        selA = sbuf("selA_s", [128, 8 * 16], BF16)
        selB = sbuf("selB_s", [32, 8 * 128], BF16)
        ones1 = sbuf("ones1_s", [128, 1], BF16)
        onesr = sbuf("onesr_s", [1, 128], BF16)
        b1c = sbuf("b1c_s", [128, 32], F32)
        b2rep = sbuf("b2rep_s", [128, D], BF16)
        gbcol = sbuf("gbcol_s", [128, 16], F32)
        scl = sbuf("scl_s", [128, 1], F32)
        mkb = sbuf("mkb_s", [128, 2], F32)
        ksum_r = sbuf("ksum_r_s", [128, 8], F32)
        kredb = sbuf("kredb_s", [128, 8], F32)
        ccin_s = sbuf("ccin_s_s", [128, 16], F32)
        ccout_s = sbuf("ccout_s_s", [128, 16], F32)
        ksum_g = sbuf("ksum_g_s", [128, 8], F32)
        selAk = sbuf("selAk_s", [128, 8 * 16], BF16)

        pb = [ctx.enter_context(nc.psum_tensor(f"pb{i}", [128, 512], F32))
              for i in range(8)]

        sems = {}
        for s_ in (["cc", "v", "s", "t", "g"]
                   + [f"ld{i}" for i in range(4)]
                   + [f"st{i}" for i in range(4)]
                   + [f"wl{i}" for i in range(2)]):
            sems[s_] = ctx.enter_context(nc.semaphore("sem_" + s_))
        tr = _Tr(sems)
        Qld = _DmaQ(tr, "ld", 4)
        Qst = _DmaQ(tr, "st", 4)
        Qwl = _DmaQ(tr, "wl", 2)

        _PAIRMAP = {0: (0, 0), 1: (32, 0), 2: (64, 0), 3: (0, 1),
                    4: (32, 1), 5: (32, 2), 6: (64, 1), 7: (64, 2)}

        def xl8(j, lo, hi):
            pair = (j // 2) if j < 8 else (4 + (j - 8) // 2)
            base, blk = _PAIRMAP[pair]
            c0 = blk * 2048 + (j % 2) * 1024
            return xl8b[base:base + 1, c0 + lo: c0 + hi]

        RE = lambda i: 1.0 / i

        def body_sync(e):
            E = "sync"
            Qld.issue(e, E, "ld_xl",
                      lambda: e.dma_start(out=xl16, in_=xl16_d))
            for t in range(min(2, NT1) if PHASE >= 1 else 0):
                Qld.issue(e, E, ("xld", t), lambda t=t: e.dma_start(
                    out=xF[:, t % 2, :, :],
                    in_=x_in[t * 256:(t + 1) * 256, :].rearrange(
                        "(s p) d -> p s d", p=128)))
            for key_, ap_d, ap_s in [
                    ("c_id", id_d, ident[:, :]), ("c_sa", sa_d, selA[:, :]),
                    ("c_sb", sb_d, selB[:, :]), ("c_o1", o1_d, ones1[:, :]),
                    ("c_or", or_d, onesr[:, :]), ("c_b1", b1_d, b1c[:, :]),
                    ("c_sc", sc_d, scl[:, :]), ("c_mb", mb_d, mkb[:, :])]:
                Qld.issue(e, E, key_,
                          lambda a=ap_s, b=ap_d: e.dma_start(out=a, in_=b))
            tr.wait(e, E, "xnl16")
            Qld.issue(e, E, ("xl8", 0), lambda: e.dma_start(
                out=xl8b[0:1, 0:4096], in_=xsq16[0:4, :]))
            Qld.issue(e, E, ("xl8", 1), lambda: e.dma_start(
                out=xl8b[32:33, 0:6144], in_=xsq16[4:10, :]))
            Qld.issue(e, E, ("xl8", 2), lambda: e.dma_start(
                out=xl8b[64:65, 0:6144], in_=xsq16[10:16, :]))
            if PHASE < 1:
                if PROLOG >= 3:
                    tr.wait(e, E, ("w2e", 31, 1))
                elif PROLOG >= 2:
                    tr.wait(e, E, ("w1e", 7, 7))
                elif PROLOG >= 1:
                    tr.wait(e, E, ("b2e", 1))
                else:
                    tr.wait(e, E, "xnl16")
                return
            for t in range(2, NT1):
                if PHASE >= 2:
                    tr.wait_opt(e, E, ("p1x", t - 2, 1, 1))
                else:
                    tr.wait_opt(e, E, ("xn", t - 2, 1))
                Qld.issue(e, E, ("xld", t), lambda t=t: e.dma_start(
                    out=xF[:, t % 2, :, :],
                    in_=x_in[t * 256:(t + 1) * 256, :].rearrange(
                        "(s p) d -> p s d", p=128)))
            if PHASE < 3:
                tr.wait(e, E, ("xnTst", NT1 - 1), ("kst", NT1 - 1),
                        ("kred", NT1 - 1))
                if PHASE >= 2:
                    tr.wait(e, E, *[("p1st", t)
                                    for t in range(max(0, NT1 - 4), NT1)])
                    tr.wait(e, E, *[("xnTst", t)
                                    for t in range(max(0, NT1 - 4), NT1)])
                    tr.wait(e, E, *[("kst", t)
                                    for t in range(max(0, NT1 - 4), NT1)])
                return
            tr.wait(e, E, "ccin_v")
            Qld.issue(e, E, "ccin_d", lambda: e.dma_start(
                out=cc_in.rearrange("b p c -> p b c"),
                in_=ccin_s[:, :].rearrange("p (b c) -> p b c", b=2)))
            tr.wait(e, E, "cc_done")
            Qld.issue(e, E, "ccout_d", lambda: e.dma_start(
                out=ccout_s[:, :].rearrange("p (b c) -> p b c", b=2),
                in_=cc_out.rearrange("b p c -> p b c")))
            if PHASE < 4:
                tr.wait(e, E, "ccout_d")
                return
            for u in range(NT2):
                if u == 0:
                    tr.wait(e, E, ("fgrp", NT1 - 1, 3))
                else:
                    tr.wait(e, E, ("gmm", u - 1, 7))
                tr.wait(e, E, ("xnTst", 2 * u), ("xnTst", 2 * u + 1))
                Qld.issue(e, E, ("xnld", u), lambda u=u: e.dma_start(
                    out=xnT2v, in_=xnt_scr[u]))
                if u > 0:
                    tr.wait(e, E, ("prod", u - 1, 5), ("prodg", u - 1, 7))
                if u == 0:
                    tr.wait(e, E, ("fgrp", NT1 - 1, 7))
                tr.wait(e, E, ("kst", 2 * u), ("kst", 2 * u + 1))
                Qld.issue(e, E, ("kld", u), lambda u=u: e.dma_start(
                    out=k2v, in_=k_scr[u]))
                if u > 0:
                    tr.wait(e, E, ("ost", u - 1))
                else:
                    tr.wait(e, E, ("p1x", NT1 - 1, 1, 1))
                tr.wait(e, E, ("p1st", 2 * u), ("p1st", 2 * u + 1))
                Qld.issue(e, E, ("p1ld", u), lambda u=u: e.dma_start(
                    out=p1in, in_=p1_scr[u]))
            tr.wait(e, E, *[("ost", u)
                            for u in range(max(0, NT2 - 4), NT2)])

        def body_gpsimd(e):
            E = "g"
            Qwl.issue(e, E, "gbcol",
                      lambda: e.dma_start(out=gbcol[:, :], in_=gb_d))
            Qwl.issue(e, E, "b2bf",
                      lambda: e.dma_start(out=b2bf[:, :], in_=b2_d))
            tr.wait(e, E, "ld_xl")
            Qwl.issue(e, E, "wl_wk",
                      lambda: e.dma_start(out=wA[:, 65536:73728], in_=wk_d))
            if PHASE >= 1:
                tr.wait(e, E, ("xl8", 2), ("xld", 1))
            for dc in range(8):
                Qwl.issue(e, E, ("wl_w1", dc), lambda dc=dc: e.dma_start(
                    out=wA[:, dc * HF:(dc + 1) * HF],
                    in_=w1_d[:, dc * HF:(dc + 1) * HF]))
            for j in range(8):
                Qwl.issue(e, E, ("wl_w2", j), lambda j=j: e.dma_start(
                    out=wA[:, 32768 + j * HF: 32768 + (j + 1) * HF],
                    in_=w2_d[:, j * HF:(j + 1) * HF]))
            if PHASE < 3:
                return
            for t in range(NT1 if PHASE >= 1 else 0):
                pass
            if PHASE < 3:
                return
            tr.wait(e, E, "ccin_d")
            tr.inc(e.collective_compute(
                "AllReduce", ALU.add, replica_groups=[list(range(NCORES))],
                ins=[cc_in[:, :, :]], outs=[cc_out[:, :, :]]),
                "cc", 1, key="cc_done")
            if PHASE < 4:
                return
            tr.wait(e, E, ("kmm", NT1 - 1, 7))
            Qwl.issue(e, E, "wl_wq", lambda: e.dma_start(
                out=wA[:, 65536:73728], in_=wq_d))
            tr.wait(e, E, ("fgrp", NT1 - 1, 23))
            Qwl.issue(e, E, "wl_wv", lambda: e.dma_start(
                out=wA[:, 49152:57344], in_=wv_d))
            tr.wait(e, E, ("fgrp", NT1 - 1, 31))
            Qwl.issue(e, E, "wl_wg", lambda: e.dma_start(
                out=wA[:, 57344:65536], in_=wg_d))
            tr.wait(e, E, ("h", NT1 - 1, 31))
            Qwl.issue(e, E, "wl_wo", lambda: e.dma_start(
                out=wA[:, 0:8192], in_=wo_d))
            for u in range(NT2):
                for fc in (6, 7):
                    tr.wait(e, E, ("qst3", u, fc), ("kld", u))
                    tr.wait_opt(e, E, ("qkp", u - 1, fc))
                    if u == 0:
                        tr.wait(e, E, ("tT", NT1 - 1, 3), ("h", NT1 - 1, 31))
                    tr.inc(e.tensor_tensor(
                        out=nprodp[:, fc % 2, :], in0=qbf[:, fc, :],
                        in1=k2v[:, fc, :], op=ALU.mult),
                        "g", key=("prodg", u, fc))

                for fc in range(8):
                    tr.wait(e, E, ("t3", u, fc), ("sgev", u, fc))
                    if u == 0 and fc < 2:
                        tr.wait(e, E, ("h", NT1 - 1, 31))
                    tr.wait_opt(e, E, ("womm", u - 1, 7))
                    tr.inc(e.tensor_tensor(
                        out=znb[:, fc, :], in0=t1p[:, fc % 2, :],
                        in1=sgbf[:, fc, :], op=ALU.mult),
                        "g", key=("zn", u, fc))

        def body_vector(e):
            E = "v"
            tr.wait(e, E, "ld_xl")
            tr.inc(e.tensor_tensor(out=xsq16, in0=xl16, in1=xl16,
                                   op=ALU.mult), "v", key=("pl", 0))
            tr.inc(e.reduce_sum(out=st16[:, 0:1], in_=xl16, axis=AX.X),
                   "v", key=("pl", 1))
            tr.wait(e, E, ("pl", 0))
            tr.inc(e.reduce_sum(out=st16[:, 1:2], in_=xsq16, axis=AX.X),
                   "v", key=("pl", 2))
            tr.wait(e, E, ("pl", 1), ("pl", 2))
            tr.inc(e.scalar_tensor_tensor(
                out=st16[:, 2:3], in0=st16[:, 0:1], scalar=RE(D),
                in1=st16[:, 0:1], op0=ALU.mult, op1=ALU.mult),
                "v", key=("pl", 3))
            tr.wait(e, E, ("pl", 3))
            tr.inc(e.tensor_sub(st16[:, 3:4], st16[:, 1:2], st16[:, 2:3]),
                   "v", key=("pl", 4))
            tr.wait(e, E, ("pl", 4))
            tr.inc(e.tensor_scalar_add(st16[:, 4:5], st16[:, 3:4], D * 1e-5),
                   "v", key=("pl", 5))
            tr.wait(e, E, ("pl", 5))
            tr.inc(e.reciprocal(st16[:, 5:6], st16[:, 4:5]),
                   "v", key=("pl", 6))
            tr.wait(e, E, "pl_sqrt")
            tr.inc(e.scalar_tensor_tensor(
                out=st16[:, 7:8], in0=st16[:, 0:1], scalar=-RE(D),
                in1=st16[:, 6:7], op0=ALU.mult, op1=ALU.mult),
                "v", key=("pl", 7))
            tr.wait(e, E, ("pl", 7))
            tr.inc(e.tensor_scalar(
                out=xsq16, in0=xl16, scalar1=st16[:, 6:7],
                scalar2=st16[:, 7:8], op0=ALU.mult, op1=ALU.add),
                "v", key="xnl16")
            tr.inc(e.memset(ksum_r[:, :], 0.0), "v", key="ksum0")
            for oh in range(2 if PROLOG >= 1 else 0):
                tr.wait(e, E, ("b2m", oh))
                tr.inc(e.tensor_copy(out=b2rep[:, oh * 512:(oh + 1) * 512],
                                     in_=pb[oh][:, :]),
                       "v", key=("b2e", oh))
            def v_folds():
                for dc in range(8 if PROLOG >= 2 else 0):
                    for hs in range(8):
                        tr.wait(e, E, ("w1m", dc, hs), ("wl_w1", dc), "c_sc")
                        sl_ = w1eff(dc, hs * 512, (hs + 1) * 512)
                        tr.inc(e.scalar_tensor_tensor(
                            out=sl_, in0=pb[4 + hs % 4][:, :],
                            scalar=scl[:, 0:1], in1=sl_,
                            op0=ALU.mult, op1=ALU.add),
                            "v", key=("w1e", dc, hs))
                for cc in range(32 if PROLOG >= 3 else 0):
                    pair = (4, 5) if cc % 2 == 0 else (6, 7)
                    for oh in range(2):
                        tr.wait(e, E, ("w2m", cc, oh), ("wl_w2", cc // 4),
                                "c_sc")
                        sl_ = w2eff(cc, oh * 512, (oh + 1) * 512)
                        tr.inc(e.scalar_tensor_tensor(
                            out=sl_, in0=pb[pair[oh]][:, :],
                            scalar=scl[:, 0:1], in1=sl_,
                            op0=ALU.mult, op1=ALU.add),
                            "v", key=("w2e", cc, oh))

            if PHASE < 1:
                v_folds()
            def ln_stats(t, xb):
                tr.wait(e, E, ("xld", t))
                tr.wait_opt(e, E, ("xn", t - 1, 1))
                tr.inc(e.reduce_sum(out=st1[:, 0:2], in_=xb, axis=AX.X),
                       "v", key=("mus", t))
                for s_ in range(2):
                    tr.wait(e, E, ("xsqs", t, s_))
                    tr.inc(e.reduce_sum(out=st1[:, 2 + s_:3 + s_],
                                        in_=xsq[:, s_, :],
                                        axis=AX.X), "v", key=("sqs", t, s_))
                tr.wait(e, E, ("mus", t), ("sqs", t, 1))
                tr.inc(e.scalar_tensor_tensor(
                    out=st1[:, 4:6], in0=st1[:, 0:2], scalar=RE(D),
                    in1=st1[:, 0:2], op0=ALU.mult, op1=ALU.mult),
                    "v", key=("lnb", t))
                tr.wait(e, E, ("lnb", t))
                tr.inc(e.tensor_sub(st1[:, 6:8], st1[:, 2:4], st1[:, 4:6]),
                       "v", key=("lnc", t))
                tr.wait(e, E, ("lnc", t))
                tr.inc(e.tensor_scalar_add(st1[:, 8:10], st1[:, 6:8],
                                           D * 1e-5), "v", key=("lne", t))
                tr.wait(e, E, ("lne", t))
                tr.inc(e.reciprocal(st1[:, 10:12], st1[:, 8:10]),
                       "v", key=("rec1", t))

            def ln_apply(t, xb):
                tr.wait(e, E, ("rstd1", t))
                tr.inc(e.scalar_tensor_tensor(
                    out=st1[:, 14:16], in0=st1[:, 0:2], scalar=-RE(D),
                    in1=st1[:, 12:14], op0=ALU.mult, op1=ALU.mult),
                    "v", key=("nmr1", t))
                for s in range(2):
                    tr.wait(e, E, ("nmr1", t))
                    tr.wait_opt(e, E, ("tT", t - 1, 2 * s + 1))
                    if t == 0:
                        tr.wait(e, E, "xnl16", ("b2m", 1))
                    tr.inc(e.tensor_scalar(
                        out=xn_tm[:, s, :], in0=xb[:, s, :],
                        scalar1=st1[:, 12 + s:13 + s],
                        scalar2=st1[:, 14 + s:15 + s],
                        op0=ALU.mult, op1=ALU.add),
                        "v", key=("xn", t, s))

            def ln_block(t, xb):
                ln_stats(t, xb)
                ln_apply(t, xb)

            def p1x_block(t):
                xb_ = xF[:, t % 2, :, :]
                for s in range(2 if PHASE >= 2 else 0):
                    for oh in range(2):
                        sl_ = p1t[:, s, oh * 512:(oh + 1) * 512]
                        tr.wait(e, E, ("b2e", 1))
                        tr.wait_opt(e, E, ("p1st", t - 1))
                        if t == 0:
                            tr.wait(e, E, "xnl16")
                        tr.inc(e.tensor_add(
                            sl_, xb_[:, s, oh * 512:(oh + 1) * 512],
                            b2rep[:, oh * 512:(oh + 1) * 512]),
                            "v", key=("p1x", t, s, oh))

            def p1_block(t):
                for s in range(2 if PHASE >= 2 else 0):
                    for oh in range(2):
                        sl_ = p1t[:, s, oh * 512:(oh + 1) * 512]
                        tr.wait(e, E, ("fgrp", t, 31), ("p1x", t, s, oh))
                        tr.inc(e.tensor_add(
                            sl_, sl_, pb[4 + s * 2 + oh][:, :]),
                            "v", key=("p1a", t, s, oh))

            for t in range(NT1 if PHASE >= 1 else 0):
                xb = xF[:, t % 2, :, :]
                if t == 0:
                    ln_block(0, xb)
                def k_evac(oc):
                    bank = pb[oc % 4]
                    tr.wait(e, E, ("kmm", t, oc), ("kex", t, oc))
                    tr.wait_opt(e, E, ("kred", t - 1), ("kst", t - 1))
                    tr.inc(e.tensor_scalar_max(kbf[:, oc, :],
                                               bank[:, 0:256], 0.0),
                           "v", key=("krl", t, oc))
                    tr.wait(e, E, ("krl", t, oc))
                    tr.inc(e.scalar_tensor_tensor(
                        out=kbf[:, oc, :], in0=ex1[:, oc % 2, :], scalar=1.0,
                        in1=kbf[:, oc, :], op0=ALU.min, op1=ALU.add),
                        "v", key=("kst3", t, oc))

                k_evac(0)
                k_evac(1)
                if t + 1 < NT1:
                    ln_stats(t + 1, xF[:, (t + 1) % 2, :, :])
                for oc in range(2, 8):
                    k_evac(oc)

                if t == 0 and PHASE >= 1:
                    v_folds()
                if t + 1 < NT1:
                    ln_apply(t + 1, xF[:, (t + 1) % 2, :, :])
                p1x_block(t)
                tr.wait(e, E, *[("kst3", t, oc) for oc in range(8)])
                tr.wait_opt(e, E, ("ksum", t - 1))
                tr.inc(e.reduce_sum(out=kredb[:, :], in_=kbf, axis=AX.X),
                       "v", key=("kred", t))
                tr.wait(e, E, "ksum0", ("kred", t))
                tr.inc(e.tensor_tensor(out=ksum_r[:, :], in0=ksum_r[:, :],
                                       in1=kredb[:, :], op=ALU.add),
                       "v", key=("ksum", t))
                if t + 1 >= NT1 and PHASE >= 3:
                    tr.wait(e, E, ("ksum", NT1 - 1), "c_mb")
                    tr.inc(e.tensor_scalar_mul(ccin_s[:, 0:8], ksum_r[:, :],
                                               mkb[:, 0:1]), "v",
                           key=("ccm", 0))
                    tr.wait(e, E, ("ccm", 0))
                    tr.inc(e.tensor_scalar_mul(ccin_s[:, 8:16], ksum_r[:, :],
                                               mkb[:, 1:2]), "v",
                           key="ccin_v")
                p1_block(t)
            if PHASE < 3:
                return
            tr.wait(e, E, ("fgrp", NT1 - 1, 31), ("kst", NT1 - 1),
                    ("kred", NT1 - 1))
            tr.inc(e.memset(rqk[:, :], 0.0), "v", key="rqk0")
            tr.inc(e.memset(mwr[:, :], 1.0), "v", key="mwr1")
            tr.wait(e, E, "ccout_d")
            tr.inc(e.tensor_scalar_mul(ksum_g[:, :], ccout_s[:, 0:8],
                                       mkb[:, 0:1]), "v", key=("ccg", 0))
            tr.wait(e, E, ("ccg", 0))
            tr.inc(e.scalar_tensor_tensor(
                out=ksum_g[:, :], in0=ccout_s[:, 8:16], scalar=mkb[:, 1:2],
                in1=ksum_g[:, :], op0=ALU.mult, op1=ALU.add),
                "v", key="ksumg")
            for fc in range(8):
                if fc == 0:
                    tr.wait(e, E, "ksumg", "c_sa")
                tr.inc(e.tensor_scalar_mul(
                    selAk[:, fc * 16:(fc + 1) * 16],
                    selA[:, fc * 16:(fc + 1) * 16], ksum_g[:, fc:fc + 1]),
                    "v", key=("sak", fc))
            def v_qrl(u, lo=0, hi=8):
                for oc in range(lo, hi):
                    tr.wait(e, E, ("qmm", u, oc), ("qex", u, oc))
                    if u == 0:
                        tr.wait(e, E, ("fgrp", NT1 - 1, 11))
                    tr.inc(e.tensor_scalar_max(qbf[:, oc, :],
                                               pb[oc % 2][:, :], 0.0),
                           "v", key=("qrl", u, oc))
                    tr.wait(e, E, ("qrl", u, oc), ("qex", u, oc))
                    tr.inc(e.scalar_tensor_tensor(
                        out=qbf[:, oc, :], in0=ex2[:, oc % 2, :], scalar=1.0,
                        in1=qbf[:, oc, :], op0=ALU.min, op1=ALU.add),
                        "v", key=("qst3", u, oc))

            for u in range(NT2 if PHASE >= 4 else 0):
                if u == 0:
                    v_qrl(0)
                for fc in range(6):
                    tr.wait(e, E, ("qst3", u, fc), ("kld", u))
                    tr.wait_opt(e, E, ("qkp", u, fc - 2),
                                ("sqp", u - 1, 6 + fc % 2),
                                ("tT", NT1 - 1, 3) if u == 0 else None)
                    tr.inc(e.tensor_tensor(
                        out=prodp[:, fc % 2, :], in0=qbf[:, fc, :],
                        in1=k2v[:, fc, :], op=ALU.mult),
                        "v", key=("prod", u, fc))
                for hh in range(2):
                    cs = slice(hh * 256, (hh + 1) * 256)
                    if hh == 0:
                        tr.wait(e, E, ("nmp", u, 7))
                        tr.wait_opt(e, E, ("rmul", u - 1, 1),
                                    ("p1st", NT1 - 1) if u == 0 else None)
                    tr.inc(e.tensor_scalar_add(recf[:, cs], pb[5][0:16, cs],
                                               1e-6),
                           "v", key=("rrec1", u, hh))
                    tr.wait(e, E, ("rrec1", u, hh))
                    tr.inc(e.reciprocal(recf[:, cs], recf[:, cs]),
                           "v", key=("rrec2", u, hh))
                    tr.wait(e, E, ("qkp", u, 7), ("rrec2", u, hh), "rqk0")
                    if hh == 0:
                        tr.wait_opt(e, E, ("rbc", u - 1, 7, 1))
                    tr.inc(e.tensor_tensor(out=rqk[0:16, cs],
                                           in0=pb[4][0:16, cs],
                                           in1=recf[:, cs], op=ALU.mult),
                           "v", key=("rmul", u, hh))
                for fc in range(8):
                    tr.wait(e, E, ("rbc", u, fc, 1), ("vev", u, fc))
                    tr.inc(e.tensor_tensor(
                        out=vbf[:, fc, :], in0=vbf[:, fc, :],
                        in1=pb[2 + fc % 2][:, :], op=ALU.mult),
                        "v", key=("zev", u, fc))
                tr.wait(e, E, ("smp", u, 7))
                tr.wait_opt(e, E, ("p1st", NT1 - 1) if u == 0 else
                            ("mwr0", u - 1))
                tr.inc(e.tensor_scalar_mul(stat2[0:1, 0:512], pb[6][0:1, :],
                                           RE(D)), "v", key=("mu", u))
                tr.wait(e, E, ("mu", u))
                tr.inc(e.tensor_tensor(out=stat2[0:1, 512:1024],
                                       in0=stat2[0:1, 0:512],
                                       in1=stat2[0:1, 0:512], op=ALU.mult),
                       "v", key=("musq", u))
                tr.wait(e, E, ("sqp", u, 7), ("musq", u))
                tr.inc(e.scalar_tensor_tensor(
                    out=stat2[0:1, 1024:1536], in0=pb[7][0:1, :],
                    scalar=RE(D), in1=stat2[0:1, 512:1024],
                    op0=ALU.mult, op1=ALU.subtract),
                    "v", key=("var", u))
                tr.wait(e, E, ("var", u))
                tr.inc(e.tensor_scalar_add(stat2[0:1, 1024:1536],
                                           stat2[0:1, 1024:1536], 1e-5),
                       "v", key=("vare", u))
                tr.wait(e, E, ("vare", u))
                if u == 0:
                    tr.wait(e, E, ("rstd1", NT1 - 1))
                tr.inc(e.reciprocal(stat2[0:1, 1536:2048],
                                    stat2[0:1, 1024:1536]),
                       "v", key=("rec2", u))
                tr.wait(e, E, ("rstds", u))
                if u == 0:
                    tr.wait(e, E, ("kst", NT1 - 1))
                tr.wait_opt(e, E, ("b1m", u - 1))
                tr.inc(e.tensor_copy(out=rstdb[0:1, :],
                                     in_=stat2[0:1, 512:1024]),
                       "v", key=("rstdbf", u))
                tr.wait(e, E, "mwr1")
                tr.wait_opt(e, E, ("b2m2", u - 1))
                tr.inc(e.scalar_tensor_tensor(
                    out=mwr[0:1, :], in0=stat2[0:1, 0:512], scalar=-1.0,
                    in1=stat2[0:1, 512:1024], op0=ALU.mult, op1=ALU.mult),
                    "v", key=("mwr0", u))
                for fc in range(8):
                    tr.wait(e, E, ("b1m", u), ("zev", u, fc))
                    if u == 0:
                        tr.wait(e, E, ("xnTst", NT1 - 1))
                    tr.wait_opt(e, E, ("zn", u, fc - 2))
                    tr.inc(e.tensor_tensor(
                        out=t1p[:, fc % 2, :], in0=vbf[:, fc, :],
                        in1=pb[6][:, :], op=ALU.mult),
                        "v", key=("t1", u, fc))
                    tr.wait(e, E, ("b2m2", u), ("t1", u, fc))
                    tr.inc(e.tensor_tensor(
                        out=t1p[:, fc % 2, :], in0=t1p[:, fc % 2, :],
                        in1=pb[7][:, :], op=ALU.add),
                        "v", key=("t2", u, fc))

                if u + 1 < NT2:
                    v_qrl(u + 1, 0, 2)
                for g in range(8):
                    s, oh = g // 2, g % 2
                    tr.wait(e, E, ("womm", u, g), ("p1ld", u))
                    tr.inc(e.tensor_add(
                        p1in[:, s, oh * 512:(oh + 1) * 512],
                        p1in[:, s, oh * 512:(oh + 1) * 512],
                        pb[(2, 3, 4, 5)[g % 4]][:, :]),
                        "v", key=("oev", u, g))
                if u + 1 < NT2:
                    v_qrl(u + 1, 2, 8)

        def body_scalar(e):
            E = "s"
            tr.wait(e, E, ("pl", 6))
            tr.inc(e.activation(out=st16[:, 6:7], in_=st16[:, 5:6],
                                func=AF.Sqrt, scale=float(D)),
                   "s", key="pl_sqrt")
            def rstd_block(t):
                tr.wait(e, E, ("rec1", t))
                tr.inc(e.activation(out=st1[:, 12:14], in_=st1[:, 10:12],
                                    func=AF.Sqrt, scale=float(D)),
                       "s", key=("rstd1", t))

            for t in range(NT1 if PHASE >= 1 else 0):
                xb = xF[:, t % 2, :, :]
                if t == 0:
                    for s_ in range(2):
                        tr.wait(e, E, ("xld", 0))
                        if s_ == 0:
                            tr.wait(e, E, ("xl8", 0), ("xl8", 1),
                                    ("xl8", 2))
                        else:
                            tr.wait(e, E, ("sqs", 0, 0))
                        tr.inc(e.activation(out=xsq[:, s_, :],
                                            in_=xb[:, s_, :],
                                            func=AF.Square),
                               "s", key=("xsqs", 0, s_))
                    rstd_block(0)
                for g in range(4):
                    s_, half = g // 2, g % 2
                    tr.wait(e, E, ("tT", t, g))
                    tr.wait_opt(e, E, ("h", t - 1, 31), ("xnTst", t - 1))
                    dst = xnT1[:, half * 4:(half + 1) * 4,
                               s_ * 128:(s_ + 1) * 128]
                    tr.inc(e.activation(
                        out=dst,
                        in_=pb[g % 2][:, :].rearrange("p (q n) -> p q n",
                                                      q=4),
                        func=AF.Identity),
                        "s", key=("tTe", t, g))
                tr.wait(e, E, *[("tTe", t, g) for g in range(4)])
                Qst.issue(e, E, ("xnTst", t), lambda t=t: e.dma_start(
                    out=xnt_scr[t // 2, :, :,
                                (t % 2) * 256:(t % 2) * 256 + 256],
                    in_=xnT1))
                for oc in range(8):
                    tr.wait(e, E, ("kmm", t, oc))
                    tr.wait_opt(e, E, ("kst3", t, oc - 2))
                    tr.inc(e.activation(out=ex1[:, oc % 2, :],
                                        in_=pb[oc % 4][:, 0:256],
                                        func=AF.Exp),
                           "s", key=("kex", t, oc))
                    if t + 1 < NT1 and oc in (0, 2):
                        s_ = oc // 2
                        tr.wait(e, E, ("xld", t + 1))
                        tr.wait_opt(e, E, ("sqs", t, s_))
                        tr.inc(e.activation(out=xsq[:, s_, :],
                                            in_=xF[:, (t + 1) % 2, s_, :],
                                            func=AF.Square),
                               "s", key=("xsqs", t + 1, s_))
                if t > 0 and PHASE >= 2:
                    tr.wait(e, E, *[("p1a", t - 1, s_, oh)
                                    for s_ in range(2) for oh in range(2)])
                    Qst.issue(e, E, ("p1st", t - 1),
                              lambda t=t: e.dma_start(
                        out=p1_scr[(t - 1) // 2, :,
                                   ((t - 1) % 2) * 2:((t - 1) % 2) * 2 + 2,
                                   :],
                        in_=p1t))
                tr.wait(e, E, *[("kst3", t, oc) for oc in range(8)])
                Qst.issue(e, E, ("kst", t), lambda t=t: e.dma_start(
                    out=k_scr[t // 2, :, :,
                              (t % 2) * 256:(t % 2) * 256 + 256],
                    in_=kbf))
                for cc in range(32 if PHASE >= 2 else 0):
                    tr.wait(e, E, ("h", t, cc), "c_b1")
                    tr.wait_opt(e, E, ("fgrp", t, cc - 2))
                    tr.inc(e.activation(out=hc[:, cc % 2, :],
                                        in_=pb[2 + cc % 2][:, 0:256],
                                        func=AF.Gelu, bias=b1c[:, cc:cc + 1]),
                           "s", key=("ge", t, cc))
                    if cc == 1 and t + 1 < NT1:
                        rstd_block(t + 1)
                if PHASE < 2 and t + 1 < NT1:
                    rstd_block(t + 1)

                if PHASE >= 2 and t == NT1 - 1:
                    tr.wait(e, E, *[("p1a", t, s, oh)
                                    for s in range(2) for oh in range(2)])
                    Qst.issue(e, E, ("p1st", t), lambda t=t: e.dma_start(
                        out=p1_scr[t // 2, :,
                                   (t % 2) * 2:(t % 2) * 2 + 2, :],
                        in_=p1t))
            def ost_block(u):
                tr.wait(e, E, *[("oev", u, g) for g in range(8)])
                Qst.issue(e, E, ("ost", u), lambda u=u: e.dma_start(
                    out=out_d[u * 512:(u + 1) * 512, :].rearrange(
                        "(s p) d -> p s d", p=128),
                    in_=p1in))

            for u in range(NT2 if PHASE >= 4 else 0):
                for oc in range(8):
                    tr.wait(e, E, ("qmm", u, oc))
                    if u == 0 and oc < 2:
                        tr.wait(e, E, ("kst", NT1 - 1))
                    tr.wait_opt(e, E, ("qst3", u, oc - 2))
                    tr.inc(e.activation(out=ex2[:, oc % 2, :],
                                        in_=pb[oc % 2][:, :], func=AF.Exp),
                           "s", key=("qex", u, oc))
                if u > 0:
                    ost_block(u - 1)
                for oc in range(8):
                    tr.wait(e, E, ("vmm", u, oc))
                    if u == 0:
                        tr.wait(e, E, ("fgrp", NT1 - 1, 15))
                    tr.wait_opt(e, E, ("smp", u - 1, oc), ("t1", u - 1, oc))
                    tr.inc(e.activation(out=vbf[:, oc, :],
                                        in_=pb[2 + oc % 2][:, :],
                                        func=AF.Identity),
                           "s", key=("vev", u, oc))
                for oc in range(8):
                    tr.wait(e, E, ("gmm", u, oc))
                    if u == 0:
                        tr.wait(e, E, ("h", NT1 - 1, 31))
                    else:
                        tr.wait_opt(e, E, ("zn", u - 1, oc))
                    tr.inc(e.activation(out=sgbf[:, oc, :],
                                        in_=pb[oc % 2][:, :], func=AF.Silu),
                           "s", key=("sgev", u, oc))
                for fc in range(8):
                    tr.wait(e, E, ("zev", u, fc))
                    tr.wait_opt(e, E, ("qkp", u, 7), ("sqp", u, fc - 2))
                    tr.inc(e.activation(out=prodp[:, fc % 2, :],
                                        in_=vbf[:, fc, :], func=AF.Square),
                           "s", key=("zsq", u, fc))
                tr.wait(e, E, ("rec2", u), ("var", u))
                tr.inc(e.activation(out=stat2[0:1, 512:1024],
                                    in_=stat2[0:1, 1536:2048], func=AF.Sqrt),
                       "s", key=("rstds", u))
                for fc in range(8):
                    tr.wait(e, E, ("t2", u, fc), "gbcol")
                    tr.wait_opt(e, E, ("zn", u, fc - 2))
                    tr.inc(e.activation(out=t1p[:, fc % 2, :],
                                        in_=t1p[:, fc % 2, :],
                                        func=AF.Identity,
                                        bias=gbcol[:, 8 + fc:9 + fc],
                                        scale=gbcol[:, fc:fc + 1]),
                           "s", key=("t3", u, fc))
                if u == NT2 - 1:
                    ost_block(u)

        def body_tensor(e):
            E = "t"
            for oh in range(2 if PROLOG >= 1 else 0):
                tr.wait(e, E, "b2bf", "c_or")
                tr.minc(e.matmul(pb[oh][:, :], onesr[0:1, :],
                                b2bf[0:1, oh * 512:(oh + 1) * 512],
                                start=True, stop=True),
                       "t", key=("b2m", oh))
            def w1fold():
              for dc in range(8 if PROLOG >= 2 else 0):
                for hs in range(8):
                    i, rsl = hs // 2, hs % 2
                    tr.wait(e, E, ("xl8", 0), ("xl8", 1), ("xl8", 2))
                    if hs >= 4:
                        tr.wait(e, E, ("w1e", dc, hs - 4))
                    elif dc > 0:
                        tr.wait(e, E, ("w1e", dc - 1, hs + 4))
                    tr.minc(e.matmul(
                        pb[4 + hs % 4][:, :],
                        xl8(2 * i + 1, dc * 128, (dc + 1) * 128),
                        xl8(2 * i, rsl * 512, (rsl + 1) * 512),
                        start=True, stop=True),
                        "t", key=("w1m", dc, hs))
            def w2fold():
              for cc in range(32 if PROLOG >= 3 else 0):
                j, c0 = cc // 8, (cc % 8) * 128
                pair = (4, 5) if cc % 2 == 0 else (6, 7)
                for i in range(4):
                    bank = pair[i // 2]
                    off = (i % 2) * 256
                    tr.wait(e, E, ("xl8", 0), ("xl8", 1), ("xl8", 2))
                    if cc < 2:
                        tr.wait(e, E, ("w1e", 7, bank - 4), ("w1e", 7, bank))
                    else:
                        tr.wait(e, E, ("w2e", cc - 2, i // 2))
                    rhs = xl8(2 * i + 8, j * 256, (j + 1) * 256)
                    tr.minc(e.matmul(
                        pb[bank][:, off:off + 256],
                        xl8(2 * i + 9, c0, c0 + 128), rhs,
                        start=True, stop=True),
                        "t", key=("w2m", cc, i // 2) if i % 2 == 1 else None)
            if PHASE < 1:
                w1fold()
                w2fold()
            for t in range(NT1 if PHASE >= 1 else 0):
                for g in range(4):
                    s, half = g // 2, g % 2
                    tr.wait(e, E, ("xn", t, s), "c_id")
                    if g >= 2:
                        tr.wait(e, E, ("tTe", t, g - 2))
                    elif t > 0:
                        tr.wait(e, E, ("krl", t - 1, 4 + g),
                                ("kex", t - 1, 4 + g))
                    else:
                        tr.wait(e, E, ("b2e", g))
                    for q in range(4):
                        fc = half * 4 + q
                        tr.minc(e.matmul(
                            pb[g % 2][:, q * 128:(q + 1) * 128],
                            xn_tm[:, s, fc * 128:(fc + 1) * 128],
                            ident[:, :], start=True, stop=True),
                            "t", key=("tT", t, g) if q == 3 else None)
                for oc in range(8):
                    bank = pb[oc % 4]
                    tr.wait(e, E, ("tTe", t, 3), "wl_wk")
                    if oc < 2:
                        tr.wait(e, E, ("tTe", t, 2 + oc))
                    elif oc < 4:
                        tr.wait_opt(e, E, ("ge", t - 1, 28 + oc)
                                    if PHASE >= 2 else ("krl", t - 1, 4 + oc))
                    else:
                        tr.wait(e, E, ("krl", t, oc - 4), ("kex", t, oc - 4))
                    for dc in range(8):
                        tr.minc(e.matmul(
                            bank[:, 0:256], wk_s(dc, oc), xnT1[:, dc, :],
                            start=(dc == 0), stop=(dc == 7)),
                            "t", key=("kmm", t, oc) if dc == 7 else None)
                if t == 0:
                    w1fold()
                    w2fold()
                for cc in range(33 if PHASE >= 2 else 0):
                    if cc < 32:
                        tr.wait(e, E, ("w1e", 7, 7))
                        if cc < 2:
                            tr.wait(e, E, ("krl", t, 6 + cc),
                                    ("kex", t, 6 + cc))
                        else:
                            tr.wait(e, E, ("ge", t, cc - 2))
                        for dc in range(8):
                            tr.minc(e.matmul(
                                pb[2 + cc % 2][:, 0:256],
                                w1eff(dc, cc * 128, (cc + 1) * 128),
                                xnT1[:, dc, :],
                                start=(dc == 0), stop=(dc == 7)),
                                "t", key=("h", t, cc) if dc == 7 else None)
                    if cc >= 1:
                        ccw = cc - 1
                        tr.wait(e, E, ("ge", t, ccw), ("w2e", 31, 1))
                        if ccw == 0:
                            if t == 0:
                                tr.wait(e, E, ("w2e", 30, 0), ("w2e", 30, 1),
                                        ("w2e", 31, 0), ("w2e", 31, 1))
                            else:
                                tr.wait(e, E, *[("p1a", t - 1, s, oh)
                                                for s in range(2)
                                                for oh in range(2)])
                        nmm = 0
                        for s in range(2):
                            for oh in range(2):
                                nmm += 1
                                tr.minc(e.matmul(
                                    pb[4 + s * 2 + oh][:, :],
                                    hc[:, ccw % 2, s * 128:(s + 1) * 128],
                                    w2eff(ccw, oh * 512, (oh + 1) * 512),
                                    start=(ccw == 0), stop=(ccw == 31)),
                                    "t", key=("fgrp", t, ccw)
                                    if nmm == 4 else None)
            def s2_q(u, lo=0, hi=8):
                for oc in range(lo, hi):
                    tr.wait(e, E, ("xnld", u), "wl_wq")
                    if oc < 2:
                        if u == 0:
                            tr.wait(e, E, ("tTe", NT1 - 1, 2 + oc))
                        else:
                            tr.wait(e, E, ("sgev", u - 1, 6 + oc))
                    else:
                        tr.wait(e, E, ("qrl", u, oc - 2), ("qex", u, oc - 2))
                    for dc in range(8):
                        tr.minc(e.matmul(
                            pb[oc % 2][:, :], wq_s(dc, oc), xnT2v[:, dc, :],
                            start=(dc == 0), stop=(dc == 7)),
                            "t", key=("qmm", u, oc) if dc == 7 else None)

            def s2_v(u):
                for oc in range(8):
                    tr.wait(e, E, "wl_wv")
                    if oc < 2:
                        if u == 0:
                            tr.wait(e, E, ("ge", NT1 - 1, 30 + oc))
                        else:
                            tr.wait(e, E, ("oev", u - 1, 4 + oc))
                    else:
                        tr.wait(e, E, ("vev", u, oc - 2))
                    for dc in range(8):
                        tr.minc(e.matmul(
                            pb[2 + oc % 2][:, :], wv_s(dc, oc),
                            xnT2v[:, dc, :],
                            start=(dc == 0), stop=(dc == 7)),
                            "t", key=("vmm", u, oc) if dc == 7 else None)

            def s2_g(u):
                for oc in range(8):
                    tr.wait(e, E, "wl_wg")
                    if oc < 2:
                        tr.wait(e, E, ("qrl", u, 6 + oc), ("qex", u, 6 + oc))
                    else:
                        tr.wait(e, E, ("sgev", u, oc - 2))
                    for dc in range(8):
                        tr.minc(e.matmul(
                            pb[oc % 2][:, :], wg_s(dc, oc), xnT2v[:, dc, :],
                            start=(dc == 0), stop=(dc == 7)),
                            "t", key=("gmm", u, oc) if dc == 7 else None)

            for u in range(NT2 if PHASE >= 4 else 0):
                if u == 0:
                    s2_q(0)
                    s2_v(0)
                    s2_g(0)
                for fc in range(8):
                    tr.wait(e, E, ("prodg", u, fc) if fc >= 6 else
                            ("prod", u, fc), "c_sa")
                    if fc == 0:
                        if u == 0:
                            tr.wait(e, E, ("p1a", NT1 - 1, 0, 0))
                        else:
                            tr.wait(e, E, ("oev", u - 1, 6))
                    tr.inc(e.matmul(
                        pb[4][0:16, :], selA[:, fc * 16:(fc + 1) * 16],
                        (nprodp if fc >= 6 else prodp)[:, fc % 2, :],
                        start=(fc == 0), stop=(fc == 7)),
                        "t", key=("qkp", u, fc))
                    tr.wait(e, E, ("qst3", u, fc))
                    if fc == 0:
                        if u == 0:
                            tr.wait(e, E, ("p1a", NT1 - 1, 0, 1), ("sak", 7))
                        else:
                            tr.wait(e, E, ("oev", u - 1, 7))
                    tr.inc(e.matmul(
                        pb[5][0:16, :], selAk[:, fc * 16:(fc + 1) * 16],
                        qbf[:, fc, :],
                        start=(fc == 0), stop=(fc == 7)),
                        "t", key=("nmp", u, fc))
                for fc in range(8):
                    for hh in range(2):
                        cs = slice(hh * 256, (hh + 1) * 256)
                        tr.wait(e, E, ("rmul", u, hh), "c_sb")
                        if hh == 0:
                            if fc < 2:
                                tr.wait(e, E, ("vev", u, 6 + fc))
                            else:
                                tr.wait(e, E, ("zev", u, fc - 2))
                        tr.minc(e.matmul(
                            pb[2 + fc % 2][:, cs],
                            selB[:, fc * 128:(fc + 1) * 128], rqk[:, cs],
                            start=True, stop=True),
                            "t", key=("rbc", u, fc, hh)
                            if hh == 1 else None)
                for fc in range(8):
                    tr.wait(e, E, ("zev", u, fc), "c_o1")
                    if fc == 0:
                        if u == 0:
                            tr.wait(e, E, ("p1a", NT1 - 1, 1, 0))
                        else:
                            tr.wait(e, E, ("t1", u - 1, 7))
                    tr.inc(e.matmul(
                        pb[6][0:1, :], ones1[:, :], vbf[:, fc, :],
                        start=(fc == 0), stop=(fc == 7)),
                        "t", key=("smp", u, fc))
                    tr.wait(e, E, ("zsq", u, fc))
                    if fc == 0:
                        if u == 0:
                            tr.wait(e, E, ("p1a", NT1 - 1, 1, 1))
                        else:
                            tr.wait(e, E, ("t2", u - 1, 7))
                    tr.inc(e.matmul(
                        pb[7][0:1, :], ones1[:, :], prodp[:, fc % 2, :],
                        start=(fc == 0), stop=(fc == 7)),
                        "t", key=("sqp", u, fc))
                if u + 1 < NT2:
                    s2_q(u + 1, 0, 2)
                tr.wait(e, E, ("rstdbf", u), ("mu", u), "c_or")
                tr.inc(e.matmul(pb[6][:, :], onesr[0:1, :], rstdb[0:1, :],
                                start=True, stop=True),
                       "t", key=("b1m", u))
                tr.wait(e, E, ("mwr0", u), ("var", u))
                tr.inc(e.matmul(pb[7][:, :], onesr[0:1, :], mwr[0:1, :],
                                start=True, stop=True),
                       "t", key=("b2m2", u))
                for half in range(2):
                    for fc in range(8):
                        for gi in range(4):
                            g = half * 4 + gi
                            s, oh = g // 2, g % 2
                            if fc == 0:
                                if g == 0:
                                    tr.wait(e, E, "wl_wo")
                                if half == 0:
                                    if g < 2:
                                        tr.wait(e, E, ("zev", u, 6 + g))
                                    elif g == 2:
                                        tr.wait(e, E, ("rmul", u, 1))
                                    else:
                                        tr.wait(e, E, ("rrec1", u, 1))
                                else:
                                    tr.wait(e, E, ("oev", u, g - 4))
                            if half == 0 and gi == 0:
                                tr.wait(e, E, ("zn", u, fc))
                            tr.minc(e.matmul(
                                pb[(2, 3, 4, 5)[gi]][:, :],
                                znb[:, fc, s * 128:(s + 1) * 128],
                                wo_s(fc, oh * 512, (oh + 1) * 512),
                                start=(fc == 0), stop=(fc == 7)),
                                "t", key=("womm", u, g) if fc == 7 else None)
                if u + 1 < NT2:
                    s2_q(u + 1, 2, 8)
                    s2_v(u + 1)
                    s2_g(u + 1)

        bodies = [("sync", body_sync), ("g", body_gpsimd),
                  ("v", body_vector), ("s", body_scalar),
                  ("t", body_tensor)]

        fake = _FakeEng()
        for _, b in bodies:
            b(fake)
        tr.begin_emit()
        block = ctx.enter_context(nc.Block())
        block.sync(body_sync)
        block.gpsimd(body_gpsimd)
        block.vector(body_vector)
        block.scalar(body_scalar)
        block.tensor(body_tensor)
        tr.check()
    return nc


def _rearr_w(w):
    M = w.shape[1]
    return np.ascontiguousarray(
        w.reshape(8, 128, M).transpose(1, 0, 2).reshape(128, 8 * M))


def _rearr_w32(w):
    M = w.shape[1]
    return np.ascontiguousarray(
        w.reshape(32, 128, M).transpose(1, 0, 2).reshape(128, 32 * M))


_E4PERM = (np.arange(1024).reshape(256, 4).T.reshape(-1))


def xl16_host(lt):
    rows = []
    for lti in [0, 1, 6, 7, 2, 3, 8, 9, 10, 11, 4, 5, 12, 13, 14, 15]:
        r = lt[lti]
        if lti >= 8 and lti % 2 == 0:
            r = r[_E4PERM]
        rows.append(r)
    return np.ascontiguousarray(np.stack(rows))


def _consts():
    ident = np.eye(128, dtype=ml_dtypes.bfloat16)
    selA = np.zeros((128, 8 * 16), dtype=ml_dtypes.bfloat16)
    selB = np.zeros((32, 8 * 128), dtype=ml_dtypes.bfloat16)
    for fc in range(8):
        for p in range(128):
            h = 2 * fc + p // 64
            selA[p, fc * 16 + h] = 1
            selB[h, fc * 128 + p] = 1
    ones1 = np.ones((128, 1), dtype=ml_dtypes.bfloat16)
    onesr = np.ones((1, 128), dtype=ml_dtypes.bfloat16)
    return ident, selA, selB, ones1, onesr


def prepare(L0_attn_hidden_states, L0_attn_Wq, L0_attn_Wk, L0_attn_Wv,
            L0_attn_Wg, L0_attn_Wo, L0_attn_gamma, L0_attn_beta,
            L0_ffn_W1, L0_ffn_b1, L0_ffn_W2, L0_ffn_b2, L0_ffn_scale):
    x = np.asarray(L0_attn_hidden_states)
    B, S, _ = x.shape
    t_core = B * S // NCORES
    cores_per_b = NCORES // B

    wq = _rearr_w(np.asarray(L0_attn_Wq))
    wk = _rearr_w(np.asarray(L0_attn_Wk))
    wv = _rearr_w(np.asarray(L0_attn_Wv))
    wg = _rearr_w(np.asarray(L0_attn_Wg))
    wo = _rearr_w(np.asarray(L0_attn_Wo))
    w1s = _rearr_w(np.ascontiguousarray(np.asarray(L0_ffn_W1).T))
    w2s = _rearr_w32(np.ascontiguousarray(np.asarray(L0_ffn_W2).T))
    b1c = np.ascontiguousarray(np.asarray(L0_ffn_b1).reshape(32, 128).T)
    b2r = np.ascontiguousarray(np.asarray(L0_ffn_b2).reshape(1, D))
    gbcol = np.ascontiguousarray(np.concatenate(
        [np.asarray(L0_attn_gamma).reshape(8, 128).T,
         np.asarray(L0_attn_beta).reshape(8, 128).T], axis=1))
    scl = np.ascontiguousarray(
        np.broadcast_to(np.asarray(L0_ffn_scale).reshape(1, 1), (128, 1))
    ).astype(np.float32)
    ident, selA, selB, ones1, onesr = _consts()

    nc = build_program(t_core)
    in_maps = []
    for c in range(NCORES):
        b = c // cores_per_b
        q = c % cores_per_b
        mkb = np.zeros((128, 2), dtype=np.float32)
        mkb[:, b] = 1.0
        in_maps.append(dict(
            x_in=np.ascontiguousarray(x[b, q * t_core:(q + 1) * t_core]),
            xl16=xl16_host(x[b, 1:17]),
            wq=wq, wk=wk, wv=wv, wg=wg, wo=wo, w1s=w1s, w2s=w2s,
            b1c=b1c, b2r=b2r, gbcol=gbcol, scl=scl, mkb=mkb,
            idn=ident, selA=selA, selB=selB, ones128=ones1, onesrow=onesr,
        ))
    return nc, in_maps


def assemble(results, B, S):
    t_core = B * S // NCORES
    cores_per_b = NCORES // B
    out = np.empty((B, S, D), dtype=np.float32)
    for c in range(NCORES):
        b = c // cores_per_b
        q = c % cores_per_b
        out[b, q * t_core:(q + 1) * t_core] = results[c]["out"]
    return out


def kernel(L0_attn_hidden_states, **kw):
    x = np.asarray(L0_attn_hidden_states)
    nc, in_maps = prepare(L0_attn_hidden_states, **kw)
    res = run_bass_kernel_spmd(nc, in_maps, core_ids=list(range(NCORES)))
    return assemble(res.results, x.shape[0], x.shape[1])


# version 22
# speedup vs baseline: 1.0933x; 1.0143x over previous
import os
import numpy as np
import ml_dtypes
import concourse.bass as bass
import concourse.mybir as mybir
from concourse.bass_utils import run_bass_kernel_spmd
from contextlib import ExitStack

F32 = mybir.dt.float32
BF16 = mybir.dt.bfloat16
AF = mybir.ActivationFunctionType
ALU = mybir.AluOpType
AX = mybir.AxisListType

D = 1024
HF = 4096
NCORES = 8
PHASE = int(os.environ.get("KPHASE", "4"))
PROLOG = int(os.environ.get("KPROLOG", "3"))


class _FakeInstr:
    def then_inc(self, sem, n=1):
        return self


class _FakeEng:
    def __getattr__(self, name):
        return lambda *a, **k: _FakeInstr()


class _Tr:

    def __init__(self, sems):
        self.sems = sems
        self.plan = True
        self.count = {k: 0 for k in sems}
        self.evt = {}
        self.waited = {}

    def begin_emit(self):
        assert self.plan
        self._plan_count = dict(self.count)
        self.plan = False
        self.count = {k: 0 for k in self.sems}
        self.waited = {}
        for q in getattr(self, "queues", []):
            q.cnt = 0

    def wait_val(self, eng, ename, sem, val):
        if self.plan or val <= 0:
            return
        w = self.waited.setdefault(ename, {})
        if w.get(sem, 0) >= val:
            return
        eng.wait_ge(self.sems[sem], val)
        w[sem] = val

    def check(self):
        assert self.count == self._plan_count, (self.count, self._plan_count)

    def inc(self, instr, sem, n=1, key=None):
        self.count[sem] += n
        tok = (sem, self.count[sem])
        if self.plan:
            if key is not None:
                assert key not in self.evt, key
                self.evt[key] = tok
        else:
            instr.then_inc(self.sems[sem], n)
            if key is not None:
                assert self.evt[key] == tok, (key, self.evt[key], tok)
        return tok

    def minc(self, instr, sem, key=None):
        if key is None:
            return None
        return self.inc(instr, sem, 1, key=key)

    def wait(self, eng, ename, *keys):
        if self.plan:
            return
        w = self.waited.setdefault(ename, {})
        for key in keys:
            if key is None:
                continue
            sem, val = self.evt[key]
            if val <= 0 or w.get(sem, 0) >= val:
                continue
            eng.wait_ge(self.sems[sem], val)
            w[sem] = val

    def wait_opt(self, eng, ename, *keys):
        if self.plan:
            return
        self.wait(eng, ename, *[k for k in keys if k in self.evt])


class _DmaQ:

    def __init__(self, tr, base, n):
        self.tr, self.n = tr, n
        self.names = [f"{base}{i}" for i in range(n)]
        self.cnt = 0
        tr.queues = getattr(tr, "queues", []) + [self]

    def issue(self, eng, ename, key, fn):
        i = self.cnt
        self.cnt += 1
        nm = self.names[i % self.n]
        self.tr.wait_val(eng, ename, nm, 16 * (i // self.n))
        self.tr.inc(fn(), nm, 16, key=key)


def build_program(t_core):
    NT1 = t_core // 256
    NT2 = t_core // 512
    assert NT1 * 256 == t_core and NT2 * 512 == t_core

    nc = bass.Bass("TRN2", target_bir_lowering=False)

    x_in = nc.dram_tensor("x_in", [t_core, D], F32, kind="ExternalInput").ap()
    xl16_d = nc.dram_tensor("xl16", [16, D], F32, kind="ExternalInput").ap()
    wq_d = nc.dram_tensor("wq", [128, 8 * D], F32, kind="ExternalInput").ap()
    wk_d = nc.dram_tensor("wk", [128, 8 * D], F32, kind="ExternalInput").ap()
    wv_d = nc.dram_tensor("wv", [128, 8 * D], F32, kind="ExternalInput").ap()
    wg_d = nc.dram_tensor("wg", [128, 8 * D], F32, kind="ExternalInput").ap()
    wo_d = nc.dram_tensor("wo", [128, 8 * D], F32, kind="ExternalInput").ap()
    w1_d = nc.dram_tensor("w1s", [128, 8 * HF], F32, kind="ExternalInput").ap()
    w2_d = nc.dram_tensor("w2s", [128, 32 * D], F32, kind="ExternalInput").ap()
    b1_d = nc.dram_tensor("b1c", [128, 32], F32, kind="ExternalInput").ap()
    b2_d = nc.dram_tensor("b2r", [1, D], F32, kind="ExternalInput").ap()
    gb_d = nc.dram_tensor("gbcol", [128, 16], F32, kind="ExternalInput").ap()
    sc_d = nc.dram_tensor("scl", [128, 1], F32, kind="ExternalInput").ap()
    mb_d = nc.dram_tensor("mkb", [128, 2], F32, kind="ExternalInput").ap()
    id_d = nc.dram_tensor("idn", [128, 128], BF16, kind="ExternalInput").ap()
    sa_d = nc.dram_tensor("selA", [128, 8 * 16], BF16, kind="ExternalInput").ap()
    sb_d = nc.dram_tensor("selB", [32, 8 * 128], BF16, kind="ExternalInput").ap()
    o1_d = nc.dram_tensor("ones128", [128, 1], BF16, kind="ExternalInput").ap()
    or_d = nc.dram_tensor("onesrow", [1, 128], BF16, kind="ExternalInput").ap()

    out_d = nc.dram_tensor("out", [t_core, D], F32, kind="ExternalOutput").ap()

    xnt_scr = nc.dram_tensor("xnt_scr", [NT2, 128, 8, 512], BF16).ap()
    k_scr = nc.dram_tensor("k_scr", [NT2, 128, 8, 512], BF16).ap()
    p1_scr = nc.dram_tensor("p1_scr", [NT2, 128, 4, D], F32).ap()
    cc_in = nc.dram_tensor("cc_in", [2, 128, 8], F32).ap()
    cc_out = nc.dram_tensor("cc_out", [2, 128, 8], F32).ap()

    ctx = ExitStack()
    with ctx:
        def sbuf(name, shape, dtype):
            return ctx.enter_context(nc.sbuf_tensor(name, shape, dtype))

        wA = sbuf("wA", [128, 73728], BF16)

        def w1eff(dc, lo, hi):
            return wA[:, dc * HF + lo: dc * HF + hi]

        def w2eff(cc, lo, hi):
            return wA[:, 32768 + cc * D + lo: 32768 + cc * D + hi]

        def wk_s(dc, oc):
            b = 65536 + dc * D + oc * 128
            return wA[:, b: b + 128]

        def wq_s(dc, oc):
            b = 65536 + dc * D + oc * 128
            return wA[:, b: b + 128]

        def wv_s(dc, oc):
            b = 49152 + dc * D + oc * 128
            return wA[:, b: b + 128]

        def wg_s(dc, oc):
            b = 57344 + dc * D + oc * 128
            return wA[:, b: b + 128]

        def wo_s(fc, lo, hi):
            return wA[:, fc * D + lo: fc * D + hi]

        xnT2v = wA[:, 32768:36864].rearrange("p (c n) -> p c n", c=8)
        k2v = wA[:, 36864:40960].rearrange("p (c n) -> p c n", c=8)
        qbf = wA[:, 40960:45056].rearrange("p (c n) -> p c n", c=8)
        vbf = wA[:, 45056:49152].rearrange("p (c n) -> p c n", c=8)
        sgbf = wA[:, 8192:12288].rearrange("p (c n) -> p c n", c=8)
        znb = wA[:, 12288:16384].rearrange("p (c n) -> p c n", c=8)

        aB = sbuf("aB", [128, 9728], BF16)
        xn_tm = aB[:, 0:2048].rearrange("p (s d) -> p s d", s=2)
        xnT1 = aB[:, 2048:4096].rearrange("p (c n) -> p c n", c=8)
        hc = aB[:, 4096:4608].rearrange("p (b n) -> p b n", b=2)
        kbf = aB[:, 4608:6656].rearrange("p (c n) -> p c n", c=8)
        mn1 = aB[:, 6656:7168].rearrange("p (b n) -> p b n", b=2)
        ex1 = aB[:, 9216:9728].rearrange("p (b n) -> p b n", b=2)
        xsq = aB[:, 7168:9216].rearrange("p (s d) -> p s d", s=2)
        prodp = aB[:, 0:1024].rearrange("p (b n) -> p b n", b=2)
        nprodp = aB[:, 1024:2048].rearrange("p (b n) -> p b n", b=2)
        t1p = aB[:, 2048:3072].rearrange("p (b n) -> p b n", b=2)
        mn2 = aB[:, 3072:4096].rearrange("p (b n) -> p b n", b=2)
        ex2 = aB[:, 5632:6656].rearrange("p (b n) -> p b n", b=2)
        xsq16 = aB[0:16, 7168:8192]

        aF = sbuf("aF", [128, 6656], F32)
        xF = aF[:, 0:4096].rearrange("p (b s d) -> p b s d", b=2, s=2)
        p1t = aF[:, 4096:6144].rearrange("p (s d) -> p s d", s=2)

        st1 = aF[:, 6144:6160]
        p1in = aF[:, 0:4096].rearrange("p (s d) -> p s d", s=4)
        recf = aF[0:16, 4096:4608]
        stat2 = aF[0:1, 4608:6656]
        xl16 = aF[0:16, 4096:5120]

        st16 = sbuf("st16_s", [16, 8], F32)
        xl8b = sbuf("xl8b_s", [128, 6144], BF16)
        b2bf = aB[0:1, 0:1024]
        rqk = aB[0:32, 4096:4608]
        mwr = aB[0:2, 4608:5120]
        rstdb = aB[0:1, 5120:5632]
        ident = sbuf("ident_s", [128, 128], BF16)
        selA = sbuf("selA_s", [128, 8 * 16], BF16)
        selB = sbuf("selB_s", [32, 8 * 128], BF16)
        ones1 = sbuf("ones1_s", [128, 1], BF16)
        onesr = sbuf("onesr_s", [1, 128], BF16)
        b1c = sbuf("b1c_s", [128, 32], F32)
        b2rep = sbuf("b2rep_s", [128, D], BF16)
        gbcol = sbuf("gbcol_s", [128, 16], F32)
        scl = sbuf("scl_s", [128, 1], F32)
        mkb = sbuf("mkb_s", [128, 2], F32)
        ksum_r = sbuf("ksum_r_s", [128, 8], F32)
        kredb = sbuf("kredb_s", [128, 8], F32)
        ccin_s = sbuf("ccin_s_s", [128, 16], F32)
        ccout_s = sbuf("ccout_s_s", [128, 16], F32)
        ksum_g = sbuf("ksum_g_s", [128, 8], F32)
        selAk = sbuf("selAk_s", [128, 8 * 16], BF16)

        pb = [ctx.enter_context(nc.psum_tensor(f"pb{i}", [128, 512], F32))
              for i in range(8)]

        sems = {}
        for s_ in (["cc", "v", "s", "t", "g"]
                   + [f"ld{i}" for i in range(4)]
                   + [f"st{i}" for i in range(4)]
                   + [f"wl{i}" for i in range(2)]):
            sems[s_] = ctx.enter_context(nc.semaphore("sem_" + s_))
        tr = _Tr(sems)
        Qld = _DmaQ(tr, "ld", 4)
        Qst = _DmaQ(tr, "st", 4)
        Qwl = _DmaQ(tr, "wl", 2)

        _PAIRMAP = {0: (0, 0), 1: (32, 0), 2: (64, 0), 3: (0, 1),
                    4: (32, 1), 5: (32, 2), 6: (64, 1), 7: (64, 2)}

        def xl8(j, lo, hi):
            pair = (j // 2) if j < 8 else (4 + (j - 8) // 2)
            base, blk = _PAIRMAP[pair]
            c0 = blk * 2048 + (j % 2) * 1024
            return xl8b[base:base + 1, c0 + lo: c0 + hi]

        RE = lambda i: 1.0 / i

        def body_sync(e):
            E = "sync"
            Qld.issue(e, E, "ld_xl",
                      lambda: e.dma_start(out=xl16, in_=xl16_d))
            for t in range(min(2, NT1) if PHASE >= 1 else 0):
                Qld.issue(e, E, ("xld", t), lambda t=t: e.dma_start(
                    out=xF[:, t % 2, :, :],
                    in_=x_in[t * 256:(t + 1) * 256, :].rearrange(
                        "(s p) d -> p s d", p=128)))
            for key_, ap_d, ap_s in [
                    ("c_id", id_d, ident[:, :]), ("c_sa", sa_d, selA[:, :]),
                    ("c_sb", sb_d, selB[:, :]), ("c_o1", o1_d, ones1[:, :]),
                    ("c_or", or_d, onesr[:, :]), ("c_b1", b1_d, b1c[:, :]),
                    ("c_sc", sc_d, scl[:, :]), ("c_mb", mb_d, mkb[:, :])]:
                Qld.issue(e, E, key_,
                          lambda a=ap_s, b=ap_d: e.dma_start(out=a, in_=b))
            tr.wait(e, E, "xnl16")
            Qld.issue(e, E, ("xl8", 0), lambda: e.dma_start(
                out=xl8b[0:1, 0:4096], in_=xsq16[0:4, :]))
            Qld.issue(e, E, ("xl8", 1), lambda: e.dma_start(
                out=xl8b[32:33, 0:6144], in_=xsq16[4:10, :]))
            Qld.issue(e, E, ("xl8", 2), lambda: e.dma_start(
                out=xl8b[64:65, 0:6144], in_=xsq16[10:16, :]))
            if PHASE < 1:
                if PROLOG >= 3:
                    tr.wait(e, E, ("w2e", 31, 1))
                elif PROLOG >= 2:
                    tr.wait(e, E, ("w1e", 7, 7))
                elif PROLOG >= 1:
                    tr.wait(e, E, ("b2e", 1))
                else:
                    tr.wait(e, E, "xnl16")
                return
            for t in range(2, NT1):
                if PHASE >= 2:
                    tr.wait_opt(e, E, ("p1x", t - 2, 1, 1))
                else:
                    tr.wait_opt(e, E, ("xn", t - 2, 1))
                Qld.issue(e, E, ("xld", t), lambda t=t: e.dma_start(
                    out=xF[:, t % 2, :, :],
                    in_=x_in[t * 256:(t + 1) * 256, :].rearrange(
                        "(s p) d -> p s d", p=128)))
            if PHASE < 3:
                tr.wait(e, E, ("xnTst", NT1 - 1), ("kst", NT1 - 1),
                        ("kred", NT1 - 1))
                if PHASE >= 2:
                    tr.wait(e, E, *[("p1st", t)
                                    for t in range(max(0, NT1 - 4), NT1)])
                    tr.wait(e, E, *[("xnTst", t)
                                    for t in range(max(0, NT1 - 4), NT1)])
                    tr.wait(e, E, *[("kst", t)
                                    for t in range(max(0, NT1 - 4), NT1)])
                return
            tr.wait(e, E, "ccin_v")
            Qld.issue(e, E, "ccin_d", lambda: e.dma_start(
                out=cc_in.rearrange("b p c -> p b c"),
                in_=ccin_s[:, :].rearrange("p (b c) -> p b c", b=2)))
            tr.wait(e, E, "cc_done")
            Qld.issue(e, E, "ccout_d", lambda: e.dma_start(
                out=ccout_s[:, :].rearrange("p (b c) -> p b c", b=2),
                in_=cc_out.rearrange("b p c -> p b c")))
            if PHASE < 4:
                tr.wait(e, E, "ccout_d")
                return
            for u in range(NT2):
                if u == 0:
                    tr.wait(e, E, ("fgrp", NT1 - 1, 3))
                else:
                    tr.wait(e, E, ("gmm", u - 1, 7))
                tr.wait(e, E, ("xnTst", 2 * u), ("xnTst", 2 * u + 1))
                Qld.issue(e, E, ("xnld", u), lambda u=u: e.dma_start(
                    out=xnT2v, in_=xnt_scr[u]))
                if u > 0:
                    tr.wait(e, E, ("prod", u - 1, 5), ("prodg", u - 1, 7))
                if u == 0:
                    tr.wait(e, E, ("fgrp", NT1 - 1, 7))
                tr.wait(e, E, ("kst", 2 * u), ("kst", 2 * u + 1))
                Qld.issue(e, E, ("kld", u), lambda u=u: e.dma_start(
                    out=k2v, in_=k_scr[u]))
                if u > 0:
                    tr.wait(e, E, ("ost", u - 1))
                else:
                    tr.wait(e, E, ("p1x", NT1 - 1, 1, 1))
                tr.wait(e, E, ("p1st", 2 * u), ("p1st", 2 * u + 1))
                Qld.issue(e, E, ("p1ld", u), lambda u=u: e.dma_start(
                    out=p1in, in_=p1_scr[u]))
            tr.wait(e, E, *[("ost", u)
                            for u in range(max(0, NT2 - 4), NT2)])

        def body_gpsimd(e):
            E = "g"
            Qwl.issue(e, E, "gbcol",
                      lambda: e.dma_start(out=gbcol[:, :], in_=gb_d))
            Qwl.issue(e, E, "b2bf",
                      lambda: e.dma_start(out=b2bf[:, :], in_=b2_d))
            tr.wait(e, E, "ld_xl")
            Qwl.issue(e, E, "wl_wk",
                      lambda: e.dma_start(out=wA[:, 65536:73728], in_=wk_d))
            if PHASE >= 1:
                tr.wait(e, E, ("xl8", 2), ("xld", 1))
            for dc in range(8):
                Qwl.issue(e, E, ("wl_w1", dc), lambda dc=dc: e.dma_start(
                    out=wA[:, dc * HF:(dc + 1) * HF],
                    in_=w1_d[:, dc * HF:(dc + 1) * HF]))
            for j in range(8):
                Qwl.issue(e, E, ("wl_w2", j), lambda j=j: e.dma_start(
                    out=wA[:, 32768 + j * HF: 32768 + (j + 1) * HF],
                    in_=w2_d[:, j * HF:(j + 1) * HF]))
            if PHASE < 3:
                return
            for t in range(NT1 if PHASE >= 1 else 0):
                pass
            if PHASE < 3:
                return
            tr.wait(e, E, "ccin_d")
            tr.inc(e.collective_compute(
                "AllReduce", ALU.add, replica_groups=[list(range(NCORES))],
                ins=[cc_in[:, :, :]], outs=[cc_out[:, :, :]]),
                "cc", 1, key="cc_done")
            if PHASE < 4:
                return
            tr.wait(e, E, ("kmm", NT1 - 1, 7))
            Qwl.issue(e, E, "wl_wq", lambda: e.dma_start(
                out=wA[:, 65536:73728], in_=wq_d))
            tr.wait(e, E, ("fgrp", NT1 - 1, 23))
            Qwl.issue(e, E, "wl_wv", lambda: e.dma_start(
                out=wA[:, 49152:57344], in_=wv_d))
            tr.wait(e, E, ("fgrp", NT1 - 1, 31))
            Qwl.issue(e, E, "wl_wg", lambda: e.dma_start(
                out=wA[:, 57344:65536], in_=wg_d))
            tr.wait(e, E, ("h", NT1 - 1, 31))
            Qwl.issue(e, E, "wl_wo", lambda: e.dma_start(
                out=wA[:, 0:8192], in_=wo_d))
            for u in range(NT2):
                for fc in (6, 7):
                    tr.wait(e, E, ("qst3", u, fc), ("kld", u))
                    tr.wait_opt(e, E, ("qkp", u - 1, fc))
                    if u == 0:
                        tr.wait(e, E, ("tT", NT1 - 1, 3), ("h", NT1 - 1, 31))
                    tr.inc(e.tensor_tensor(
                        out=nprodp[:, fc % 2, :], in0=qbf[:, fc, :],
                        in1=k2v[:, fc, :], op=ALU.mult),
                        "g", key=("prodg", u, fc))

                for fc in range(8):
                    tr.wait(e, E, ("t3", u, fc), ("sgev", u, fc))
                    if u == 0 and fc < 2:
                        tr.wait(e, E, ("h", NT1 - 1, 31))
                    tr.wait_opt(e, E, ("womm", u - 1, 7))
                    tr.inc(e.tensor_tensor(
                        out=znb[:, fc, :], in0=t1p[:, fc % 2, :],
                        in1=sgbf[:, fc, :], op=ALU.mult),
                        "g", key=("zn", u, fc))

        def body_vector(e):
            E = "v"
            tr.wait(e, E, "ld_xl")
            tr.inc(e.tensor_tensor(out=xsq16, in0=xl16, in1=xl16,
                                   op=ALU.mult), "v", key=("pl", 0))
            tr.inc(e.reduce_sum(out=st16[:, 0:1], in_=xl16, axis=AX.X),
                   "v", key=("pl", 1))
            tr.wait(e, E, ("pl", 0))
            tr.inc(e.reduce_sum(out=st16[:, 1:2], in_=xsq16, axis=AX.X),
                   "v", key=("pl", 2))
            tr.wait(e, E, ("pl", 1), ("pl", 2))
            tr.inc(e.scalar_tensor_tensor(
                out=st16[:, 2:3], in0=st16[:, 0:1], scalar=RE(D),
                in1=st16[:, 0:1], op0=ALU.mult, op1=ALU.mult),
                "v", key=("pl", 3))
            tr.wait(e, E, ("pl", 3))
            tr.inc(e.tensor_sub(st16[:, 3:4], st16[:, 1:2], st16[:, 2:3]),
                   "v", key=("pl", 4))
            tr.wait(e, E, ("pl", 4))
            tr.inc(e.tensor_scalar_add(st16[:, 4:5], st16[:, 3:4], D * 1e-5),
                   "v", key=("pl", 5))
            tr.wait(e, E, ("pl", 5))
            tr.inc(e.reciprocal(st16[:, 5:6], st16[:, 4:5]),
                   "v", key=("pl", 6))
            tr.wait(e, E, "pl_sqrt")
            tr.inc(e.scalar_tensor_tensor(
                out=st16[:, 7:8], in0=st16[:, 0:1], scalar=-RE(D),
                in1=st16[:, 6:7], op0=ALU.mult, op1=ALU.mult),
                "v", key=("pl", 7))
            tr.wait(e, E, ("pl", 7))
            tr.inc(e.tensor_scalar(
                out=xsq16, in0=xl16, scalar1=st16[:, 6:7],
                scalar2=st16[:, 7:8], op0=ALU.mult, op1=ALU.add),
                "v", key="xnl16")
            tr.inc(e.memset(ksum_r[:, :], 0.0), "v", key="ksum0")
            for oh in range(2 if PROLOG >= 1 else 0):
                tr.wait(e, E, ("b2m", oh))
                tr.inc(e.tensor_copy(out=b2rep[:, oh * 512:(oh + 1) * 512],
                                     in_=pb[oh][:, :]),
                       "v", key=("b2e", oh))
            def v_folds():
                for dc in range(8 if PROLOG >= 2 else 0):
                    for hs in range(8):
                        tr.wait(e, E, ("w1m", dc, hs), ("wl_w1", dc), "c_sc")
                        sl_ = w1eff(dc, hs * 512, (hs + 1) * 512)
                        tr.inc(e.scalar_tensor_tensor(
                            out=sl_, in0=pb[4 + hs % 4][:, :],
                            scalar=scl[:, 0:1], in1=sl_,
                            op0=ALU.mult, op1=ALU.add),
                            "v", key=("w1e", dc, hs))
                for cc in range(32 if PROLOG >= 3 else 0):
                    pair = (4, 5) if cc % 2 == 0 else (6, 7)
                    for oh in range(2):
                        tr.wait(e, E, ("w2m", cc, oh), ("wl_w2", cc // 4),
                                "c_sc")
                        sl_ = w2eff(cc, oh * 512, (oh + 1) * 512)
                        tr.inc(e.scalar_tensor_tensor(
                            out=sl_, in0=pb[pair[oh]][:, :],
                            scalar=scl[:, 0:1], in1=sl_,
                            op0=ALU.mult, op1=ALU.add),
                            "v", key=("w2e", cc, oh))

            if PHASE < 1:
                v_folds()
            def ln_stats(t, xb):
                tr.wait(e, E, ("xld", t))
                tr.wait_opt(e, E, ("xn", t - 1, 1))
                tr.inc(e.reduce_sum(out=st1[:, 0:2], in_=xb, axis=AX.X),
                       "v", key=("mus", t))
                for s_ in range(2):
                    tr.wait(e, E, ("xsqs", t, s_))
                    tr.inc(e.reduce_sum(out=st1[:, 2 + s_:3 + s_],
                                        in_=xsq[:, s_, :],
                                        axis=AX.X), "v", key=("sqs", t, s_))
                tr.wait(e, E, ("mus", t), ("sqs", t, 1))
                tr.inc(e.scalar_tensor_tensor(
                    out=st1[:, 4:6], in0=st1[:, 0:2], scalar=RE(D),
                    in1=st1[:, 0:2], op0=ALU.mult, op1=ALU.mult),
                    "v", key=("lnb", t))
                tr.wait(e, E, ("lnb", t))
                tr.inc(e.tensor_sub(st1[:, 6:8], st1[:, 2:4], st1[:, 4:6]),
                       "v", key=("lnc", t))
                tr.wait(e, E, ("lnc", t))
                tr.inc(e.tensor_scalar_add(st1[:, 8:10], st1[:, 6:8],
                                           D * 1e-5), "v", key=("lne", t))
                tr.wait(e, E, ("lne", t))
                tr.inc(e.reciprocal(st1[:, 10:12], st1[:, 8:10]),
                       "v", key=("rec1", t))

            def ln_apply(t, xb):
                tr.wait(e, E, ("rstd1", t))
                tr.inc(e.scalar_tensor_tensor(
                    out=st1[:, 14:16], in0=st1[:, 0:2], scalar=-RE(D),
                    in1=st1[:, 12:14], op0=ALU.mult, op1=ALU.mult),
                    "v", key=("nmr1", t))
                for s in range(2):
                    tr.wait(e, E, ("nmr1", t))
                    tr.wait_opt(e, E, ("tT", t - 1, 2 * s + 1))
                    if t == 0:
                        tr.wait(e, E, "xnl16", ("b2m", 1))
                    tr.inc(e.tensor_scalar(
                        out=xn_tm[:, s, :], in0=xb[:, s, :],
                        scalar1=st1[:, 12 + s:13 + s],
                        scalar2=st1[:, 14 + s:15 + s],
                        op0=ALU.mult, op1=ALU.add),
                        "v", key=("xn", t, s))

            def ln_block(t, xb):
                ln_stats(t, xb)
                ln_apply(t, xb)

            def p1x_block(t):
                xb_ = xF[:, t % 2, :, :]
                for s in range(2 if PHASE >= 2 else 0):
                    for oh in range(2):
                        sl_ = p1t[:, s, oh * 512:(oh + 1) * 512]
                        tr.wait(e, E, ("b2e", 1))
                        tr.wait_opt(e, E, ("p1st", t - 1))
                        if t == 0:
                            tr.wait(e, E, "xnl16")
                        tr.inc(e.tensor_add(
                            sl_, xb_[:, s, oh * 512:(oh + 1) * 512],
                            b2rep[:, oh * 512:(oh + 1) * 512]),
                            "v", key=("p1x", t, s, oh))

            def p1_block(t):
                for s in range(2 if PHASE >= 2 else 0):
                    for oh in range(2):
                        sl_ = p1t[:, s, oh * 512:(oh + 1) * 512]
                        tr.wait(e, E, ("fgrp", t, 31), ("p1x", t, s, oh))
                        tr.inc(e.tensor_add(
                            sl_, sl_, pb[4 + s * 2 + oh][:, :]),
                            "v", key=("p1a", t, s, oh))

            for t in range(NT1 if PHASE >= 1 else 0):
                xb = xF[:, t % 2, :, :]
                if t == 0:
                    ln_block(0, xb)
                def k_evac(oc):
                    bank = pb[oc % 4]
                    tr.wait(e, E, ("kmm", t, oc), ("kex", t, oc))
                    tr.wait_opt(e, E, ("kred", t - 1), ("kst", t - 1))
                    tr.inc(e.tensor_scalar_max(kbf[:, oc, :],
                                               bank[:, 0:256], 0.0),
                           "v", key=("krl", t, oc))
                    tr.wait(e, E, ("krl", t, oc))
                    tr.inc(e.scalar_tensor_tensor(
                        out=kbf[:, oc, :], in0=ex1[:, oc % 2, :], scalar=1.0,
                        in1=kbf[:, oc, :], op0=ALU.min, op1=ALU.add),
                        "v", key=("kst3", t, oc))

                for oc in range(8):
                    k_evac(oc)
                if t == 0 and PHASE >= 1:
                    v_folds()
                if t + 1 < NT1:
                    ln_stats(t + 1, xF[:, (t + 1) % 2, :, :])
                    ln_apply(t + 1, xF[:, (t + 1) % 2, :, :])
                p1x_block(t)
                tr.wait(e, E, *[("kst3", t, oc) for oc in range(8)])
                tr.wait_opt(e, E, ("ksum", t - 1))
                tr.inc(e.reduce_sum(out=kredb[:, :], in_=kbf, axis=AX.X),
                       "v", key=("kred", t))
                tr.wait(e, E, "ksum0", ("kred", t))
                tr.inc(e.tensor_tensor(out=ksum_r[:, :], in0=ksum_r[:, :],
                                       in1=kredb[:, :], op=ALU.add),
                       "v", key=("ksum", t))
                if t + 1 >= NT1 and PHASE >= 3:
                    tr.wait(e, E, ("ksum", NT1 - 1), "c_mb")
                    tr.inc(e.tensor_scalar_mul(ccin_s[:, 0:8], ksum_r[:, :],
                                               mkb[:, 0:1]), "v",
                           key=("ccm", 0))
                    tr.wait(e, E, ("ccm", 0))
                    tr.inc(e.tensor_scalar_mul(ccin_s[:, 8:16], ksum_r[:, :],
                                               mkb[:, 1:2]), "v",
                           key="ccin_v")
                p1_block(t)
            if PHASE < 3:
                return
            tr.wait(e, E, ("fgrp", NT1 - 1, 31), ("kst", NT1 - 1),
                    ("kred", NT1 - 1))
            tr.inc(e.memset(rqk[:, :], 0.0), "v", key="rqk0")
            tr.inc(e.memset(mwr[:, :], 1.0), "v", key="mwr1")
            tr.wait(e, E, "ccout_d")
            tr.inc(e.tensor_scalar_mul(ksum_g[:, :], ccout_s[:, 0:8],
                                       mkb[:, 0:1]), "v", key=("ccg", 0))
            tr.wait(e, E, ("ccg", 0))
            tr.inc(e.scalar_tensor_tensor(
                out=ksum_g[:, :], in0=ccout_s[:, 8:16], scalar=mkb[:, 1:2],
                in1=ksum_g[:, :], op0=ALU.mult, op1=ALU.add),
                "v", key="ksumg")
            for fc in range(8):
                if fc == 0:
                    tr.wait(e, E, "ksumg", "c_sa")
                tr.inc(e.tensor_scalar_mul(
                    selAk[:, fc * 16:(fc + 1) * 16],
                    selA[:, fc * 16:(fc + 1) * 16], ksum_g[:, fc:fc + 1]),
                    "v", key=("sak", fc))
            def v_qrl(u, lo=0, hi=8):
                for oc in range(lo, hi):
                    tr.wait(e, E, ("qmm", u, oc), ("qex", u, oc))
                    if u == 0:
                        tr.wait(e, E, ("fgrp", NT1 - 1, 11))
                    tr.inc(e.tensor_scalar_max(qbf[:, oc, :],
                                               pb[oc % 2][:, :], 0.0),
                           "v", key=("qrl", u, oc))
                    tr.wait(e, E, ("qrl", u, oc), ("qex", u, oc))
                    tr.inc(e.scalar_tensor_tensor(
                        out=qbf[:, oc, :], in0=ex2[:, oc % 2, :], scalar=1.0,
                        in1=qbf[:, oc, :], op0=ALU.min, op1=ALU.add),
                        "v", key=("qst3", u, oc))

            for u in range(NT2 if PHASE >= 4 else 0):
                if u == 0:
                    v_qrl(0)
                for fc in range(6):
                    tr.wait(e, E, ("qst3", u, fc), ("kld", u))
                    tr.wait_opt(e, E, ("qkp", u, fc - 2),
                                ("sqp", u - 1, 6 + fc % 2),
                                ("tT", NT1 - 1, 3) if u == 0 else None)
                    tr.inc(e.tensor_tensor(
                        out=prodp[:, fc % 2, :], in0=qbf[:, fc, :],
                        in1=k2v[:, fc, :], op=ALU.mult),
                        "v", key=("prod", u, fc))
                for hh in range(2):
                    cs = slice(hh * 256, (hh + 1) * 256)
                    if hh == 0:
                        tr.wait(e, E, ("nmp", u, 7))
                        tr.wait_opt(e, E, ("rmul", u - 1, 1),
                                    ("p1st", NT1 - 1) if u == 0 else None)
                    tr.inc(e.tensor_scalar_add(recf[:, cs], pb[5][0:16, cs],
                                               1e-6),
                           "v", key=("rrec1", u, hh))
                    tr.wait(e, E, ("rrec1", u, hh))
                    tr.inc(e.reciprocal(recf[:, cs], recf[:, cs]),
                           "v", key=("rrec2", u, hh))
                    tr.wait(e, E, ("qkp", u, 7), ("rrec2", u, hh), "rqk0")
                    if hh == 0:
                        tr.wait_opt(e, E, ("rbc", u - 1, 7, 1))
                    tr.inc(e.tensor_tensor(out=rqk[0:16, cs],
                                           in0=pb[4][0:16, cs],
                                           in1=recf[:, cs], op=ALU.mult),
                           "v", key=("rmul", u, hh))
                for fc in range(8):
                    tr.wait(e, E, ("rbc", u, fc, 1), ("vev", u, fc))
                    tr.inc(e.tensor_tensor(
                        out=vbf[:, fc, :], in0=vbf[:, fc, :],
                        in1=pb[2 + fc % 2][:, :], op=ALU.mult),
                        "v", key=("zev", u, fc))
                tr.wait(e, E, ("smp", u, 7))
                tr.wait_opt(e, E, ("p1st", NT1 - 1) if u == 0 else
                            ("mwr0", u - 1))
                tr.inc(e.tensor_scalar_mul(stat2[0:1, 0:512], pb[6][0:1, :],
                                           RE(D)), "v", key=("mu", u))
                tr.wait(e, E, ("mu", u))
                tr.inc(e.tensor_tensor(out=stat2[0:1, 512:1024],
                                       in0=stat2[0:1, 0:512],
                                       in1=stat2[0:1, 0:512], op=ALU.mult),
                       "v", key=("musq", u))
                tr.wait(e, E, ("sqp", u, 7), ("musq", u))
                tr.inc(e.scalar_tensor_tensor(
                    out=stat2[0:1, 1024:1536], in0=pb[7][0:1, :],
                    scalar=RE(D), in1=stat2[0:1, 512:1024],
                    op0=ALU.mult, op1=ALU.subtract),
                    "v", key=("var", u))
                tr.wait(e, E, ("var", u))
                tr.inc(e.tensor_scalar_add(stat2[0:1, 1024:1536],
                                           stat2[0:1, 1024:1536], 1e-5),
                       "v", key=("vare", u))
                tr.wait(e, E, ("vare", u))
                if u == 0:
                    tr.wait(e, E, ("rstd1", NT1 - 1))
                tr.inc(e.reciprocal(stat2[0:1, 1536:2048],
                                    stat2[0:1, 1024:1536]),
                       "v", key=("rec2", u))
                tr.wait(e, E, ("rstds", u))
                if u == 0:
                    tr.wait(e, E, ("kst", NT1 - 1))
                tr.wait_opt(e, E, ("b1m", u - 1))
                tr.inc(e.tensor_copy(out=rstdb[0:1, :],
                                     in_=stat2[0:1, 512:1024]),
                       "v", key=("rstdbf", u))
                tr.wait(e, E, "mwr1")
                tr.wait_opt(e, E, ("b2m2", u - 1))
                tr.inc(e.scalar_tensor_tensor(
                    out=mwr[0:1, :], in0=stat2[0:1, 0:512], scalar=-1.0,
                    in1=stat2[0:1, 512:1024], op0=ALU.mult, op1=ALU.mult),
                    "v", key=("mwr0", u))
                for fc in range(8):
                    tr.wait(e, E, ("b1m", u), ("zev", u, fc))
                    if u == 0:
                        tr.wait(e, E, ("xnTst", NT1 - 1))
                    tr.wait_opt(e, E, ("zn", u, fc - 2))
                    tr.inc(e.tensor_tensor(
                        out=t1p[:, fc % 2, :], in0=vbf[:, fc, :],
                        in1=pb[6][:, :], op=ALU.mult),
                        "v", key=("t1", u, fc))
                    tr.wait(e, E, ("b2m2", u), ("t1", u, fc))
                    tr.inc(e.tensor_tensor(
                        out=t1p[:, fc % 2, :], in0=t1p[:, fc % 2, :],
                        in1=pb[7][:, :], op=ALU.add),
                        "v", key=("t2", u, fc))

                if u + 1 < NT2:
                    v_qrl(u + 1, 0, 2)
                for g in range(8):
                    s, oh = g // 2, g % 2
                    tr.wait(e, E, ("womm", u, g), ("p1ld", u))
                    tr.inc(e.tensor_add(
                        p1in[:, s, oh * 512:(oh + 1) * 512],
                        p1in[:, s, oh * 512:(oh + 1) * 512],
                        pb[(2, 3, 4, 5)[g % 4]][:, :]),
                        "v", key=("oev", u, g))
                if u + 1 < NT2:
                    v_qrl(u + 1, 2, 8)

        def body_scalar(e):
            E = "s"
            tr.wait(e, E, ("pl", 6))
            tr.inc(e.activation(out=st16[:, 6:7], in_=st16[:, 5:6],
                                func=AF.Sqrt, scale=float(D)),
                   "s", key="pl_sqrt")
            def rstd_block(t):
                tr.wait(e, E, ("rec1", t))
                tr.inc(e.activation(out=st1[:, 12:14], in_=st1[:, 10:12],
                                    func=AF.Sqrt, scale=float(D)),
                       "s", key=("rstd1", t))

            for t in range(NT1 if PHASE >= 1 else 0):
                xb = xF[:, t % 2, :, :]
                if t == 0:
                    for s_ in range(2):
                        tr.wait(e, E, ("xld", 0))
                        if s_ == 0:
                            tr.wait(e, E, ("xl8", 0), ("xl8", 1),
                                    ("xl8", 2))
                        else:
                            tr.wait(e, E, ("sqs", 0, 0))
                        tr.inc(e.activation(out=xsq[:, s_, :],
                                            in_=xb[:, s_, :],
                                            func=AF.Square),
                               "s", key=("xsqs", 0, s_))
                    rstd_block(0)
                for g in range(4):
                    s_, half = g // 2, g % 2
                    tr.wait(e, E, ("tT", t, g))
                    tr.wait_opt(e, E, ("h", t - 1, 31), ("xnTst", t - 1))
                    dst = xnT1[:, half * 4:(half + 1) * 4,
                               s_ * 128:(s_ + 1) * 128]
                    tr.inc(e.activation(
                        out=dst,
                        in_=pb[g % 2][:, :].rearrange("p (q n) -> p q n",
                                                      q=4),
                        func=AF.Identity),
                        "s", key=("tTe", t, g))
                tr.wait(e, E, *[("tTe", t, g) for g in range(4)])
                Qst.issue(e, E, ("xnTst", t), lambda t=t: e.dma_start(
                    out=xnt_scr[t // 2, :, :,
                                (t % 2) * 256:(t % 2) * 256 + 256],
                    in_=xnT1))
                for oc in range(8):
                    tr.wait(e, E, ("kmm", t, oc))
                    tr.wait_opt(e, E, ("kst3", t, oc - 2))
                    tr.inc(e.activation(out=ex1[:, oc % 2, :],
                                        in_=pb[oc % 4][:, 0:256],
                                        func=AF.Exp),
                           "s", key=("kex", t, oc))
                    if t + 1 < NT1 and oc in (0, 2):
                        s_ = oc // 2
                        tr.wait(e, E, ("xld", t + 1))
                        tr.wait_opt(e, E, ("sqs", t, s_))
                        tr.inc(e.activation(out=xsq[:, s_, :],
                                            in_=xF[:, (t + 1) % 2, s_, :],
                                            func=AF.Square),
                               "s", key=("xsqs", t + 1, s_))
                if t > 0 and PHASE >= 2:
                    tr.wait(e, E, *[("p1a", t - 1, s_, oh)
                                    for s_ in range(2) for oh in range(2)])
                    Qst.issue(e, E, ("p1st", t - 1),
                              lambda t=t: e.dma_start(
                        out=p1_scr[(t - 1) // 2, :,
                                   ((t - 1) % 2) * 2:((t - 1) % 2) * 2 + 2,
                                   :],
                        in_=p1t))
                tr.wait(e, E, *[("kst3", t, oc) for oc in range(8)])
                Qst.issue(e, E, ("kst", t), lambda t=t: e.dma_start(
                    out=k_scr[t // 2, :, :,
                              (t % 2) * 256:(t % 2) * 256 + 256],
                    in_=kbf))
                for cc in range(32 if PHASE >= 2 else 0):
                    tr.wait(e, E, ("h", t, cc), "c_b1")
                    tr.wait_opt(e, E, ("fgrp", t, cc - 2))
                    tr.inc(e.activation(out=hc[:, cc % 2, :],
                                        in_=pb[2 + cc % 2][:, 0:256],
                                        func=AF.Gelu, bias=b1c[:, cc:cc + 1]),
                           "s", key=("ge", t, cc))
                    if cc == 1 and t + 1 < NT1:
                        rstd_block(t + 1)
                if PHASE < 2 and t + 1 < NT1:
                    rstd_block(t + 1)

                if PHASE >= 2 and t == NT1 - 1:
                    tr.wait(e, E, *[("p1a", t, s, oh)
                                    for s in range(2) for oh in range(2)])
                    Qst.issue(e, E, ("p1st", t), lambda t=t: e.dma_start(
                        out=p1_scr[t // 2, :,
                                   (t % 2) * 2:(t % 2) * 2 + 2, :],
                        in_=p1t))
            def ost_block(u):
                tr.wait(e, E, *[("oev", u, g) for g in range(8)])
                Qst.issue(e, E, ("ost", u), lambda u=u: e.dma_start(
                    out=out_d[u * 512:(u + 1) * 512, :].rearrange(
                        "(s p) d -> p s d", p=128),
                    in_=p1in))

            for u in range(NT2 if PHASE >= 4 else 0):
                for oc in range(8):
                    tr.wait(e, E, ("qmm", u, oc))
                    if u == 0 and oc < 2:
                        tr.wait(e, E, ("kst", NT1 - 1))
                    tr.wait_opt(e, E, ("qst3", u, oc - 2))
                    tr.inc(e.activation(out=ex2[:, oc % 2, :],
                                        in_=pb[oc % 2][:, :], func=AF.Exp),
                           "s", key=("qex", u, oc))
                if u > 0:
                    ost_block(u - 1)
                for oc in range(8):
                    tr.wait(e, E, ("vmm", u, oc))
                    if u == 0:
                        tr.wait(e, E, ("fgrp", NT1 - 1, 15))
                    tr.wait_opt(e, E, ("smp", u - 1, oc), ("t1", u - 1, oc))
                    tr.inc(e.activation(out=vbf[:, oc, :],
                                        in_=pb[2 + oc % 2][:, :],
                                        func=AF.Identity),
                           "s", key=("vev", u, oc))
                for oc in range(8):
                    tr.wait(e, E, ("gmm", u, oc))
                    if u == 0:
                        tr.wait(e, E, ("h", NT1 - 1, 31))
                    else:
                        tr.wait_opt(e, E, ("zn", u - 1, oc))
                    tr.inc(e.activation(out=sgbf[:, oc, :],
                                        in_=pb[oc % 2][:, :], func=AF.Silu),
                           "s", key=("sgev", u, oc))
                for fc in range(8):
                    tr.wait(e, E, ("zev", u, fc))
                    tr.wait_opt(e, E, ("qkp", u, 7), ("sqp", u, fc - 2))
                    tr.inc(e.activation(out=prodp[:, fc % 2, :],
                                        in_=vbf[:, fc, :], func=AF.Square),
                           "s", key=("zsq", u, fc))
                tr.wait(e, E, ("rec2", u), ("var", u))
                tr.inc(e.activation(out=stat2[0:1, 512:1024],
                                    in_=stat2[0:1, 1536:2048], func=AF.Sqrt),
                       "s", key=("rstds", u))
                for fc in range(8):
                    tr.wait(e, E, ("t2", u, fc), "gbcol")
                    tr.wait_opt(e, E, ("zn", u, fc - 2))
                    tr.inc(e.activation(out=t1p[:, fc % 2, :],
                                        in_=t1p[:, fc % 2, :],
                                        func=AF.Identity,
                                        bias=gbcol[:, 8 + fc:9 + fc],
                                        scale=gbcol[:, fc:fc + 1]),
                           "s", key=("t3", u, fc))
                if u == NT2 - 1:
                    ost_block(u)

        def body_tensor(e):
            E = "t"
            for oh in range(2 if PROLOG >= 1 else 0):
                tr.wait(e, E, "b2bf", "c_or")
                tr.minc(e.matmul(pb[oh][:, :], onesr[0:1, :],
                                b2bf[0:1, oh * 512:(oh + 1) * 512],
                                start=True, stop=True),
                       "t", key=("b2m", oh))
            def w1fold():
              for dc in range(8 if PROLOG >= 2 else 0):
                for hs in range(8):
                    i, rsl = hs // 2, hs % 2
                    tr.wait(e, E, ("xl8", 0), ("xl8", 1), ("xl8", 2))
                    if hs >= 4:
                        tr.wait(e, E, ("w1e", dc, hs - 4))
                    elif dc > 0:
                        tr.wait(e, E, ("w1e", dc - 1, hs + 4))
                    tr.minc(e.matmul(
                        pb[4 + hs % 4][:, :],
                        xl8(2 * i + 1, dc * 128, (dc + 1) * 128),
                        xl8(2 * i, rsl * 512, (rsl + 1) * 512),
                        start=True, stop=True),
                        "t", key=("w1m", dc, hs))
            def w2fold():
              for cc in range(32 if PROLOG >= 3 else 0):
                j, c0 = cc // 8, (cc % 8) * 128
                pair = (4, 5) if cc % 2 == 0 else (6, 7)
                for i in range(4):
                    bank = pair[i // 2]
                    off = (i % 2) * 256
                    tr.wait(e, E, ("xl8", 0), ("xl8", 1), ("xl8", 2))
                    if cc < 2:
                        tr.wait(e, E, ("w1e", 7, bank - 4), ("w1e", 7, bank))
                    else:
                        tr.wait(e, E, ("w2e", cc - 2, i // 2))
                    rhs = xl8(2 * i + 8, j * 256, (j + 1) * 256)
                    tr.minc(e.matmul(
                        pb[bank][:, off:off + 256],
                        xl8(2 * i + 9, c0, c0 + 128), rhs,
                        start=True, stop=True),
                        "t", key=("w2m", cc, i // 2) if i % 2 == 1 else None)
            if PHASE < 1:
                w1fold()
                w2fold()
            for t in range(NT1 if PHASE >= 1 else 0):
                for g in range(4):
                    s, half = g // 2, g % 2
                    tr.wait(e, E, ("xn", t, s), "c_id")
                    if g >= 2:
                        tr.wait(e, E, ("tTe", t, g - 2))
                    elif t > 0:
                        tr.wait(e, E, ("krl", t - 1, 4 + g),
                                ("kex", t - 1, 4 + g))
                    else:
                        tr.wait(e, E, ("b2e", g))
                    for q in range(4):
                        fc = half * 4 + q
                        tr.minc(e.matmul(
                            pb[g % 2][:, q * 128:(q + 1) * 128],
                            xn_tm[:, s, fc * 128:(fc + 1) * 128],
                            ident[:, :], start=True, stop=True),
                            "t", key=("tT", t, g) if q == 3 else None)
                for oc in range(8):
                    bank = pb[oc % 4]
                    tr.wait(e, E, ("tTe", t, 3), "wl_wk")
                    if oc < 2:
                        tr.wait(e, E, ("tTe", t, 2 + oc))
                    elif oc < 4:
                        tr.wait_opt(e, E, ("ge", t - 1, 28 + oc)
                                    if PHASE >= 2 else ("krl", t - 1, 4 + oc))
                    else:
                        tr.wait(e, E, ("krl", t, oc - 4), ("kex", t, oc - 4))
                    for dc in range(8):
                        tr.minc(e.matmul(
                            bank[:, 0:256], wk_s(dc, oc), xnT1[:, dc, :],
                            start=(dc == 0), stop=(dc == 7)),
                            "t", key=("kmm", t, oc) if dc == 7 else None)
                if t == 0:
                    w1fold()
                    w2fold()
                for cc in range(33 if PHASE >= 2 else 0):
                    if cc < 32:
                        tr.wait(e, E, ("w1e", 7, 7))
                        if cc < 2:
                            tr.wait(e, E, ("krl", t, 6 + cc),
                                    ("kex", t, 6 + cc))
                        else:
                            tr.wait(e, E, ("ge", t, cc - 2))
                        for dc in range(8):
                            tr.minc(e.matmul(
                                pb[2 + cc % 2][:, 0:256],
                                w1eff(dc, cc * 128, (cc + 1) * 128),
                                xnT1[:, dc, :],
                                start=(dc == 0), stop=(dc == 7)),
                                "t", key=("h", t, cc) if dc == 7 else None)
                    if cc >= 1:
                        ccw = cc - 1
                        tr.wait(e, E, ("ge", t, ccw), ("w2e", 31, 1))
                        if ccw == 0:
                            if t == 0:
                                tr.wait(e, E, ("w2e", 30, 0), ("w2e", 30, 1),
                                        ("w2e", 31, 0), ("w2e", 31, 1))
                            else:
                                tr.wait(e, E, *[("p1a", t - 1, s, oh)
                                                for s in range(2)
                                                for oh in range(2)])
                        nmm = 0
                        for s in range(2):
                            for oh in range(2):
                                nmm += 1
                                tr.minc(e.matmul(
                                    pb[4 + s * 2 + oh][:, :],
                                    hc[:, ccw % 2, s * 128:(s + 1) * 128],
                                    w2eff(ccw, oh * 512, (oh + 1) * 512),
                                    start=(ccw == 0), stop=(ccw == 31)),
                                    "t", key=("fgrp", t, ccw)
                                    if nmm == 4 else None)
            def s2_q(u, lo=0, hi=8):
                for oc in range(lo, hi):
                    tr.wait(e, E, ("xnld", u), "wl_wq")
                    if oc < 2:
                        if u == 0:
                            tr.wait(e, E, ("tTe", NT1 - 1, 2 + oc))
                        else:
                            tr.wait(e, E, ("sgev", u - 1, 6 + oc))
                    else:
                        tr.wait(e, E, ("qrl", u, oc - 2), ("qex", u, oc - 2))
                    for dc in range(8):
                        tr.minc(e.matmul(
                            pb[oc % 2][:, :], wq_s(dc, oc), xnT2v[:, dc, :],
                            start=(dc == 0), stop=(dc == 7)),
                            "t", key=("qmm", u, oc) if dc == 7 else None)

            def s2_v(u):
                for oc in range(8):
                    tr.wait(e, E, "wl_wv")
                    if oc < 2:
                        if u == 0:
                            tr.wait(e, E, ("ge", NT1 - 1, 30 + oc))
                        else:
                            tr.wait(e, E, ("oev", u - 1, 4 + oc))
                    else:
                        tr.wait(e, E, ("vev", u, oc - 2))
                    for dc in range(8):
                        tr.minc(e.matmul(
                            pb[2 + oc % 2][:, :], wv_s(dc, oc),
                            xnT2v[:, dc, :],
                            start=(dc == 0), stop=(dc == 7)),
                            "t", key=("vmm", u, oc) if dc == 7 else None)

            def s2_g(u):
                for oc in range(8):
                    tr.wait(e, E, "wl_wg")
                    if oc < 2:
                        tr.wait(e, E, ("qrl", u, 6 + oc), ("qex", u, 6 + oc))
                    else:
                        tr.wait(e, E, ("sgev", u, oc - 2))
                    for dc in range(8):
                        tr.minc(e.matmul(
                            pb[oc % 2][:, :], wg_s(dc, oc), xnT2v[:, dc, :],
                            start=(dc == 0), stop=(dc == 7)),
                            "t", key=("gmm", u, oc) if dc == 7 else None)

            for u in range(NT2 if PHASE >= 4 else 0):
                if u == 0:
                    s2_q(0)
                    s2_v(0)
                    s2_g(0)
                for fc in range(8):
                    tr.wait(e, E, ("prodg", u, fc) if fc >= 6 else
                            ("prod", u, fc), "c_sa")
                    if fc == 0:
                        if u == 0:
                            tr.wait(e, E, ("p1a", NT1 - 1, 0, 0))
                        else:
                            tr.wait(e, E, ("oev", u - 1, 6))
                    tr.inc(e.matmul(
                        pb[4][0:16, :], selA[:, fc * 16:(fc + 1) * 16],
                        (nprodp if fc >= 6 else prodp)[:, fc % 2, :],
                        start=(fc == 0), stop=(fc == 7)),
                        "t", key=("qkp", u, fc))
                    tr.wait(e, E, ("qst3", u, fc))
                    if fc == 0:
                        if u == 0:
                            tr.wait(e, E, ("p1a", NT1 - 1, 0, 1), ("sak", 7))
                        else:
                            tr.wait(e, E, ("oev", u - 1, 7))
                    tr.inc(e.matmul(
                        pb[5][0:16, :], selAk[:, fc * 16:(fc + 1) * 16],
                        qbf[:, fc, :],
                        start=(fc == 0), stop=(fc == 7)),
                        "t", key=("nmp", u, fc))
                for fc in range(8):
                    for hh in range(2):
                        cs = slice(hh * 256, (hh + 1) * 256)
                        tr.wait(e, E, ("rmul", u, hh), "c_sb")
                        if hh == 0:
                            if fc < 2:
                                tr.wait(e, E, ("vev", u, 6 + fc))
                            else:
                                tr.wait(e, E, ("zev", u, fc - 2))
                        tr.minc(e.matmul(
                            pb[2 + fc % 2][:, cs],
                            selB[:, fc * 128:(fc + 1) * 128], rqk[:, cs],
                            start=True, stop=True),
                            "t", key=("rbc", u, fc, hh)
                            if hh == 1 else None)
                for fc in range(8):
                    tr.wait(e, E, ("zev", u, fc), "c_o1")
                    if fc == 0:
                        if u == 0:
                            tr.wait(e, E, ("p1a", NT1 - 1, 1, 0))
                        else:
                            tr.wait(e, E, ("t1", u - 1, 7))
                    tr.inc(e.matmul(
                        pb[6][0:1, :], ones1[:, :], vbf[:, fc, :],
                        start=(fc == 0), stop=(fc == 7)),
                        "t", key=("smp", u, fc))
                    tr.wait(e, E, ("zsq", u, fc))
                    if fc == 0:
                        if u == 0:
                            tr.wait(e, E, ("p1a", NT1 - 1, 1, 1))
                        else:
                            tr.wait(e, E, ("t2", u - 1, 7))
                    tr.inc(e.matmul(
                        pb[7][0:1, :], ones1[:, :], prodp[:, fc % 2, :],
                        start=(fc == 0), stop=(fc == 7)),
                        "t", key=("sqp", u, fc))
                if u + 1 < NT2:
                    s2_q(u + 1, 0, 2)
                tr.wait(e, E, ("rstdbf", u), ("mu", u), "c_or")
                tr.inc(e.matmul(pb[6][:, :], onesr[0:1, :], rstdb[0:1, :],
                                start=True, stop=True),
                       "t", key=("b1m", u))
                tr.wait(e, E, ("mwr0", u), ("var", u))
                tr.inc(e.matmul(pb[7][:, :], onesr[0:1, :], mwr[0:1, :],
                                start=True, stop=True),
                       "t", key=("b2m2", u))
                for half in range(2):
                    for fc in range(8):
                        for gi in range(4):
                            g = half * 4 + gi
                            s, oh = g // 2, g % 2
                            if fc == 0:
                                if g == 0:
                                    tr.wait(e, E, "wl_wo")
                                if half == 0:
                                    if g < 2:
                                        tr.wait(e, E, ("zev", u, 6 + g))
                                    elif g == 2:
                                        tr.wait(e, E, ("rmul", u, 1))
                                    else:
                                        tr.wait(e, E, ("rrec1", u, 1))
                                else:
                                    tr.wait(e, E, ("oev", u, g - 4))
                            if half == 0 and gi == 0:
                                tr.wait(e, E, ("zn", u, fc))
                            tr.minc(e.matmul(
                                pb[(2, 3, 4, 5)[gi]][:, :],
                                znb[:, fc, s * 128:(s + 1) * 128],
                                wo_s(fc, oh * 512, (oh + 1) * 512),
                                start=(fc == 0), stop=(fc == 7)),
                                "t", key=("womm", u, g) if fc == 7 else None)
                if u + 1 < NT2:
                    s2_q(u + 1, 2, 8)
                    s2_v(u + 1)
                    s2_g(u + 1)

        bodies = [("sync", body_sync), ("g", body_gpsimd),
                  ("v", body_vector), ("s", body_scalar),
                  ("t", body_tensor)]

        fake = _FakeEng()
        for _, b in bodies:
            b(fake)
        tr.begin_emit()
        block = ctx.enter_context(nc.Block())
        block.sync(body_sync)
        block.gpsimd(body_gpsimd)
        block.vector(body_vector)
        block.scalar(body_scalar)
        block.tensor(body_tensor)
        tr.check()
    return nc


def _rearr_w(w):
    M = w.shape[1]
    return np.ascontiguousarray(
        w.reshape(8, 128, M).transpose(1, 0, 2).reshape(128, 8 * M))


def _rearr_w32(w):
    M = w.shape[1]
    return np.ascontiguousarray(
        w.reshape(32, 128, M).transpose(1, 0, 2).reshape(128, 32 * M))


_E4PERM = (np.arange(1024).reshape(256, 4).T.reshape(-1))


def xl16_host(lt):
    rows = []
    for lti in [0, 1, 6, 7, 2, 3, 8, 9, 10, 11, 4, 5, 12, 13, 14, 15]:
        r = lt[lti]
        if lti >= 8 and lti % 2 == 0:
            r = r[_E4PERM]
        rows.append(r)
    return np.ascontiguousarray(np.stack(rows))


def _consts():
    ident = np.eye(128, dtype=ml_dtypes.bfloat16)
    selA = np.zeros((128, 8 * 16), dtype=ml_dtypes.bfloat16)
    selB = np.zeros((32, 8 * 128), dtype=ml_dtypes.bfloat16)
    for fc in range(8):
        for p in range(128):
            h = 2 * fc + p // 64
            selA[p, fc * 16 + h] = 1
            selB[h, fc * 128 + p] = 1
    ones1 = np.ones((128, 1), dtype=ml_dtypes.bfloat16)
    onesr = np.ones((1, 128), dtype=ml_dtypes.bfloat16)
    return ident, selA, selB, ones1, onesr


def prepare(L0_attn_hidden_states, L0_attn_Wq, L0_attn_Wk, L0_attn_Wv,
            L0_attn_Wg, L0_attn_Wo, L0_attn_gamma, L0_attn_beta,
            L0_ffn_W1, L0_ffn_b1, L0_ffn_W2, L0_ffn_b2, L0_ffn_scale):
    x = np.asarray(L0_attn_hidden_states)
    B, S, _ = x.shape
    t_core = B * S // NCORES
    cores_per_b = NCORES // B

    wq = _rearr_w(np.asarray(L0_attn_Wq))
    wk = _rearr_w(np.asarray(L0_attn_Wk))
    wv = _rearr_w(np.asarray(L0_attn_Wv))
    wg = _rearr_w(np.asarray(L0_attn_Wg))
    wo = _rearr_w(np.asarray(L0_attn_Wo))
    w1s = _rearr_w(np.ascontiguousarray(np.asarray(L0_ffn_W1).T))
    w2s = _rearr_w32(np.ascontiguousarray(np.asarray(L0_ffn_W2).T))
    b1c = np.ascontiguousarray(np.asarray(L0_ffn_b1).reshape(32, 128).T)
    b2r = np.ascontiguousarray(np.asarray(L0_ffn_b2).reshape(1, D))
    gbcol = np.ascontiguousarray(np.concatenate(
        [np.asarray(L0_attn_gamma).reshape(8, 128).T,
         np.asarray(L0_attn_beta).reshape(8, 128).T], axis=1))
    scl = np.ascontiguousarray(
        np.broadcast_to(np.asarray(L0_ffn_scale).reshape(1, 1), (128, 1))
    ).astype(np.float32)
    ident, selA, selB, ones1, onesr = _consts()

    nc = build_program(t_core)
    in_maps = []
    for c in range(NCORES):
        b = c // cores_per_b
        q = c % cores_per_b
        mkb = np.zeros((128, 2), dtype=np.float32)
        mkb[:, b] = 1.0
        in_maps.append(dict(
            x_in=np.ascontiguousarray(x[b, q * t_core:(q + 1) * t_core]),
            xl16=xl16_host(x[b, 1:17]),
            wq=wq, wk=wk, wv=wv, wg=wg, wo=wo, w1s=w1s, w2s=w2s,
            b1c=b1c, b2r=b2r, gbcol=gbcol, scl=scl, mkb=mkb,
            idn=ident, selA=selA, selB=selB, ones128=ones1, onesrow=onesr,
        ))
    return nc, in_maps


def assemble(results, B, S):
    t_core = B * S // NCORES
    cores_per_b = NCORES // B
    out = np.empty((B, S, D), dtype=np.float32)
    for c in range(NCORES):
        b = c // cores_per_b
        q = c % cores_per_b
        out[b, q * t_core:(q + 1) * t_core] = results[c]["out"]
    return out


def kernel(L0_attn_hidden_states, **kw):
    x = np.asarray(L0_attn_hidden_states)
    nc, in_maps = prepare(L0_attn_hidden_states, **kw)
    res = run_bass_kernel_spmd(nc, in_maps, core_ids=list(range(NCORES)))
    return assemble(res.results, x.shape[0], x.shape[1])
